# version 14
# baseline (speedup 1.0000x reference)
import numpy as np
import ml_dtypes
import concourse.bass as bass
import concourse.mybir as mybir
from concourse.bass_utils import run_bass_kernel_spmd

F32 = mybir.dt.float32
BF16 = mybir.dt.bfloat16
AF = mybir.ActivationFunctionType
ALU = mybir.AluOpType
BF = ml_dtypes.bfloat16

NSEG = 8
NPT = 4
T = 4096
D = 1024
NH = 16

ENGINES = ("sync", "gpsimd", "vector", "scalar", "tensor")
_CACHE = {}
DEBUG_PHASES = 99
DEBUG_CUT = set()


class Sched:
    def __init__(self, nc, stack):
        self.nc = nc
        self.items = {e: [] for e in ENGINES}
        self.sems = {}
        self.cnt = {}
        self._stack = stack

    def sem(self, name):
        if name not in self.sems:
            self.sems[name] = self._stack.enter_context(self.nc.semaphore(name))
            self.cnt[name] = 0
        return self.sems[name]

    def add(self, eng, fn, deps=(), semname=None, inc=None):
        deps = [d for d in deps if d is not None]
        if semname is None:
            if eng in ("sync", "gpsimd"):
                base = "dx" if eng == "sync" else "dg"
                k = self.cnt.get(base + "_n", 0)
                self.cnt[base + "_n"] = k + 1
                semname = f"{base}{k % 4}"
                self.sem(semname)
                if self.cnt[semname] > 0:
                    deps.append((semname, self.cnt[semname]))
                inc = 16
            else:
                semname = {"vector": "vv", "scalar": "ss", "tensor": "tt"}[eng]
                inc = 1
        elif inc is None:
            inc = 1
        self.sem(semname)
        for (sn, val) in deps:
            assert val <= self.cnt[sn], f"future dep {sn}:{val}>{self.cnt[sn]}"
        self.cnt[semname] += inc
        self.items[eng].append((fn, tuple(deps), semname, inc))
        return (semname, self.cnt[semname])

    DMASEMS = ("dx0", "dx1", "dx2", "dx3", "dg0", "dg1", "dg2", "dg3")

    def fence(self, names=("vv", "ss", "tt") + DMASEMS):
        return [(n, self.cnt[n]) for n in names if n in self.cnt and self.cnt[n] > 0]

    def dma_fence(self):
        return [(n, self.cnt[n]) for n in self.DMASEMS if n in self.cnt and self.cnt[n] > 0]

    def barrier(self):
        name = f"ph{self.cnt.get('_nbar', 0)}"
        self.cnt["_nbar"] = self.cnt.get("_nbar", 0) + 1
        self.sem(name)
        for e in ENGINES:
            self.cnt[name] += 1
            self.items[e].append((None, (), name, 1))
        return [(name, 5)]

    def emit(self, block):
        def mk(engname):
            def body(eng):
                lastw = {}
                for (fn, deps, semname, inc) in self.items[engname]:
                    for (sn, val) in deps:
                        if lastw.get(sn, -1) < val:
                            eng.wait_ge(self.sems[sn], val)
                            lastw[sn] = val
                    inst = eng.nop() if fn is None else fn(eng)
                    inst.then_inc(self.sems[semname], inc)
            return body
        block.sync(mk("sync"))
        block.gpsimd(mk("gpsimd"))
        block.vector(mk("vector"))
        block.scalar(mk("scalar"))
        block.tensor(mk("tensor"))


class Arena:
    def __init__(self, ap, cols):
        self.ap = ap
        self.cols = cols
        self.off = 0
        self.marks = []

    def alloc(self, n):
        o = self.off
        self.off += n
        assert self.off <= self.cols, f"arena overflow {self.off}>{self.cols}"
        return self.ap[:, o:o + n]

    def mark(self):
        self.marks.append(self.off)

    def reset(self):
        self.off = self.marks.pop()


def r3(ap, c):
    return ap.rearrange("q (a b) -> q a b", b=c)


def build_nc():
    nc = bass.Bass("TRN2", target_bir_lowering=False, debug=False)

    def din(name, shape, dt=F32):
        return nc.dram_tensor(name, list(shape), dt, kind="ExternalInput").ap()

    x_in = din("xseg", (NSEG, 128, NPT * D))
    xhalo = din("xhalo", (128, 2 * D))
    wq_d = din("wq", (D, D)); wk_d = din("wk", (D, D)); wv_d = din("wv", (D, D))
    wgq_d = din("wgq", (D, 512)); wgk_d = din("wgk", (D, 512)); wgv_d = din("wgv", (D, 512))
    ro_d = din("r_out", (512, D)); wo_d = din("w_out", (D, D))
    wgate_d = din("wgate", (D, 32))
    cn1_d = din("cn_w1", (D, D)); cn2_d = din("cn_w2", (D, 512))
    cn3_d = din("cn_w3", (512, 256)); cn4_d = din("cn_w4", (256, 1))
    b1_d = din("cn_b1", (128, 8)); b2_d = din("cn_b2", (128, 4))
    b3_d = din("cn_b3", (128, 2)); b4_d = din("cn_b4", (1, 1))
    g1_d = din("gc_w1", (D, 512)); g2_d = din("gc_w2", (512, 256)); g3_d = din("gc_w3", (256, 1))
    gb1_d = din("gc_b1", (128, 4)); gb2_d = din("gc_b2", (128, 2)); gb3_d = din("gc_b3", (1, 1))
    ar_d = din("ar", (1, 1))
    wm_d = din("winmask", (128, 1024), BF16)
    wm0_d = din("winmask0", (128, 1024), BF16)
    lm_d = din("latmask", (NSEG, 128, NPT * 512), BF16)
    idb_d = din("ident_bf", (128, 128), BF16)
    idf_d = din("ident_f32", (128, 128), F32)
    onec_d = din("ones_col", (128, 1), BF16)
    oner_d = din("ones_row", (1, 128), BF16)

    y_out = nc.dram_tensor("y", [NSEG, 128, NPT * D], F32, kind="ExternalOutput").ap()

    sc_localg = nc.dram_tensor("sc_localg", [NSEG, 128, NPT * D], BF16).ap()
    sc_gq = nc.dram_tensor("sc_gq", [NSEG, 128, 4 * 512], BF16).ap()
    sc_gr = nc.dram_tensor("sc_gr", [NSEG, 128, NPT * 16], F32).ap()
    sc_x2 = nc.dram_tensor("sc_x2", [NSEG, 128, NPT * D], F32).ap()
    sc_oa = nc.dram_tensor("sc_oa", [NSEG, 128, 8 * 512], F32).ap()
    sc_ob = nc.dram_tensor("sc_ob", [NSEG, 128, 8 * 512], F32).ap()
    cc_pi = nc.dram_tensor("cc_pi", [16, D], F32).ap()
    cc_po = nc.dram_tensor("cc_po", [64, D], F32).ap()
    cc_gi = nc.dram_tensor("cc_gi", [128, 8], F32).ap()
    cc_go = nc.dram_tensor("cc_go", [128, 8], F32).ap()

    from contextlib import ExitStack
    st = ExitStack()
    with st:
        S = Sched(nc, st)

        def sb(name, shape, dt=F32):
            return st.enter_context(nc.sbuf_tensor(name, list(shape), dt))[:]

        WA = Arena(sb("w_arena", (128, 28930), BF16), 28930)
        A32 = Arena(sb("a32", (128, 19050), F32), 19050)
        A16 = Arena(sb("a16", (128, 28830), BF16), 28830)
        cpool = sb("cpool", (128, 8))
        idbf = sb("idbf", (128, 128), BF16)
        idf32 = sb("idf32", (128, 128), F32)
        onec = sb("onec", (128, 1), BF16)
        oner = sb("oner", (1, 128), BF16)
        wmask = sb("wmask", (128, 1024), BF16)
        wmask0 = sb("wmask0", (128, 1024), BF16)
        stats = sb("stats", (128, 48))
        st2 = sb("st2", (128, 16))
        pooled = sb("pooled", (128, 160))
        gates = sb("gates", (128, NPT * 32))
        glr = sb("glr", (128, 16))
        den = sb("den", (128, 16))
        biases = sb("biases", (128, 24))
        ar_sb = sb("ar_sb", (1, 8))
        gsum = sb("gsum", (128, 8))
        mean_bf = sb("mean_bf", (128, 8), BF16)
        hg1 = sb("hg1", (128, 4), BF16)
        hg2 = sb("hg2", (128, 2), BF16)
        gk_fm = sb("gk_fm", (128, 4 * 64), BF16)
        gv_fm = sb("gv_fm", (128, 4 * 64), BF16)
        gvr = sb("gvr", (128, 8 * 512), BF16)
        pooled_fm = sb("pooled_fm", (128, 8 * 64), BF16)

        def gvr_ap(lh):
            if lh % 2 == 0:
                return gvr[0:64, 1024 * (lh // 2):1024 * (lh // 2) + 1024]
            return w_ro[0:64, 1024 * (lh // 2):1024 * (lh // 2) + 1024]

        ps2 = [st.enter_context(nc.psum_tensor(f"pp{i}", [128, 1024], F32))[:] for i in range(4)]
        ps = [ps2[i // 2][:, 512 * (i % 2):512 * (i % 2) + 512] for i in range(8)]

        w_qkv = WA.alloc(24576)
        w_gq = WA.alloc(4096)
        w_gate = WA.alloc(256)
        WA.off = 0
        w_gkv = WA.alloc(8192)
        w_ro = WA.alloc(4096)
        w_wo = WA.alloc(8192)
        WA.off = 0
        w_cn1 = WA.alloc(8192)
        w_cn2 = WA.alloc(4096)
        w_cn3 = WA.alloc(1024)
        w_cn4 = WA.alloc(2)
        w_gc1 = WA.alloc(4096)
        w_gc2 = WA.alloc(1024)
        w_gc3 = WA.alloc(2)

        dma = lambda eng, out, in_, deps=(): S.add(
            eng, lambda e, o=out, i=in_: e.dma_start(out=o, in_=i), deps)
        dma_nc = lambda eng, out, in_, deps=(): S.add(
            eng, lambda e, o=out, i=in_: e.dma_start(out=o, in_=i,
                                                     allow_slow_non_contiguous=True), deps)
        vop = lambda fn, deps=(): S.add("vector", fn, deps)
        sop = lambda fn, deps=(): S.add("scalar", fn, deps)
        top = lambda fn, deps=(): S.add("tensor", fn, deps)

        dma("sync", idbf, idb_d)
        dma("sync", idf32, idf_d)
        dma("sync", onec, onec_d)
        dma("sync", oner, oner_d)
        dma("sync", wmask, wm_d)
        dma("sync", wmask0, wm0_d)
        dma("sync", ar_sb[0:1, 0:1], ar_d)
        for c, v in enumerate((1e-5, -0.5, 0.125, -1.0, 0.5, 1.0)):
            vop(lambda e, c=c, v=v: e.memset(cpool[:, c:c + 1], v))

        def ldw(dst, src, kt, cols, deps=()):
            tks = {}
            for k in range(kt):
                tk = dma("gpsimd", dst[:, k * cols:(k + 1) * cols],
                         src[k * 128:(k + 1) * 128, :], deps)
                tks[tk[0]] = tk
            return list(tks.values())

        ldw(w_qkv[:, 0:8192], wq_d, 8, 1024)
        ldw(w_qkv[:, 8192:16384], wk_d, 8, 1024)
        ldw(w_qkv[:, 16384:24576], wv_d, 8, 1024)
        ldw(w_gq, wgq_d, 8, 512)
        ldw(w_gate, wgate_d, 8, 32)
        dma("sync", biases[:, 0:8], b1_d)
        dma("sync", biases[:, 8:12], b2_d)
        dma("sync", biases[:, 12:14], b3_d)
        dma("sync", biases[:, 14:18], gb1_d)
        dma("sync", biases[:, 18:20], gb2_d)
        tb = dma("sync", biases[0:1, 20:21], b4_d)
        tb2 = dma("sync", biases[0:1, 21:22], gb3_d)
        vop(lambda e: e.tensor_scalar(out=biases[0:1, 22:23], in0=biases[0:1, 20:21],
                                      scalar1=0.5, scalar2=None, op0=ALU.mult), [tb])
        vop(lambda e: e.tensor_scalar(out=biases[0:1, 23:24], in0=biases[0:1, 21:22],
                                      scalar1=-1.0, scalar2=None, op0=ALU.mult), [tb2])

        A32.mark(); A16.mark()
        x_ring = [A32.alloc(NPT * D) for _ in range(2)]
        graw = A32.alloc(512)
        pooled_t = A32.alloc(128)
        ln1t = A16.alloc(NPT * D)
        q_fm = A16.alloc(8 * 512)
        ln1f = A16.alloc(8 * 512)
        gq_sb = A16.alloc(4 * 512)
        k_ring = [A16.alloc(8 * 256) for _ in range(3)]
        v_ring = [A16.alloc(16 * 65) for _ in range(6)]
        p_t = [A32.alloc(512).bitcast(BF16) for _ in range(4)]
        localg = A32.alloc(NPT * D // 2).bitcast(BF16)

        vr1 = None
        for sl in range(6):
            vr1 = vop(lambda e, sl=sl: e.memset(r3(v_ring[sl], 65)[:, :, 64:65], 1.0))
        ph0 = S.barrier() + S.dma_fence()
        dep_ph1 = ph0 + S.fence(("vv",)) + S.dma_fence()

        bank_free = {i: None for i in range(8)}
        x_free = [None, None]
        pt_free = [None, None, None, None]
        kv_tok = {}
        kv_ktok = {}
        kslot_free = {}
        vslot_free = {}
        ln_gate = [None]
        gq_free = None
        lg_free = None

        def layernorm(xap_fn, dst_fn, npt, deps, apply_deps=(), apply_eng="vector"):
            t1 = None
            dd = list(deps) + [ln_gate[0]]
            for p in range(npt):
                lo, hi = xap_fn(p)
                t1 = vop(lambda e, p=p, lo=lo: e.bn_stats(stats[:, 12 * p:12 * p + 6], lo), dd)
                t1 = vop(lambda e, p=p, hi=hi: e.bn_stats(stats[:, 12 * p + 6:12 * p + 12], hi), dd)
            ta = None
            for p in range(npt):
                ta = vop(lambda e, p=p: e.bn_aggr(st2[:, 2 * p:2 * p + 2],
                                                  stats[:, 12 * p:12 * p + 12]), [t1])
            va = st2[:, 1:2 * npt:2]
            tl = sop(lambda e, va=va: e.activation(out=va, in_=va, func=AF.Ln,
                                                   bias=cpool[:, 0:1]), [ta])
            tl = sop(lambda e, va=va: e.activation(out=va, in_=va, func=AF.Exp,
                                                   scale=cpool[:, 1:2]), [tl])
            tnb = vop(lambda e, va=va, npt=npt: e.scalar_tensor_tensor(
                out=st2[:, 8:8 + npt], in0=st2[:, 0:2 * npt:2], scalar=-1.0, in1=va,
                op0=ALU.mult, op1=ALU.mult), [tl])
            outs = []
            ad = [tnb] + list(apply_deps)
            for p in range(npt):
                lo, hi = xap_fn(p)
                dst = dst_fn(p)
                if apply_eng == "scalar":
                    outs.append(sop(lambda e, p=p, lo=lo, dst=dst: e.activation(
                        out=dst[:, 0:512], in_=lo, func=AF.Identity,
                        scale=st2[:, 2 * p + 1:2 * p + 2],
                        bias=st2[:, 8 + p:9 + p]), ad))
                    outs.append(sop(lambda e, p=p, hi=hi, dst=dst: e.activation(
                        out=dst[:, 512:1024], in_=hi, func=AF.Identity,
                        scale=st2[:, 2 * p + 1:2 * p + 2],
                        bias=st2[:, 8 + p:9 + p]), ad))
                    continue
                outs.append(vop(lambda e, p=p, lo=lo, dst=dst: e.tensor_scalar(
                    out=dst[:, 0:512], in0=lo, scalar1=st2[:, 2 * p + 1:2 * p + 2],
                    scalar2=st2[:, 8 + p:9 + p], op0=ALU.mult, op1=ALU.add), ad))
                outs.append(vop(lambda e, p=p, hi=hi, dst=dst: e.tensor_scalar(
                    out=dst[:, 512:1024], in0=hi, scalar1=st2[:, 2 * p + 1:2 * p + 2],
                    scalar2=st2[:, 8 + p:9 + p], op0=ALU.mult, op1=ALU.add), ad))
            ln_gate[0] = outs[-1]
            return outs

        def transpose_tok2fm(src_tok, npt, dst_fm, deps, fp32=False, pool=False,
                             gsum_first=None):
            ident = idf32 if fp32 else idbf
            ev = None
            for p in range(npt):
                for half in range(2):
                    bk = half
                    def mm(e, p=p, half=half, bk=bk, first=(gsum_first[0] if gsum_first else False)):
                        r = None
                        for gg in range(4):
                            g = half * 4 + gg
                            r = e.matmul(ps[bk][:, gg * 128:(gg + 1) * 128],
                                         src_tok[:, p * 1024 + g * 128:p * 1024 + (g + 1) * 128],
                                         ident, start=True, stop=True)
                            if pool:
                                r = e.matmul(ps[2][:, (g * 4 + p):(g * 4 + p + 1)],
                                             src_tok[:, p * 1024 + g * 128:p * 1024 + (g + 1) * 128],
                                             onec, start=True, stop=True)
                            if gsum_first is not None:
                                r = e.matmul(ps[7][:, g:g + 1],
                                             src_tok[:, p * 1024 + g * 128:p * 1024 + (g + 1) * 128],
                                             cpool[:, 5:6], start=first, stop=False,
                                             skip_group_check=True)
                        return r
                    tm = top(mm, list(deps) + [bank_free[bk]] +
                             ([bank_free[2]] if pool else []))
                    if gsum_first is not None:
                        gsum_first[0] = False
                    ev = sop(lambda e, p=p, half=half, bk=bk: e.copy(
                        out=r3(dst_fm, 512)[:, half * 4:(half + 1) * 4, p * 128:(p + 1) * 128],
                        in_=r3(ps[bk], 128)), [tm])
                    bank_free[bk] = ev
            return ev

        def kv_block(blk, tcol, dep_lnf):
            ks = (blk + 1) % 3
            ev = None
            k_ev = None
            for i in range(8):
                bk = 3 + (i % 2)
                def mmk(e, i=i, bk=bk, tcol=tcol):
                    r = None
                    for k in range(8):
                        r = e.matmul(ps[bk][:, 0:256],
                                     w_qkv[:, 8192 + k * 1024 + i * 128:8192 + k * 1024 + (i + 1) * 128],
                                     ln1f[:, k * 512 + tcol:k * 512 + tcol + 256],
                                     start=(k == 0), stop=(k == 7))
                    return r
                tm = top(mmk, [dep_lnf, bank_free[bk], kslot_free.get(ks)])
                ev = sop(lambda e, i=i, bk=bk, ks=ks: e.copy(
                    out=k_ring[ks][:, i * 256:(i + 1) * 256], in_=ps[bk][:, 0:256]), [tm])
                bank_free[bk] = ev
                k_ev = ev
            for pp in range(2):
                vs = ks * 2 + pp
                for jh in range(2):
                    bk = 5 + jh
                    def mmv(e, pp=pp, jh=jh, tcol=tcol, bk=bk):
                        r = None
                        for k in range(8):
                            r = e.matmul(ps[bk][:, 0:512],
                                         ln1f[:, k * 512 + tcol + pp * 128:k * 512 + tcol + pp * 128 + 128],
                                         w_qkv[:, 16384 + k * 1024 + jh * 512:16384 + k * 1024 + (jh + 1) * 512],
                                         start=(k == 0), stop=(k == 7))
                        return r
                    tm = top(mmv, [dep_lnf, bank_free[bk], vslot_free.get(vs)])
                    ev = sop(lambda e, vs=vs, jh=jh, bk=bk: e.copy(
                        out=r3(v_ring[vs], 65)[:, jh * 8:(jh + 1) * 8, 0:64],
                        in_=r3(ps[bk], 64)), [tm])
                    bank_free[bk] = ev
            kv_tok[blk] = ev
            kv_ktok[blk] = k_ev
            return ev

        def proj(dst, wcols, n_i, stride, jt, deps, banks=(3, 4)):
            evs = []
            ev = None
            for i in range(n_i):
                bk = banks[i % 2]
                def mm(e, i=i, bk=bk):
                    r = None
                    for k in range(8):
                        r = e.matmul(ps[bk][:, 0:jt],
                                     wcols[:, k * stride + i * 128:k * stride + (i + 1) * 128],
                                     ln1f[:, k * 512:k * 512 + jt],
                                     start=(k == 0), stop=(k == 7))
                    return r
                tm = top(mm, list(deps) + [bank_free[bk]])
                ev = sop(lambda e, i=i, bk=bk, jt=jt: e.copy(
                    out=dst[:, i * jt:(i + 1) * jt], in_=ps[bk][:, 0:jt]), [tm])
                bank_free[bk] = ev
                evs.append(ev)
            return evs

        for _b in range(4):
            bank_free[_b] = vop(lambda e, _b=_b: e.memset(ps[_b][:, 0:512], 0.0), ph0)
        t_xh = dma("sync", x_ring[1][:, 0:2 * D], xhalo, dep_ph1)
        ln_h = layernorm(lambda p: (x_ring[1][:, p * D:p * D + 512],
                                    x_ring[1][:, p * D + 512:(p + 1) * D]),
                         lambda p: ln1t[:, p * D:(p + 1) * D], 2, [t_xh] + dep_ph1)
        ev_trh = transpose_tok2fm(ln1t, 2, ln1f, [ln_h[-1]] + dep_ph1)
        t_kvh = kv_block(-1, 0, ev_trh)
        x_free[1] = ln_h[-1]

        lg_spill = [None] * NSEG
        gq_spill = [None] * NSEG
        gr_spill = [None] * NSEG
        glr_war = {}
        ln1f_last_reader = t_kvh
        tp = None
        ln_pipe = {}

        def emit_ln1(s, ln1t_free):
            xr = s % 2
            t_x = dma("sync", x_ring[xr], x_in[s], [x_free[xr]] + dep_ph1)
            ln_a = layernorm(lambda p, xr=xr: (x_ring[xr][:, p * D:p * D + 512],
                                               x_ring[xr][:, p * D + 512:(p + 1) * D]),
                             lambda p: ln1t[:, p * D:(p + 1) * D], NPT, [t_x],
                             apply_deps=[ln1t_free])
            x_free[xr] = ln_a[-1]
            ln_pipe[s] = ln_a

        emit_ln1(0, ev_trh)
        for s in range(NSEG):
            xr = s % 2
            ln_a = ln_pipe.pop(s)
            ev_tr = transpose_tok2fm(ln1t, NPT, ln1f,
                                     [ln_a[-1], ln1f_last_reader])
            for cl in range(2):
                tp = vop(lambda e, cl=cl, s=s: e.reduce_sum(
                    pooled[:, 8 * (2 * s + cl):8 * (2 * s + cl) + 8],
                    r3(ln1f, 512)[:, :, 256 * cl:256 * cl + 256],
                    axis=mybir.AxisListType.X), [ev_tr])
            if s == NSEG - 1:
                tps = vop(lambda e: e.tensor_scalar(out=pooled[:, 0:128], in0=pooled[:, 0:128],
                                                    scalar1=1.0 / 256, scalar2=None, op0=ALU.mult),
                          [tp])
                tpt0 = top(lambda e: e.matmul(ps[3][:, 0:128], pooled[:, 0:128], idf32,
                                              start=True, stop=True),
                           [tps, bank_free[3]])
                tpt0 = vop(lambda e: e.tensor_copy(out=pooled_t, in_=ps[3][:, 0:128]), [tpt0])
                bank_free[3] = tpt0
                t_pool_out = dma("gpsimd", cc_pi, pooled_t, [tpt0])
                t_ag = S.add("gpsimd", lambda e: e.collective_compute(
                    "AllGather", ALU.bypass, replica_groups=[[0, 1, 2, 3], [4, 5, 6, 7]],
                    ins=[cc_pi], outs=[cc_po]), [t_pool_out], semname="cc", inc=1)
            evq_l = proj(q_fm[:, 0:8 * 512], w_qkv[:, 0:8192], 8, 1024, 512, [ev_tr])
            ev_q = evq_l[-1]
            ev_gq = proj(gq_sb, w_gq, 4, 512, 512, [ev_tr, gq_free])[-1]
            gq_spill[s] = dma("gpsimd", sc_gq[s], gq_sb, [ev_gq])
            gq_free = gq_spill[s]
            def mmg(e):
                r = None
                for k in range(8):
                    r = e.matmul(ps[7][0:32, 0:512], w_gate[:, k * 32:(k + 1) * 32],
                                 q_fm[:, k * 512:(k + 1) * 512], start=(k == 0), stop=(k == 7))
                return r
            tg = top(mmg, [ev_q, bank_free[7]])
            tg = vop(lambda e: e.tensor_copy(out=graw[0:32, :], in_=ps[7][0:32, 0:512]), [tg])
            bank_free[7] = tg
            def mmgt(e):
                r = None
                for p in range(NPT):
                    r = e.matmul(ps[7][:, p * 32:(p + 1) * 32], graw[0:32, p * 128:(p + 1) * 128],
                                 idf32[0:32, 0:32], start=True, stop=True)
                return r
            tgt = top(mmgt, [tg, bank_free[7]])
            tge = sop(lambda e: e.activation(out=gates, in_=ps[7][:, 0:128], func=AF.Exp,
                                             scale=cpool[:, 3:4]),
                      [tgt, gr_spill[s - 1] if s > 0 else None])
            tge = vop(lambda e: e.tensor_scalar(out=gates, in0=gates, scalar1=1.0,
                                                scalar2=None, op0=ALU.add), [tge])
            tge = vop(lambda e: e.reciprocal(out=gates, in_=gates), [tge])
            bank_free[7] = tge
            gr_spill[s] = dma("gpsimd", sc_gr[s], r3(gates, 32)[:, :, 16:32], [tge])
            kv_block(2 * s, 0, ev_tr)
            kv_block(2 * s + 1, 256, ev_tr)
            ln1f_last_reader = S.fence(("tt",))[0]
            if s + 1 < NSEG:
                emit_ln1(s + 1, ev_tr)

            for lb in range(2):
                blk = 2 * s + lb
                mask = wmask0 if blk == 0 else wmask
                last_tsc = None
                pv_last = {}

                def gate_grp(hgrp, p_idx_base):
                    nh = (7, 7, 2)[hgrp]
                    hoff = 7 * hgrp
                    for qc in range(2):
                        p_idx = p_idx_base + qc
                        bk = (4 if qc == 0 else 6) + (hgrp & 1)
                        tr_ = vop(lambda e, bk=bk, nh=nh, hoff=hoff: e.reciprocal(
                            out=glr[:, hoff:hoff + nh],
                            in_=r3(ps[bk][:, 0:nh * 65], 65)[:, 0:nh, 64:65].rearrange(
                                "q a b -> q (a b)")), [pv_last[bk], glr_war.get(hgrp)])
                        tr_ = vop(lambda e, nh=nh, hoff=hoff, p_idx=p_idx: e.tensor_mul(
                            out=glr[:, hoff:hoff + nh], in0=glr[:, hoff:hoff + nh],
                            in1=gates[:, 32 * p_idx + hoff:32 * p_idx + hoff + nh]), [tr_, tge])
                        tr_ = vop(lambda e, bk=bk, nh=nh, hoff=hoff, p_idx=p_idx: e.tensor_tensor(
                            out=r3(localg[:, p_idx * D:(p_idx + 1) * D], 64)[:, hoff:hoff + nh, :],
                            in0=r3(ps[bk][:, 0:nh * 65], 65)[:, 0:nh, 0:64],
                            in1=glr[:, hoff:hoff + nh][:, :, None].broadcast_to([128, nh, 64]),
                            op=ALU.mult), [tr_, lg_free])
                        glr_war[hgrp] = tr_
                        bank_free[bk] = tr_

                def emit_sc(h):
                    ptb = p_t[h % 4]
                    sb0, sb1 = (0, 1) if h % 2 == 0 else (2, 3)
                    def mms(e, h=h, blk=blk, lb=lb, sb0=sb0, sb1=sb1):
                        r = None
                        for jc in range(4):
                            kslot = ((blk - 1 if jc < 2 else blk) + 1) % 3
                            kc = 256 * (h // 2) + 128 * (jc % 2)
                            i0_ = 128 if jc == 3 else 0
                            i1_ = 128 if jc == 0 else 256
                            bnk = sb0 if jc < 2 else sb1
                            r = e.matmul(ps[bnk][:, 256 * (jc % 2) + i0_:256 * (jc % 2) + i1_],
                                         k_ring[kslot][64 * (h % 2):64 * (h % 2) + 64, kc:kc + 128],
                                         q_fm[64 * (h % 2):64 * (h % 2) + 64,
                                              512 * (h // 2) + 256 * lb + i0_:512 * (h // 2) + 256 * lb + i1_],
                                         start=True, stop=True)
                        return r
                    tsc = top(mms, [kv_ktok[blk - 1], kv_ktok[blk], evq_l[h // 2],
                                    bank_free[sb0], bank_free[sb1]])
                    te = sop(lambda e, ptb=ptb, sb0=sb0: e.activation(
                        out=ptb, in_=ps2[sb0 // 2], func=AF.Exp, scale=cpool[:, 2:3]),
                        [tsc, pt_free[h % 4]])
                    bank_free[sb0] = te
                    bank_free[sb1] = te
                    tmk = vop(lambda e, ptb=ptb, mask=mask: e.tensor_mul(
                        out=ptb, in0=ptb, in1=mask), [te])
                    hstate[h] = (tmk, tsc)

                def emit_pv(h):
                    ptb = p_t[h % 4]
                    tmk, _ = hstate[h]
                    hgrp = min(h // 7, 2)
                    hoff = h - 7 * hgrp
                    for qc in range(2):
                        bk = (4 if qc == 0 else 6) + (hgrp & 1)
                        jcs = (0, 1, 2) if qc == 0 else (1, 2, 3)
                        def mmpv(e, h=h, qc=qc, bk=bk, hoff=hoff, blk=blk, ptb=ptb, jcs=jcs):
                            r = None
                            for ji, jc in enumerate(jcs):
                                vs = (((blk - 1 if jc < 2 else blk) + 1) % 3) * 2 + (jc % 2)
                                r = e.matmul(ps[bk][:, hoff * 65:hoff * 65 + 65],
                                             ptb[:, 256 * jc + 128 * qc:256 * jc + 128 * qc + 128],
                                             v_ring[vs][:, 65 * h:65 * h + 65],
                                             start=(ji == 0), stop=(ji == 2))
                            return r
                        tpv = top(mmpv, [tmk, bank_free[bk], kv_tok[blk - 1], kv_tok[blk]])
                        pv_last[bk] = tpv
                    pt_free[h % 4] = pv_last[6 + (hgrp & 1)]
                    if h == 6:
                        gate_grp(0, 2 * lb)
                    elif h == 13:
                        gate_grp(1, 2 * lb)
                    elif h == 15:
                        gate_grp(2, 2 * lb)

                hstate = {}
                for h in range(NH):
                    emit_sc(h)
                    if h >= 2:
                        emit_pv(h - 2)
                emit_pv(NH - 2)
                emit_pv(NH - 1)
                last_tsc = hstate[NH - 1][1]
                kslot_free[blk % 3] = last_tsc
                vslot_free[(blk % 3) * 2] = pv_last[6]
                vslot_free[(blk % 3) * 2 + 1] = pv_last[6]
            lg_spill[s] = dma("gpsimd", sc_localg[s], localg,
                              [bank_free[4], bank_free[5], bank_free[6], bank_free[7]])
            lg_free = lg_spill[s]

        def finish_debug():
            tk = None
            for s_ in range(NSEG):
                tk = dma("gpsimd", y_out[s_], x_ring[0])
            S.add("sync", None, S.dma_fence(), semname="fin", inc=1)
            with nc.Block() as block:
                S.emit(block)

        ph1 = S.fence()
        if DEBUG_PHASES <= 1:
            finish_debug()
            return nc

        t_wgk = ldw(w_gkv[:, 0:4096], wgk_d, 8, 512, ph1)
        t_wgv = ldw(w_gkv[:, 4096:8192], wgv_d, 8, 512, ph1)
        t_wro = ldw(w_ro, ro_d, 4, 1024, ph1)
        t_wwo = ldw(w_wo, wo_d, 8, 1024, ph1)

        A32.reset(); A16.reset()
        A32.mark(); A16.mark()
        pooled_tk = A32.alloc(1024)
        t_ptk = dma("sync", pooled_tk[0:64, :], cc_po, [t_ag] + ph1)
        ev_pt = None
        for g in range(8):
            bk = g % 2
            tm = top(lambda e, g=g, bk=bk: e.matmul(
                ps[bk][:, 0:64], pooled_tk[0:64, g * 128:(g + 1) * 128],
                idf32[0:64, 0:64], start=True, stop=True),
                [t_ptk, bank_free[bk]] + ph1)
            ev_pt = vop(lambda e, g=g, bk=bk: e.tensor_copy(
                out=pooled_fm[:, g * 64:(g + 1) * 64], in_=ps[bk][:, 0:64]), [tm])
            bank_free[bk] = ev_pt

        def gkgv(dst, woff, depw):
            ev = None
            for i in range(4):
                bk = 2 + (i % 2)
                def mm(e, i=i, bk=bk, woff=woff):
                    r = None
                    for k in range(8):
                        r = e.matmul(ps[bk][:, 0:64],
                                     w_gkv[:, woff + k * 512 + i * 128:woff + k * 512 + (i + 1) * 128],
                                     pooled_fm[:, k * 64:(k + 1) * 64],
                                     start=(k == 0), stop=(k == 7))
                    return r
                tm = top(mm, [ev_pt, bank_free[bk]] + list(depw))
                ev = vop(lambda e, i=i, bk=bk, dst=dst: e.tensor_copy(
                    out=dst[:, i * 64:(i + 1) * 64], in_=ps[bk][:, 0:64]), [tm])
                bank_free[bk] = ev
            return ev
        ev_gk = gkgv(gk_fm, 0, t_wgk)
        ev_gv = gkgv(gv_fm, 4096, t_wgv)
        ev_gvr = None
        for lh in range(8):
            for jh in range(2):
                bk = 4 + 2 * (lh % 2) + jh
                def mm(e, lh=lh, jh=jh, bk=bk):
                    return e.matmul(ps[bk][0:64, 0:512],
                                    gv_fm[64 * (lh % 2):64 * (lh % 2) + 64,
                                          64 * (lh // 2):64 * (lh // 2) + 64],
                                    w_ro[64 * (lh % 2):64 * (lh % 2) + 64,
                                         (lh // 2) * 1024 + jh * 512:(lh // 2) * 1024 + (jh + 1) * 512],
                                    start=True, stop=True)
                tm = top(mm, [ev_gv, bank_free[bk]] + t_wro)
                ev_gvr = vop(lambda e, lh=lh, jh=jh, bk=bk: e.tensor_copy(
                    out=gvr_ap(lh)[:, jh * 512:(jh + 1) * 512], in_=ps[bk][0:64, 0:512]), [tm])
                bank_free[bk] = ev_gvr
        ph15 = S.fence()
        if DEBUG_PHASES <= 2:
            finish_debug()
            return nc

        A32.reset(); A16.reset()
        A32.mark(); A16.mark()
        xs_r = [A32.alloc(NPT * D) for _ in range(2)]
        ln3 = A32.alloc(NPT * D)
        o0t = ln3
        o0f = A32.alloc(8 * 512)
        remg_r = [A32.alloc(D) for _ in range(2)]
        bstats = A32.alloc(96)
        bst2 = A32.alloc(32)
        lmask = A16.alloc(NPT * 512)
        plat_r = [A16.alloc(512) for _ in range(2)]
        platT = A16.alloc(8 * 512)
        lgseg = A16.alloc(NPT * D)
        comb_r = [A16.alloc(D) for _ in range(2)]
        comb_f = A16.alloc(8 * 256)
        gq2r = [A16.alloc(4 * 512) for _ in range(2)]
        attn_r = [A16.alloc(NPT * D) for _ in range(2)]

        gsum_tok = [None]
        x2_spill = [[] for _ in range(NSEG)]
        o0_spill = [None] * NSEG
        pv = {"gq2": [None, None], "lg": None, "plat": [None, None],
              "remg": [None, None], "comb": [None, None],
              "o0f": None, "gr": None,
              "lm": None, "attn": [None, None], "den": [None, None]}
        pv_xs = [[], []]
        pv_ln3 = [None] * NPT
        pvb3 = [None] * NPT
        pvb2 = [None] * NPT
        pvap3 = [None] * NPT
        pvap2 = [None] * NPT
        AB = {}
        STA = {}

        def ph2_A(s):
            ab = s < NSEG
            xr = s % 2
            sB = s - 1
            xrB = sB % 2
            stB = STA.pop(sB, None)
            if sB >= 0:
                t_xsB = dma("sync", xs_r[xrB], x_in[sB], pv_xs[xrB] + ph15)
                pv_xs[xrB] = []
            gq2 = gq2r[s % 2]
            if ab:
                t_lm = dma("sync", lmask, lm_d[s], [pv["lm"]] + ph15)
                t_gq = dma("sync", gq2, sc_gq[s], [gq_spill[s], pv["gq2"][s % 2]] + ph15)
                t_lg = dma("sync", lgseg, sc_localg[s], [lg_spill[s], pv["lg"]] + ph15)
                t_gr = dma("sync", gates[:, 0:64], sc_gr[s], [gr_spill[s], pv["gr"]] + ph15)
            st = [dict() for _ in range(NPT)]

            def st0(p):
                pr = p % 2
                plat = plat_r[pr]
                dn = den[:, 8 * pr:8 * pr + 8]
                def mml(e, p=p):
                    r = None
                    for lh in range(8):
                        bnk = 0 if lh % 2 == 0 else 1
                        r = e.matmul(ps[bnk][:, 64 * (lh // 2):64 * (lh // 2) + 64],
                                     gq2[64 * (lh % 2):64 * (lh % 2) + 64,
                                         512 * (lh // 2) + 128 * p:512 * (lh // 2) + 128 * p + 128],
                                     gk_fm[64 * (lh % 2):64 * (lh % 2) + 64,
                                           64 * (lh // 2):64 * (lh // 2) + 64],
                                     start=True, stop=True)
                    return r
                tl = top(mml, [t_gq, ev_gk, bank_free[0], bank_free[1]])
                te = sop(lambda e, plat=plat: e.activation(
                    out=r3(plat, 64)[:, 0::2, :], in_=r3(ps[0][:, 0:256], 64),
                    func=AF.Exp, scale=cpool[:, 2:3]), [tl, pv["plat"][pr]])
                te = sop(lambda e, plat=plat: e.activation(
                    out=r3(plat, 64)[:, 1::2, :], in_=r3(ps[1][:, 0:256], 64),
                    func=AF.Exp, scale=cpool[:, 2:3]), [tl])
                bank_free[0] = te
                bank_free[1] = te
                tm_ = vop(lambda e, p=p, plat=plat: e.tensor_mul(
                    out=plat, in0=plat, in1=lmask[:, p * 512:(p + 1) * 512]), [te, t_lm])
                st[p]["lm"] = tm_
                td = vop(lambda e, plat=plat, dn=dn: e.reduce_sum(
                    dn, r3(plat, 64), axis=mybir.AxisListType.X), [tm_, pv["den"][pr]])
                td = vop(lambda e, dn=dn: e.reciprocal(out=dn, in_=dn), [td])
                tn = vop(lambda e, plat=plat, dn=dn: e.tensor_tensor(
                    out=r3(plat, 64), in0=r3(plat, 64),
                    in1=dn[:, :, None].broadcast_to([128, 8, 64]), op=ALU.mult), [td])
                pv["plat"][pr] = tn
                pv["den"][pr] = tn
                st[p]["tn"] = tn

            def st1(p):
                pr = p % 2
                plat = plat_r[pr]
                tn = st[p]["tn"]
                tts = {}
                for lh in range(8):
                    bk = 2 + (lh % 2)
                    tts[bk] = top(lambda e, lh=lh, bk=bk, plat=plat, p=p: e.matmul(
                        ps[bk][0:64, 128 * (lh // 2):128 * (lh // 2) + 128],
                        plat[:, 64 * lh:64 * lh + 64], idbf, start=True, stop=True),
                        [tn, bank_free[bk]])
                tev = None
                for bk in (2, 3):
                    tev = sop(lambda e, bk=bk, p=p: e.copy(
                        out=r3(platT, 512)[0:64, (bk - 2)::2, 128 * p:128 * p + 128],
                        in_=r3(ps[bk][0:64, 0:512], 128)), [tts[2], tts[3]])
                    bank_free[bk] = tev
                st[p]["tev"] = tev

            def st2(p):
                pr = p % 2
                remg = remg_r[pr]
                def mmr(e, p=p):
                    r = None
                    for lh in range(8):
                        for jh in range(2):
                            r = e.matmul(ps[4 + jh][:, 0:512],
                                         platT[0:64, 512 * lh + 128 * p:512 * lh + 128 * p + 128],
                                         gvr_ap(lh)[:, jh * 512:(jh + 1) * 512],
                                         start=(lh == 0), stop=(lh == 7))
                    return r
                trm = top(mmr, [st[p]["tev"], ev_gvr, bank_free[4], bank_free[5]])
                gpair = gates[:, 16 * p:16 * p + 16][:, :, None].broadcast_to([128, 16, 64])
                tg1 = vop(lambda e, remg=remg, gpair=gpair: e.tensor_tensor(
                    out=r3(remg, 64), in0=r3(ps2[2], 64), in1=gpair, op=ALU.mult),
                    [trm, t_gr, pv["remg"][pr]])
                bank_free[4] = tg1
                bank_free[5] = tg1
                st[p]["tg1"] = tg1
                st[p]["trm"] = trm

            def st3(p):
                pr = p % 2
                remg = remg_r[pr]
                comb = comb_r[pr]
                tcb = vop(lambda e, p=p, remg=remg, comb=comb: e.tensor_add(
                    out=comb, in0=remg, in1=lgseg[:, p * D:(p + 1) * D]),
                    [st[p]["tg1"], t_lg, pv["comb"][pr]])
                pv["remg"][pr] = tcb
                tev2 = None
                for half in range(2):
                    bk = 2 + half
                    def mmt(e, half=half, bk=bk, comb=comb):
                        r = None
                        for gg in range(4):
                            g = half * 4 + gg
                            r = e.matmul(ps[bk][:, gg * 128:(gg + 1) * 128],
                                         comb[:, g * 128:(g + 1) * 128],
                                         idbf, start=True, stop=True)
                        return r
                    tmt = top(mmt, [tcb, bank_free[bk]])
                    tev2 = sop(lambda e, p=p, half=half, bk=bk: e.copy(
                        out=r3(comb_f, 256)[:, half * 4:(half + 1) * 4,
                                            (p % 2) * 128:(p % 2) * 128 + 128],
                        in_=r3(ps[bk], 128)), [tmt])
                    bank_free[bk] = tev2
                pv["comb"][pr] = tev2
                st[p]["tev2"] = tev2

            def st4(p):
                def mmw(e, p=p):
                    r = None
                    for jh in range(2):
                        for k in range(8):
                            r = e.matmul(ps[6 + jh][:, 0:512],
                                         comb_f[:, k * 256 + 128 * (p % 2):
                                                k * 256 + 128 * (p % 2) + 128],
                                         w_wo[:, k * 1024 + jh * 512:k * 1024 + (jh + 1) * 512],
                                         start=(k == 0), stop=(k == 7))
                    return r
                tw = top(mmw, [st[p]["tev2"], bank_free[6], bank_free[7]] + t_wwo)
                ta = None
                for jh in range(2):
                    ta = sop(lambda e, p=p, jh=jh, xr=xr: e.copy(
                        out=attn_r[xr][:, p * D + jh * 512:p * D + jh * 512 + 512],
                        in_=ps[6 + jh]), [tw, pv["attn"][xr]])
                    bank_free[6 + jh] = ta
                st[p]["ta"] = ta
                st[p]["tw"] = tw

            def st5(p):
                if sB < 0:
                    return
                ta_d = stB[p]["ta"]
                c0 = 12 * p
                t1 = vop(lambda e, p=p: e.bn_stats(
                    bstats[:, c0:c0 + 6], attn_r[xrB][:, p * D:p * D + 512]),
                    [ta_d, pvb3[p]])
                t1 = vop(lambda e, p=p: e.bn_stats(
                    bstats[:, c0 + 6:c0 + 12], attn_r[xrB][:, p * D + 512:(p + 1) * D]),
                    [ta_d])
                if p == NPT - 1:
                    pv["attn"][xrB] = t1
                ta2 = vop(lambda e, p=p: e.bn_aggr(bst2[:, 2 * p:2 * p + 2],
                                                   bstats[:, c0:c0 + 12]), [t1])
                pvb3[p] = ta2
                va = bst2[:, 2 * p + 1:2 * p + 2]
                tl = sop(lambda e, va=va: e.activation(out=va, in_=va, func=AF.Ln,
                                                       bias=cpool[:, 0:1]),
                         [ta2, pvap3[p]])
                tl = sop(lambda e, va=va: e.activation(out=va, in_=va, func=AF.Exp,
                                                       scale=cpool[:, 1:2]), [tl])
                tnb = vop(lambda e, p=p, va=va: e.scalar_tensor_tensor(
                    out=bst2[:, 8 + p:9 + p], in0=bst2[:, 2 * p:2 * p + 1], scalar=-1.0,
                    in1=va, op0=ALU.mult, op1=ALU.mult), [tl])
                tap = None
                for hf in range(2):
                    tap = vop(lambda e, p=p, hf=hf, va=va: e.tensor_scalar(
                        out=ln3[:, p * D + hf * 512:p * D + hf * 512 + 512],
                        in0=attn_r[xrB][:, p * D + hf * 512:p * D + hf * 512 + 512],
                        scalar1=va, scalar2=bst2[:, 8 + p:9 + p],
                        op0=ALU.mult, op1=ALU.add), [tnb, pv_ln3[p]])
                pvap3[p] = tap
                tx2 = vop(lambda e, p=p: e.tensor_add(
                    out=xs_r[xrB][:, p * D:(p + 1) * D], in0=ln3[:, p * D:(p + 1) * D],
                    in1=xs_r[xrB][:, p * D:(p + 1) * D]), [tap, t_xsB])
                spl = dma("gpsimd", sc_x2[sB][:, p * D:(p + 1) * D],
                          xs_r[xrB][:, p * D:(p + 1) * D], [tx2])
                x2_spill[sB].append(spl)
                st[p]["tx2"] = tx2

            def st6(p):
                if sB < 0:
                    return
                tx2 = st[p]["tx2"]
                c0 = 48 + 12 * p
                t1 = vop(lambda e, p=p: e.bn_stats(
                    bstats[:, c0:c0 + 6], xs_r[xrB][:, p * D:p * D + 512]),
                    [tx2, pvb2[p]])
                t1 = vop(lambda e, p=p: e.bn_stats(
                    bstats[:, c0 + 6:c0 + 12], xs_r[xrB][:, p * D + 512:(p + 1) * D]),
                    [tx2])
                ta2 = vop(lambda e, p=p: e.bn_aggr(bst2[:, 16 + 2 * p:16 + 2 * p + 2],
                                                   bstats[:, c0:c0 + 12]), [t1])
                pvb2[p] = ta2
                va = bst2[:, 16 + 2 * p + 1:16 + 2 * p + 2]
                tl = sop(lambda e, va=va: e.activation(out=va, in_=va, func=AF.Ln,
                                                       bias=cpool[:, 0:1]),
                         [ta2, pvap2[p]])
                tl = sop(lambda e, va=va: e.activation(out=va, in_=va, func=AF.Exp,
                                                       scale=cpool[:, 1:2]), [tl])
                tnb = vop(lambda e, p=p, va=va: e.scalar_tensor_tensor(
                    out=bst2[:, 24 + p:25 + p], in0=bst2[:, 16 + 2 * p:16 + 2 * p + 1],
                    scalar=-1.0, in1=va, op0=ALU.mult, op1=ALU.mult), [tl])
                tap = None
                for hf in range(2):
                    tap = vop(lambda e, p=p, hf=hf, va=va: e.tensor_scalar(
                        out=o0t[:, p * D + hf * 512:p * D + hf * 512 + 512],
                        in0=xs_r[xrB][:, p * D + hf * 512:p * D + hf * 512 + 512],
                        scalar1=va, scalar2=bst2[:, 24 + p:25 + p],
                        op0=ALU.mult, op1=ALU.add), [tnb])
                pvap2[p] = tap
                st[p]["ln2"] = tap
                if p == NPT - 1:
                    pv_xs[xrB] = [tap] + x2_spill[sB]

            def stgap(p):
                return

            def st7(p):
                if sB < 0:
                    return
                tln2 = st[p]["ln2"]
                tev3 = None
                for half in range(2):
                    obk = 2 + half
                    def mmo(e, p=p, half=half, obk=obk):
                        r = None
                        for gg in range(4):
                            g = half * 4 + gg
                            r = e.matmul(ps[obk][:, gg * 128:(gg + 1) * 128],
                                         o0t[:, p * D + g * 128:p * D + (g + 1) * 128],
                                         idf32, start=True, stop=True)
                        return r
                    tmo = top(mmo, [tln2, bank_free[obk]])
                    tev3 = sop(lambda e, p=p, half=half, obk=obk: e.copy(
                        out=r3(o0f, 512)[:, half * 4:(half + 1) * 4, p * 128:(p + 1) * 128],
                        in_=r3(ps[obk], 128)), [tmo, pv["o0f"]])
                    bank_free[obk] = tev3
                pv_ln3[p] = tev3
                if p == NPT - 1:
                    tgp = vop(lambda e: e.reduce_sum(pooled[:, 0:8], r3(o0f, 512),
                                                     axis=mybir.AxisListType.X),
                              [tev3, gsum_tok[0]])
                    if sB == 0:
                        tgp = vop(lambda e: e.tensor_copy(out=gsum, in_=pooled[:, 0:8]),
                                  [tgp])
                    else:
                        tgp = vop(lambda e: e.tensor_add(out=gsum, in0=gsum,
                                                         in1=pooled[:, 0:8]),
                                  [tgp, gsum_tok[0]])
                    gsum_tok[0] = tgp
                    o0_spill[sB] = dma("gpsimd", sc_oa[sB], o0f, [tev3])
                    pv["o0f"] = o0_spill[sB]

            if ab:
                stages = (st0, st1, st2, st3, st4, st5, st6, stgap, stgap,
                          stgap, st7)
            else:
                stages = (st5, st6)
            for k in range(len(stages) + NPT - 1):
                for p in reversed(range(NPT)):
                    kk = k - p
                    if 0 <= kk < len(stages):
                        stages[kk](p)
            if ab:
                AB[s] = (st[NPT - 1]["ta"], st[NPT - 1]["lm"])
                STA[s] = st
            else:
                tail_st["st"] = st

        tail_st = {}
        w3toks = []
        for s in range(NSEG + 1):
            ph2_A(s)
            if s < NSEG:
                pv["gq2"][s % 2] = S.fence(("tt",))[0]
                pv["lg"] = S.fence(("vv",))[0]
                pv["gr"] = pv["lg"]
                pv["lm"] = AB[s][1]
            if s == NSEG - 1:
                w3toks = [st[3]["trm"] for st in (STA[s],)][0:1] + [STA[s][3]["tw"]]

        def emit_st7_tail():
            sB = NSEG - 1
            stT = tail_st["st"]
            for p in range(NPT):
                tln2 = stT[p]["ln2"]
                tev3 = None
                for half in range(2):
                    obk = 2 + half
                    def mmo(e, p=p, half=half, obk=obk):
                        r = None
                        for gg in range(4):
                            g = half * 4 + gg
                            r = e.matmul(ps[obk][:, gg * 128:(gg + 1) * 128],
                                         o0t[:, p * D + g * 128:p * D + (g + 1) * 128],
                                         idf32, start=True, stop=True)
                        return r
                    tmo = top(mmo, [tln2, bank_free[obk]])
                    tev3 = sop(lambda e, p=p, half=half, obk=obk: e.copy(
                        out=r3(o0f, 512)[:, half * 4:(half + 1) * 4, p * 128:(p + 1) * 128],
                        in_=r3(ps[obk], 128)), [tmo, pv["o0f"]])
                    bank_free[obk] = tev3
                if p == NPT - 1:
                    tgp = vop(lambda e: e.reduce_sum(pooled[:, 0:8], r3(o0f, 512),
                                                     axis=mybir.AxisListType.X),
                              [tev3, gsum_tok[0]])
                    tgp = vop(lambda e: e.tensor_add(out=gsum, in0=gsum,
                                                     in1=pooled[:, 0:8]),
                              [tgp, gsum_tok[0]])
                    gsum_tok[0] = tgp
                    o0_spill[sB] = dma("gpsimd", sc_oa[sB], o0f, [tev3])
                    pv["o0f"] = o0_spill[sB]

        ph2 = S.fence()

        t_cn1 = ldw(w_cn1, cn1_d, 8, 1024, w3toks)
        t_cn2 = ldw(w_cn2, cn2_d, 8, 512, w3toks)
        t_cn3 = ldw(w_cn3, cn3_d, 4, 256, w3toks)
        t_cn4 = ldw(w_cn4, cn4_d, 2, 1, w3toks)
        t_gc1 = ldw(w_gc1, g1_d, 8, 512, w3toks)
        t_gc2 = ldw(w_gc2, g2_d, 4, 256, w3toks)
        t_gc3 = ldw(w_gc3, g3_d, 2, 1, w3toks)

        ph25_tok = {}

        def emit_ar():
            t_gso = dma("sync", cc_gi, gsum, [gsum_tok[0]])
            if "noar" in DEBUG_CUT:
                t_arr = dma("gpsimd", cc_go, cc_gi, [t_gso])
            else:
                t_arr = S.add("gpsimd", lambda e: e.collective_compute(
                    "AllReduce", ALU.add, replica_groups=[[0, 1, 2, 3], [4, 5, 6, 7]],
                    ins=[cc_gi], outs=[cc_go]), [t_gso], semname="cc", inc=1)
            ph25_tok["gsi"] = dma("sync", gsum, cc_go, [t_arr, gsum_tok[0]])

        def emit_ph25():
            tmn = vop(lambda e: e.tensor_scalar(out=gsum, in0=gsum, scalar1=1.0 / 16384,
                                                scalar2=None, op0=ALU.mult),
                      [ph25_tok["gsi"]] + ph2)
            tmn = vop(lambda e: e.tensor_copy(out=mean_bf, in_=gsum), [tmn])
            th = None
            for i in range(4):
                th = top(lambda e, i=i: [e.matmul(
                    ps[0][:, i:i + 1], w_gc1[:, k * 512 + i * 128:k * 512 + (i + 1) * 128],
                    mean_bf[:, k:k + 1], start=(k == 0), stop=(k == 7)) for k in range(8)][-1],
                    [tmn, bank_free[0]] + t_gc1)
            tg1e = None
            for i in range(4):
                tg1e = sop(lambda e, i=i: e.activation(
                    out=hg1[:, i:i + 1], in_=ps[0][:, i:i + 1], func=AF.Gelu,
                    bias=biases[:, 14 + i:15 + i]), [th])
            bank_free[0] = tg1e
            th2 = None
            for i in range(2):
                th2 = top(lambda e, i=i: [e.matmul(
                    ps[1][:, i:i + 1], w_gc2[:, k * 256 + i * 128:k * 256 + (i + 1) * 128],
                    hg1[:, k:k + 1], start=(k == 0), stop=(k == 3)) for k in range(4)][-1],
                    [tg1e, bank_free[1]] + t_gc2)
            tg2e = None
            for i in range(2):
                tg2e = sop(lambda e, i=i: e.activation(
                    out=hg2[:, i:i + 1], in_=ps[1][:, i:i + 1], func=AF.Gelu,
                    bias=biases[:, 18 + i:19 + i]), [th2])
            bank_free[1] = tg2e
            tm3 = top(lambda e: [e.matmul(ps[3][0:1, 0:1], w_gc3[:, k:k + 1], hg2[:, k:k + 1],
                                          start=(k == 0), stop=(k == 1)) for k in range(2)][-1],
                      [tg2e, bank_free[3]] + t_gc3)
            tgf = sop(lambda e: e.activation(out=ar_sb[0:1, 4:5], in_=ps[3][0:1, 0:1], func=AF.Exp,
                                             scale=cpool[0:1, 3:4], bias=biases[0:1, 23:24]), [tm3])
            bank_free[3] = tgf
            tgf = vop(lambda e: e.tensor_scalar(out=ar_sb[0:1, 4:5], in0=ar_sb[0:1, 4:5],
                                                scalar1=1.0, scalar2=None, op0=ALU.add), [tgf])
            tgf = vop(lambda e: e.reciprocal(out=ar_sb[0:1, 4:5], in_=ar_sb[0:1, 4:5]), [tgf])
            tab = vop(lambda e: e.tensor_scalar(out=ar_sb[0:1, 3:4], in0=ar_sb[0:1, 4:5],
                                                scalar1=1.4, scalar2=-0.7, op0=ALU.mult,
                                                op1=ALU.add), [tgf])
            tab = vop(lambda e: e.tensor_mul(out=ar_sb[0:1, 3:4], in0=ar_sb[0:1, 3:4],
                                             in1=ar_sb[0:1, 0:1]), [tab])
            tab = vop(lambda e: e.tensor_scalar(out=ar_sb[0:1, 1:2], in0=ar_sb[0:1, 3:4],
                                                scalar1=1.0, scalar2=0.1, op0=ALU.mult,
                                                op1=ALU.add), [tab])
            tab = vop(lambda e: e.tensor_scalar(out=ar_sb[0:1, 2:3], in0=ar_sb[0:1, 0:1],
                                                scalar1=0.3, scalar2=None, op0=ALU.mult), [tab])
            ph25_tok["tab"] = tab

        if DEBUG_PHASES <= 3:
            emit_st7_tail()
            emit_ar()
            emit_ph25()
            finish_debug()
            return nc

        A32.reset(); A16.reset()
        A32.mark(); A16.mark()
        of_ring = [A32.alloc(8 * 512) for _ in range(2)]
        o3t = A32.alloc(NPT * D)
        tmpq = A32.alloc(1024)
        x2l = A32.alloc(NPT * D)
        crow = A32.alloc(512)
        cb_sb = A32.alloc(512)
        ob_ring = [A16.alloc(8 * 512) for _ in range(2)]
        h1 = A16.alloc(8 * 512)
        h2 = A16.alloc(4 * 512)
        h3 = A16.alloc(2 * 512)
        crow_bf = A16.alloc(512)

        of_free = [None, None]
        ob_free = [None, None]
        st_tok = [o0_spill[s] for s in range(NSEG)]
        dep_ld = S.fence(("tt", "ss", "vv"))

        def stok(s):
            return st_tok[s] if st_tok[s] is not None else o0_spill[s]
        p3 = {"h1": None, "h2": None, "h3": None, "crow": None, "cb": None,
              "cb2": None, "tmpq": None, "y": None, "x2l": None}
        osrc = [sc_oa, sc_ob, sc_oa]
        odst = [sc_ob, sc_oa, None]
        TWSRC = {0: 4, 1: 5, 2: 6, 3: 7, 4: 0, 5: 1, 6: 2, 7: 3}
        TWSGN = {0: 1, 1: -1, 2: -1, 3: 1}

        iters = [(cyc, s) for cyc in range(3) for s in range(NSEG)]
        pend = {}

        def ph3_loads(cyc, s):
            xr = s % 2
            if cyc == 0 and s == 0:
                t_of = dma("sync", of_ring[0], osrc[0][0],
                           [stok(0)] + list(pv_xs[0]))
                t_ob = dma("gpsimd", ob_ring[0], osrc[0][0],
                           [stok(0), AB[NSEG - 1][1]])
            else:
                xdep = x2_spill[6 + xr] if cyc == 0 and s < 2 else []
                t_of = dma("sync", of_ring[xr], osrc[cyc][s],
                           [of_free[xr], stok(s)] + dep_ld + xdep)
                t_ob = dma("gpsimd", ob_ring[xr], osrc[cyc][s],
                           [ob_free[xr], stok(s)] + dep_ld)
            pend[(cyc, s)] = (t_of, t_ob)

        ph3_loads(0, 0)
        ph3_loads(0, 1)
        for it_idx, (cyc, s) in enumerate(iters):
            if True:
                xr = s % 2
                t_of, t_ob = pend.pop((cyc, s))
                ev1 = None
                for i in range(8):
                    bk = i % 2
                    tm = top(lambda e, i=i, bk=bk, xr=xr: [e.matmul(
                        ps[bk][:, 0:512], w_cn1[:, k * 1024 + i * 128:k * 1024 + (i + 1) * 128],
                        ob_ring[xr][:, k * 512:(k + 1) * 512], start=(k == 0), stop=(k == 7))
                        for k in range(8)][-1], [t_ob, bank_free[bk]] + t_cn1)
                    ev1 = sop(lambda e, i=i, bk=bk: e.activation(
                        out=h1[:, i * 512:(i + 1) * 512], in_=ps[bk], func=AF.Gelu,
                        bias=biases[:, i:i + 1]), [tm, p3["h1"]])
                    bank_free[bk] = ev1
                p3["h1"] = ev1
                if it_idx == 0:
                    emit_st7_tail()
                    emit_ar()
                ev2 = None
                for i in range(4):
                    bk = i % 2
                    tm = top(lambda e, i=i, bk=bk: [e.matmul(
                        ps[bk][:, 0:512], w_cn2[:, k * 512 + i * 128:k * 512 + (i + 1) * 128],
                        h1[:, k * 512:(k + 1) * 512], start=(k == 0), stop=(k == 7))
                        for k in range(8)][-1], [ev1, bank_free[bk]] + t_cn2)
                    ev2 = sop(lambda e, i=i, bk=bk: e.activation(
                        out=h2[:, i * 512:(i + 1) * 512], in_=ps[bk], func=AF.Gelu,
                        bias=biases[:, 8 + i:9 + i]), [tm, p3["h2"]])
                    bank_free[bk] = ev2
                p3["h2"] = ev2
                ev3 = None
                for i in range(2):
                    bk = i % 2
                    tm = top(lambda e, i=i, bk=bk: [e.matmul(
                        ps[bk][:, 0:512], w_cn3[:, k * 256 + i * 128:k * 256 + (i + 1) * 128],
                        h2[:, k * 512:(k + 1) * 512], start=(k == 0), stop=(k == 3))
                        for k in range(4)][-1], [ev2, bank_free[bk]] + t_cn3)
                    ev3 = sop(lambda e, i=i, bk=bk: e.activation(
                        out=h3[:, i * 512:(i + 1) * 512], in_=ps[bk], func=AF.Gelu,
                        bias=biases[:, 12 + i:13 + i]), [tm, p3["h3"]])
                    bank_free[bk] = ev3
                p3["h3"] = ev3
                tm4 = top(lambda e: [e.matmul(ps[2][0:1, 0:512], w_cn4[:, k:k + 1],
                                              h3[:, k * 512:(k + 1) * 512],
                                              start=(k == 0), stop=(k == 1)) for k in range(2)][-1],
                          [ev3, bank_free[2]] + t_cn4)
                if it_idx == 0:
                    emit_ph25()
                tcw = sop(lambda e: e.activation(out=crow[0:1, :], in_=ps[2][0:1, 0:512],
                                                 func=AF.Tanh, scale=cpool[0:1, 4:5],
                                                 bias=biases[0:1, 22:23]),
                          [tm4, p3["crow"], o0_spill[NSEG - 1]])
                bank_free[2] = tcw
                tcw = vop(lambda e: e.tensor_scalar(out=crow[0:1, :], in0=crow[0:1, :],
                                                    scalar1=ar_sb[0:1, 2:3],
                                                    scalar2=ar_sb[0:1, 1:2],
                                                    op0=ALU.mult, op1=ALU.add),
                          [tcw, ph25_tok["tab"]])
                tcw = vop(lambda e: e.tensor_copy(out=crow_bf[0:1, :], in_=crow[0:1, :]), [tcw])
                p3["crow"] = tcw
                tcb = top(lambda e: e.matmul(ps[3][:, 0:512], oner, crow_bf[0:1, :],
                                             start=True, stop=True),
                          [tcw, bank_free[3], p3["cb"]])
                tcs = sop(lambda e: e.copy(out=cb_sb, in_=ps[3][:, 0:512]),
                          [tcb, p3["cb2"]])
                bank_free[3] = tcs
                tup = None
                tmfl = None
                tevl = None
                for f4 in range(4):
                    fa, fb = f4, f4 + 4
                    ta_ = vop(lambda e, fb=fb, xr=xr: e.tensor_tensor(
                        out=tmpq[:, 0:512], in0=cb_sb,
                        in1=of_ring[xr][:, fb * 512:(fb + 1) * 512], op=ALU.mult),
                        [tcs, t_of, p3["tmpq"], o0_spill[NSEG - 1]])
                    tb_ = vop(lambda e, fa=fa, xr=xr: e.tensor_tensor(
                        out=tmpq[:, 512:1024], in0=cb_sb,
                        in1=of_ring[xr][:, fa * 512:(fa + 1) * 512], op=ALU.mult),
                        [tcs, t_of])
                    opa = ALU.add if TWSGN[fa // 2] > 0 else ALU.subtract
                    opb = ALU.add if TWSGN[fb // 2] > 0 else ALU.subtract
                    tu1 = vop(lambda e, fa=fa, xr=xr, opa=opa: e.tensor_tensor(
                        out=of_ring[xr][:, fa * 512:(fa + 1) * 512],
                        in0=of_ring[xr][:, fa * 512:(fa + 1) * 512],
                        in1=tmpq[:, 0:512], op=opa), [ta_, tb_])
                    tup = vop(lambda e, fb=fb, xr=xr, opb=opb: e.tensor_tensor(
                        out=of_ring[xr][:, fb * 512:(fb + 1) * 512],
                        in0=of_ring[xr][:, fb * 512:(fb + 1) * 512],
                        in1=tmpq[:, 512:1024], op=opb), [tu1])
                    p3["tmpq"] = tup
                    if cyc == 2:
                        for g, bk, tk in ((fa, 4, tu1), (fb, 5, tup)):
                            def mmf(e, g=g, bk=bk, xr=xr):
                                r = None
                                for p in range(NPT):
                                    r = e.matmul(ps[bk][:, p * 128:(p + 1) * 128],
                                                 of_ring[xr][:, g * 512 + 128 * p:g * 512 + 128 * p + 128],
                                                 idf32, start=True, stop=True)
                                return r
                            tmfl = top(mmf, [tk, bank_free[bk]])
                            tevl = sop(lambda e, g=g, bk=bk: e.copy(
                                out=r3(o3t, D)[:, :, g * 128:(g + 1) * 128],
                                in_=r3(ps[bk][:, 0:512], 128)), [tmfl, p3["y"]])
                            bank_free[bk] = tevl
                p3["cb"] = tup
                p3["cb2"] = tup
                ob_free[xr] = ev1
                if it_idx + 1 < len(iters) and iters[it_idx + 1] not in pend:
                    ph3_loads(*iters[it_idx + 1])
                if cyc < 2:
                    st_tok[s] = dma("gpsimd", odst[cyc][s], of_ring[xr], [tup])
                    of_free[xr] = st_tok[s]
                else:
                    t_x2 = dma("gpsimd", x2l, sc_x2[s],
                               x2_spill[s] + [p3["x2l"], o0_spill[NSEG - 1]])
                    tev4 = tevl
                    of_free[xr] = tmfl
                    ln4 = layernorm(lambda p: (o3t[:, p * D:p * D + 512],
                                               o3t[:, p * D + 512:(p + 1) * D]),
                                    lambda p: o3t[:, p * D:(p + 1) * D], NPT,
                                    [tev4])
                    tyv = None
                    for p in range(NPT):
                        tyv = vop(lambda e, p=p: e.tensor_add(
                            out=o3t[:, p * D:(p + 1) * D], in0=o3t[:, p * D:(p + 1) * D],
                            in1=x2l[:, p * D:(p + 1) * D]), [ln4[-1], t_x2])
                    ty = dma("sync", y_out[s], o3t, [tyv])
                    p3["y"] = ty
                    p3["x2l"] = tyv

        S.add("sync", None, S.dma_fence(), semname="fin", inc=1)

        with nc.Block() as block:
            S.emit(block)

    return nc


def _host_prep(inputs, core):
    b, q = core // 4, core % 4
    x = np.asarray(inputs["L0_attn_x"])
    xc = np.asarray(x[b, q * T:(q + 1) * T], np.float32)
    xseg = np.ascontiguousarray(
        xc.reshape(NSEG, NPT, 128, D).transpose(0, 2, 1, 3).reshape(NSEG, 128, NPT * D))
    if q == 0:
        xh = np.zeros((256, D), np.float32)
    else:
        xh = np.asarray(x[b, q * T - 256:q * T], np.float32)
    xhalo = np.ascontiguousarray(xh.reshape(2, 128, D).transpose(1, 0, 2).reshape(128, 2 * D))

    jj = np.arange(128)[:, None]
    ii = np.arange(256)[None, :]
    m0 = (ii < jj).astype(np.float32)
    m1 = (ii < jj + 128).astype(np.float32)
    m2 = (ii >= jj).astype(np.float32)
    m3 = (ii >= jj + 128).astype(np.float32)
    wmask = np.concatenate([m0, m1, m2, m3], axis=1).astype(BF)
    if q == 0:
        wmask0 = np.concatenate([0 * m0, 0 * m1, m2, m3], axis=1).astype(BF)
    else:
        wmask0 = wmask

    lat = np.zeros((NSEG, 128, NPT * 512), np.float32)
    for s in range(NSEG):
        for p in range(NPT):
            C = q * 16 + (s * NPT + p) // 2
            n_ok = min(C + 1, 64)
            for lh in range(8):
                lat[s, :, p * 512 + lh * 64: p * 512 + lh * 64 + n_ok] = 1.0
    latmask = lat.astype(BF)

    glw = np.asarray(inputs["L0_attn_gate_local_w"], np.float32)
    grw = np.asarray(inputs["L0_attn_gate_remote_w"], np.float32)
    wgate = np.zeros((D, 32), np.float32)
    for h in range(16):
        wgate[64 * h:64 * h + 64, h] = glw[h]
        wgate[64 * h:64 * h + 64, 16 + h] = grw[h]

    f32 = np.float32
    g = lambda k: np.asarray(inputs[k], f32)
    return {
        "xseg": xseg, "xhalo": xhalo,
        "wq": g("L0_attn_wq"), "wk": g("L0_attn_wk"), "wv": g("L0_attn_wv"),
        "wgq": g("L0_attn_wgq"), "wgk": g("L0_attn_wgk"), "wgv": g("L0_attn_wgv"),
        "r_out": g("L0_attn_r_out_w"), "w_out": g("L0_attn_w_out"), "wgate": wgate,
        "cn_w1": g("L0_ffn_cn_w1"), "cn_w2": g("L0_ffn_cn_w2"),
        "cn_w3": g("L0_ffn_cn_w3"), "cn_w4": g("L0_ffn_cn_w4"),
        "cn_b1": np.ascontiguousarray(g("L0_ffn_cn_b1").reshape(8, 128).T),
        "cn_b2": np.ascontiguousarray(g("L0_ffn_cn_b2").reshape(4, 128).T),
        "cn_b3": np.ascontiguousarray(g("L0_ffn_cn_b3").reshape(2, 128).T),
        "cn_b4": g("L0_ffn_cn_b4").reshape(1, 1),
        "gc_w1": g("L0_ffn_gc_w1"), "gc_w2": g("L0_ffn_gc_w2"), "gc_w3": g("L0_ffn_gc_w3"),
        "gc_b1": np.ascontiguousarray(g("L0_ffn_gc_b1").reshape(4, 128).T),
        "gc_b2": np.ascontiguousarray(g("L0_ffn_gc_b2").reshape(2, 128).T),
        "gc_b3": g("L0_ffn_gc_b3").reshape(1, 1),
        "ar": g("L0_ffn_adaptive_range").reshape(1, 1),
        "winmask": wmask, "winmask0": wmask0, "latmask": latmask,
        "ident_bf": np.eye(128, dtype=np.float32).astype(BF),
        "ident_f32": np.eye(128, dtype=np.float32),
        "ones_col": np.ones((128, 1), BF),
        "ones_row": np.ones((1, 128), BF),
    }


def kernel(**inputs):
    if "nc" not in _CACHE:
        _CACHE["nc"] = build_nc()
    nc = _CACHE["nc"]
    in_maps = [_host_prep(inputs, c) for c in range(8)]
    res = run_bass_kernel_spmd(nc, in_maps, core_ids=list(range(8)))
    out = np.zeros((2, 4 * T, D), np.float32)
    for c in range(8):
        b, q = c // 4, c % 4
        y = np.asarray(res.results[c]["y"])
        out[b, q * T:(q + 1) * T] = y.reshape(NSEG, 128, NPT, D).transpose(
            0, 2, 1, 3).reshape(T, D)
    return out


# version 15
# speedup vs baseline: 1.0770x; 1.0770x over previous
import numpy as np
import ml_dtypes
import concourse.bass as bass
import concourse.mybir as mybir
from concourse.bass_utils import run_bass_kernel_spmd

F32 = mybir.dt.float32
BF16 = mybir.dt.bfloat16
AF = mybir.ActivationFunctionType
ALU = mybir.AluOpType
BF = ml_dtypes.bfloat16

NSEG = 8
NPT = 4
T = 4096
D = 1024
NH = 16

ENGINES = ("sync", "gpsimd", "vector", "scalar", "tensor")
_CACHE = {}
DEBUG_PHASES = 99
DEBUG_CUT = set()


class Sched:
    def __init__(self, nc, stack):
        self.nc = nc
        self.items = {e: [] for e in ENGINES}
        self.sems = {}
        self.cnt = {}
        self._stack = stack

    def sem(self, name):
        if name not in self.sems:
            self.sems[name] = self._stack.enter_context(self.nc.semaphore(name))
            self.cnt[name] = 0
        return self.sems[name]

    def add(self, eng, fn, deps=(), semname=None, inc=None):
        deps = [d for d in deps if d is not None]
        if semname is None:
            if eng in ("sync", "gpsimd"):
                base = "dx" if eng == "sync" else "dg"
                k = self.cnt.get(base + "_n", 0)
                self.cnt[base + "_n"] = k + 1
                semname = f"{base}{k % 4}"
                self.sem(semname)
                if self.cnt[semname] > 0:
                    deps.append((semname, self.cnt[semname]))
                inc = 16
            else:
                semname = {"vector": "vv", "scalar": "ss", "tensor": "tt"}[eng]
                inc = 1
        elif inc is None:
            inc = 1
        self.sem(semname)
        for (sn, val) in deps:
            assert val <= self.cnt[sn], f"future dep {sn}:{val}>{self.cnt[sn]}"
        self.cnt[semname] += inc
        self.items[eng].append((fn, tuple(deps), semname, inc))
        return (semname, self.cnt[semname])

    DMASEMS = ("dx0", "dx1", "dx2", "dx3", "dg0", "dg1", "dg2", "dg3")

    def fence(self, names=("vv", "ss", "tt") + DMASEMS):
        return [(n, self.cnt[n]) for n in names if n in self.cnt and self.cnt[n] > 0]

    def dma_fence(self):
        return [(n, self.cnt[n]) for n in self.DMASEMS if n in self.cnt and self.cnt[n] > 0]

    def barrier(self):
        name = f"ph{self.cnt.get('_nbar', 0)}"
        self.cnt["_nbar"] = self.cnt.get("_nbar", 0) + 1
        self.sem(name)
        for e in ENGINES:
            self.cnt[name] += 1
            self.items[e].append((None, (), name, 1))
        return [(name, 5)]

    def emit(self, block):
        def mk(engname):
            def body(eng):
                lastw = {}
                for (fn, deps, semname, inc) in self.items[engname]:
                    for (sn, val) in deps:
                        if lastw.get(sn, -1) < val:
                            eng.wait_ge(self.sems[sn], val)
                            lastw[sn] = val
                    inst = eng.nop() if fn is None else fn(eng)
                    inst.then_inc(self.sems[semname], inc)
            return body
        block.sync(mk("sync"))
        block.gpsimd(mk("gpsimd"))
        block.vector(mk("vector"))
        block.scalar(mk("scalar"))
        block.tensor(mk("tensor"))


class Arena:
    def __init__(self, ap, cols):
        self.ap = ap
        self.cols = cols
        self.off = 0
        self.marks = []

    def alloc(self, n):
        o = self.off
        self.off += n
        assert self.off <= self.cols, f"arena overflow {self.off}>{self.cols}"
        return self.ap[:, o:o + n]

    def mark(self):
        self.marks.append(self.off)

    def reset(self):
        self.off = self.marks.pop()


def r3(ap, c):
    return ap.rearrange("q (a b) -> q a b", b=c)


def build_nc():
    nc = bass.Bass("TRN2", target_bir_lowering=False, debug=False)

    def din(name, shape, dt=F32):
        return nc.dram_tensor(name, list(shape), dt, kind="ExternalInput").ap()

    x_in = din("xseg", (NSEG, 128, NPT * D))
    xhalo = din("xhalo", (128, 2 * D))
    wq_d = din("wq", (D, D)); wk_d = din("wk", (D, D)); wv_d = din("wv", (D, D))
    wgq_d = din("wgq", (D, 512)); wgk_d = din("wgk", (D, 512)); wgv_d = din("wgv", (D, 512))
    ro_d = din("r_out", (512, D)); wo_d = din("w_out", (D, D))
    wgate_d = din("wgate", (D, 32))
    cn1_d = din("cn_w1", (D, D)); cn2_d = din("cn_w2", (D, 512))
    cn3_d = din("cn_w3", (512, 256)); cn4_d = din("cn_w4", (256, 1))
    b1_d = din("cn_b1", (128, 8)); b2_d = din("cn_b2", (128, 4))
    b3_d = din("cn_b3", (128, 2)); b4_d = din("cn_b4", (1, 1))
    g1_d = din("gc_w1", (D, 512)); g2_d = din("gc_w2", (512, 256)); g3_d = din("gc_w3", (256, 1))
    gb1_d = din("gc_b1", (128, 4)); gb2_d = din("gc_b2", (128, 2)); gb3_d = din("gc_b3", (1, 1))
    ar_d = din("ar", (1, 1))
    wm_d = din("winmask", (128, 1024), BF16)
    wm0_d = din("winmask0", (128, 1024), BF16)
    lm_d = din("latmask", (NSEG, 128, NPT * 512), BF16)
    idb_d = din("ident_bf", (128, 128), BF16)
    idf_d = din("ident_f32", (128, 128), F32)
    onec_d = din("ones_col", (128, 1), BF16)
    oner_d = din("ones_row", (1, 128), BF16)

    y_out = nc.dram_tensor("y", [NSEG, 128, NPT * D], F32, kind="ExternalOutput").ap()

    sc_localg = nc.dram_tensor("sc_localg", [NSEG, 128, NPT * D], BF16).ap()
    sc_gq = nc.dram_tensor("sc_gq", [NSEG, 128, 4 * 512], BF16).ap()
    sc_gr = nc.dram_tensor("sc_gr", [NSEG, 128, NPT * 16], F32).ap()
    sc_x2 = nc.dram_tensor("sc_x2", [NSEG, 128, NPT * D], F32).ap()
    sc_oa = nc.dram_tensor("sc_oa", [NSEG, 128, 8 * 512], F32).ap()
    sc_ob = nc.dram_tensor("sc_ob", [NSEG, 128, 8 * 512], F32).ap()
    cc_pi = nc.dram_tensor("cc_pi", [16, D], F32).ap()
    cc_po = nc.dram_tensor("cc_po", [64, D], F32).ap()
    cc_gi = nc.dram_tensor("cc_gi", [128, 8], F32).ap()
    cc_go = nc.dram_tensor("cc_go", [128, 8], F32).ap()

    from contextlib import ExitStack
    st = ExitStack()
    with st:
        S = Sched(nc, st)

        def sb(name, shape, dt=F32):
            return st.enter_context(nc.sbuf_tensor(name, list(shape), dt))[:]

        WA = Arena(sb("w_arena", (128, 28930), BF16), 28930)
        A32 = Arena(sb("a32", (128, 19050), F32), 19050)
        A16 = Arena(sb("a16", (128, 28830), BF16), 28830)
        cpool = sb("cpool", (128, 8))
        idbf = sb("idbf", (128, 128), BF16)
        idf32 = sb("idf32", (128, 128), F32)
        onec = sb("onec", (128, 1), BF16)
        oner = sb("oner", (1, 128), BF16)
        wmask = sb("wmask", (128, 1024), BF16)
        wmask0 = sb("wmask0", (128, 1024), BF16)
        stats = sb("stats", (128, 48))
        st2 = sb("st2", (128, 16))
        pooled = sb("pooled", (128, 160))
        gates = sb("gates", (128, NPT * 32))
        glr = sb("glr", (128, 16))
        den = sb("den", (128, 16))
        biases = sb("biases", (128, 24))
        ar_sb = sb("ar_sb", (1, 8))
        gsum = sb("gsum", (128, 8))
        mean_bf = sb("mean_bf", (128, 8), BF16)
        hg1 = sb("hg1", (128, 4), BF16)
        hg2 = sb("hg2", (128, 2), BF16)
        gk_fm = sb("gk_fm", (128, 4 * 64), BF16)
        gv_fm = sb("gv_fm", (128, 4 * 64), BF16)
        gvr = sb("gvr", (128, 8 * 512), BF16)
        pooled_fm = sb("pooled_fm", (128, 8 * 64), BF16)

        def gvr_ap(lh):
            if lh % 2 == 0:
                return gvr[0:64, 1024 * (lh // 2):1024 * (lh // 2) + 1024]
            return w_ro[0:64, 1024 * (lh // 2):1024 * (lh // 2) + 1024]

        ps2 = [st.enter_context(nc.psum_tensor(f"pp{i}", [128, 1024], F32))[:] for i in range(4)]
        ps = [ps2[i // 2][:, 512 * (i % 2):512 * (i % 2) + 512] for i in range(8)]

        w_qkv = WA.alloc(24576)
        w_gq = WA.alloc(4096)
        w_gate = WA.alloc(256)
        WA.off = 0
        w_gkv = WA.alloc(8192)
        w_ro = WA.alloc(4096)
        w_wo = WA.alloc(8192)
        WA.off = 0
        w_cn1 = WA.alloc(8192)
        w_cn2 = WA.alloc(4096)
        w_cn3 = WA.alloc(1024)
        w_cn4 = WA.alloc(2)
        w_gc1 = WA.alloc(4096)
        w_gc2 = WA.alloc(1024)
        w_gc3 = WA.alloc(2)

        dma = lambda eng, out, in_, deps=(): S.add(
            eng, lambda e, o=out, i=in_: e.dma_start(out=o, in_=i), deps)
        dma_nc = lambda eng, out, in_, deps=(): S.add(
            eng, lambda e, o=out, i=in_: e.dma_start(out=o, in_=i,
                                                     allow_slow_non_contiguous=True), deps)
        vop = lambda fn, deps=(): S.add("vector", fn, deps)
        sop = lambda fn, deps=(): S.add("scalar", fn, deps)
        top = lambda fn, deps=(): S.add("tensor", fn, deps)

        dma("sync", idbf, idb_d)
        dma("sync", idf32, idf_d)
        dma("sync", onec, onec_d)
        dma("sync", oner, oner_d)
        dma("sync", wmask, wm_d)
        dma("sync", wmask0, wm0_d)
        dma("sync", ar_sb[0:1, 0:1], ar_d)
        for c, v in enumerate((1e-5, -0.5, 0.125, -1.0, 0.5, 1.0)):
            vop(lambda e, c=c, v=v: e.memset(cpool[:, c:c + 1], v))

        def ldw(dst, src, kt, cols, deps=()):
            tks = {}
            for k in range(kt):
                tk = dma("gpsimd", dst[:, k * cols:(k + 1) * cols],
                         src[k * 128:(k + 1) * 128, :], deps)
                tks[tk[0]] = tk
            return list(tks.values())

        ldw(w_qkv[:, 0:8192], wq_d, 8, 1024)
        ldw(w_qkv[:, 8192:16384], wk_d, 8, 1024)
        ldw(w_qkv[:, 16384:24576], wv_d, 8, 1024)
        ldw(w_gq, wgq_d, 8, 512)
        ldw(w_gate, wgate_d, 8, 32)
        dma("sync", biases[:, 0:8], b1_d)
        dma("sync", biases[:, 8:12], b2_d)
        dma("sync", biases[:, 12:14], b3_d)
        dma("sync", biases[:, 14:18], gb1_d)
        dma("sync", biases[:, 18:20], gb2_d)
        tb = dma("sync", biases[0:1, 20:21], b4_d)
        tb2 = dma("sync", biases[0:1, 21:22], gb3_d)
        vop(lambda e: e.tensor_scalar(out=biases[0:1, 22:23], in0=biases[0:1, 20:21],
                                      scalar1=0.5, scalar2=None, op0=ALU.mult), [tb])
        vop(lambda e: e.tensor_scalar(out=biases[0:1, 23:24], in0=biases[0:1, 21:22],
                                      scalar1=-1.0, scalar2=None, op0=ALU.mult), [tb2])

        A32.mark(); A16.mark()
        x_ring = [A32.alloc(NPT * D) for _ in range(2)]
        graw = A32.alloc(512)
        pooled_t = A32.alloc(128)
        ln1t = A16.alloc(NPT * D)
        q_fm = A16.alloc(8 * 512)
        ln1f = A16.alloc(8 * 512)
        gq_sb = A16.alloc(4 * 512)
        k_ring = [A16.alloc(8 * 256) for _ in range(3)]
        v_ring = [A16.alloc(16 * 65) for _ in range(6)]
        p_t = [A32.alloc(512).bitcast(BF16) for _ in range(4)]
        localg = A32.alloc(NPT * D // 2).bitcast(BF16)

        vr1 = None
        for sl in range(6):
            vr1 = vop(lambda e, sl=sl: e.memset(r3(v_ring[sl], 65)[:, :, 64:65], 1.0))
        ph0 = S.barrier() + S.dma_fence()
        dep_ph1 = ph0 + S.fence(("vv",)) + S.dma_fence()

        bank_free = {i: None for i in range(8)}
        x_free = [None, None]
        pt_free = [None, None, None, None]
        kv_tok = {}
        kv_ktok = {}
        kslot_free = {}
        vslot_free = {}
        ln_gate = [None]
        gq_free = None
        lg_free = None

        def layernorm(xap_fn, dst_fn, npt, deps, apply_deps=(), apply_eng="vector"):
            t1 = None
            dd = list(deps) + [ln_gate[0]]
            for p in range(npt):
                lo, hi = xap_fn(p)
                t1 = vop(lambda e, p=p, lo=lo: e.bn_stats(stats[:, 12 * p:12 * p + 6], lo), dd)
                t1 = vop(lambda e, p=p, hi=hi: e.bn_stats(stats[:, 12 * p + 6:12 * p + 12], hi), dd)
            ta = None
            for p in range(npt):
                ta = vop(lambda e, p=p: e.bn_aggr(st2[:, 2 * p:2 * p + 2],
                                                  stats[:, 12 * p:12 * p + 12]), [t1])
            va = st2[:, 1:2 * npt:2]
            tl = sop(lambda e, va=va: e.activation(out=va, in_=va, func=AF.Ln,
                                                   bias=cpool[:, 0:1]), [ta])
            tl = sop(lambda e, va=va: e.activation(out=va, in_=va, func=AF.Exp,
                                                   scale=cpool[:, 1:2]), [tl])
            tnb = vop(lambda e, va=va, npt=npt: e.scalar_tensor_tensor(
                out=st2[:, 8:8 + npt], in0=st2[:, 0:2 * npt:2], scalar=-1.0, in1=va,
                op0=ALU.mult, op1=ALU.mult), [tl])
            outs = []
            ad = [tnb] + list(apply_deps)
            for p in range(npt):
                lo, hi = xap_fn(p)
                dst = dst_fn(p)
                if apply_eng == "scalar":
                    outs.append(sop(lambda e, p=p, lo=lo, dst=dst: e.activation(
                        out=dst[:, 0:512], in_=lo, func=AF.Identity,
                        scale=st2[:, 2 * p + 1:2 * p + 2],
                        bias=st2[:, 8 + p:9 + p]), ad))
                    outs.append(sop(lambda e, p=p, hi=hi, dst=dst: e.activation(
                        out=dst[:, 512:1024], in_=hi, func=AF.Identity,
                        scale=st2[:, 2 * p + 1:2 * p + 2],
                        bias=st2[:, 8 + p:9 + p]), ad))
                    continue
                outs.append(vop(lambda e, p=p, lo=lo, dst=dst: e.tensor_scalar(
                    out=dst[:, 0:512], in0=lo, scalar1=st2[:, 2 * p + 1:2 * p + 2],
                    scalar2=st2[:, 8 + p:9 + p], op0=ALU.mult, op1=ALU.add), ad))
                outs.append(vop(lambda e, p=p, hi=hi, dst=dst: e.tensor_scalar(
                    out=dst[:, 512:1024], in0=hi, scalar1=st2[:, 2 * p + 1:2 * p + 2],
                    scalar2=st2[:, 8 + p:9 + p], op0=ALU.mult, op1=ALU.add), ad))
            ln_gate[0] = outs[-1]
            return outs

        def transpose_tok2fm(src_tok, npt, dst_fm, deps, fp32=False, pool=False,
                             gsum_first=None):
            ident = idf32 if fp32 else idbf
            ev = None
            for p in range(npt):
                for half in range(2):
                    bk = half
                    def mm(e, p=p, half=half, bk=bk, first=(gsum_first[0] if gsum_first else False)):
                        r = None
                        for gg in range(4):
                            g = half * 4 + gg
                            r = e.matmul(ps[bk][:, gg * 128:(gg + 1) * 128],
                                         src_tok[:, p * 1024 + g * 128:p * 1024 + (g + 1) * 128],
                                         ident, start=True, stop=True)
                            if pool:
                                r = e.matmul(ps[2][:, (g * 4 + p):(g * 4 + p + 1)],
                                             src_tok[:, p * 1024 + g * 128:p * 1024 + (g + 1) * 128],
                                             onec, start=True, stop=True)
                            if gsum_first is not None:
                                r = e.matmul(ps[7][:, g:g + 1],
                                             src_tok[:, p * 1024 + g * 128:p * 1024 + (g + 1) * 128],
                                             cpool[:, 5:6], start=first, stop=False,
                                             skip_group_check=True)
                        return r
                    tm = top(mm, list(deps) + [bank_free[bk]] +
                             ([bank_free[2]] if pool else []))
                    if gsum_first is not None:
                        gsum_first[0] = False
                    ev = sop(lambda e, p=p, half=half, bk=bk: e.copy(
                        out=r3(dst_fm, 512)[:, half * 4:(half + 1) * 4, p * 128:(p + 1) * 128],
                        in_=r3(ps[bk], 128)), [tm])
                    bank_free[bk] = ev
            return ev

        def kv_block(blk, tcol, dep_lnf):
            ks = (blk + 1) % 3
            ev = None
            k_ev = None
            for i in range(8):
                bk = 3 + (i % 2)
                def mmk(e, i=i, bk=bk, tcol=tcol):
                    r = None
                    for k in range(8):
                        r = e.matmul(ps[bk][:, 0:256],
                                     w_qkv[:, 8192 + k * 1024 + i * 128:8192 + k * 1024 + (i + 1) * 128],
                                     ln1f[:, k * 512 + tcol:k * 512 + tcol + 256],
                                     start=(k == 0), stop=(k == 7))
                    return r
                tm = top(mmk, [dep_lnf, bank_free[bk], kslot_free.get(ks)])
                ev = sop(lambda e, i=i, bk=bk, ks=ks: e.copy(
                    out=k_ring[ks][:, i * 256:(i + 1) * 256], in_=ps[bk][:, 0:256]), [tm])
                bank_free[bk] = ev
                k_ev = ev
            for pp in range(2):
                vs = ks * 2 + pp
                for jh in range(2):
                    bk = 5 + jh
                    def mmv(e, pp=pp, jh=jh, tcol=tcol, bk=bk):
                        r = None
                        for k in range(8):
                            r = e.matmul(ps[bk][:, 0:512],
                                         ln1f[:, k * 512 + tcol + pp * 128:k * 512 + tcol + pp * 128 + 128],
                                         w_qkv[:, 16384 + k * 1024 + jh * 512:16384 + k * 1024 + (jh + 1) * 512],
                                         start=(k == 0), stop=(k == 7))
                        return r
                    tm = top(mmv, [dep_lnf, bank_free[bk], vslot_free.get(vs)])
                    ev = sop(lambda e, vs=vs, jh=jh, bk=bk: e.copy(
                        out=r3(v_ring[vs], 65)[:, jh * 8:(jh + 1) * 8, 0:64],
                        in_=r3(ps[bk], 64)), [tm])
                    bank_free[bk] = ev
            kv_tok[blk] = ev
            kv_ktok[blk] = k_ev
            return ev

        def proj(dst, wcols, n_i, stride, jt, deps, banks=(3, 4)):
            evs = []
            ev = None
            for i in range(n_i):
                bk = banks[i % 2]
                def mm(e, i=i, bk=bk):
                    r = None
                    for k in range(8):
                        r = e.matmul(ps[bk][:, 0:jt],
                                     wcols[:, k * stride + i * 128:k * stride + (i + 1) * 128],
                                     ln1f[:, k * 512:k * 512 + jt],
                                     start=(k == 0), stop=(k == 7))
                    return r
                tm = top(mm, list(deps) + [bank_free[bk]])
                ev = sop(lambda e, i=i, bk=bk, jt=jt: e.copy(
                    out=dst[:, i * jt:(i + 1) * jt], in_=ps[bk][:, 0:jt]), [tm])
                bank_free[bk] = ev
                evs.append(ev)
            return evs

        for _b in range(4):
            bank_free[_b] = vop(lambda e, _b=_b: e.memset(ps[_b][:, 0:512], 0.0), ph0)
        t_xh = dma("sync", x_ring[1][:, 0:2 * D], xhalo, dep_ph1)
        ln_h = layernorm(lambda p: (x_ring[1][:, p * D:p * D + 512],
                                    x_ring[1][:, p * D + 512:(p + 1) * D]),
                         lambda p: ln1t[:, p * D:(p + 1) * D], 2, [t_xh] + dep_ph1)
        ev_trh = transpose_tok2fm(ln1t, 2, ln1f, [ln_h[-1]] + dep_ph1)
        t_kvh = kv_block(-1, 0, ev_trh)
        x_free[1] = ln_h[-1]

        lg_spill = [None] * NSEG
        gq_spill = [None] * NSEG
        gr_spill = [None] * NSEG
        glr_war = {}
        ln1f_last_reader = t_kvh
        tp = None
        ln_pipe = {}

        def emit_ln1(s, ln1t_free):
            xr = s % 2
            t_x = dma("sync", x_ring[xr], x_in[s], [x_free[xr]] + dep_ph1)
            ln_a = layernorm(lambda p, xr=xr: (x_ring[xr][:, p * D:p * D + 512],
                                               x_ring[xr][:, p * D + 512:(p + 1) * D]),
                             lambda p: ln1t[:, p * D:(p + 1) * D], NPT, [t_x],
                             apply_deps=[ln1t_free])
            x_free[xr] = ln_a[-1]
            ln_pipe[s] = ln_a

        emit_ln1(0, ev_trh)
        for s in range(NSEG):
            xr = s % 2
            ln_a = ln_pipe.pop(s)
            ev_tr = transpose_tok2fm(ln1t, NPT, ln1f,
                                     [ln_a[-1], ln1f_last_reader])
            for cl in range(2):
                tp = vop(lambda e, cl=cl, s=s: e.reduce_sum(
                    pooled[:, 8 * (2 * s + cl):8 * (2 * s + cl) + 8],
                    r3(ln1f, 512)[:, :, 256 * cl:256 * cl + 256],
                    axis=mybir.AxisListType.X), [ev_tr])
            if s == NSEG - 1:
                tps = vop(lambda e: e.tensor_scalar(out=pooled[:, 0:128], in0=pooled[:, 0:128],
                                                    scalar1=1.0 / 256, scalar2=None, op0=ALU.mult),
                          [tp])
                tpt0 = top(lambda e: e.matmul(ps[3][:, 0:128], pooled[:, 0:128], idf32,
                                              start=True, stop=True),
                           [tps, bank_free[3]])
                tpt0 = vop(lambda e: e.tensor_copy(out=pooled_t, in_=ps[3][:, 0:128]), [tpt0])
                bank_free[3] = tpt0
                t_pool_out = dma("gpsimd", cc_pi, pooled_t, [tpt0])
                t_ag = S.add("gpsimd", lambda e: e.collective_compute(
                    "AllGather", ALU.bypass, replica_groups=[[0, 1, 2, 3], [4, 5, 6, 7]],
                    ins=[cc_pi], outs=[cc_po]), [t_pool_out], semname="cc", inc=1)
            evq_l = proj(q_fm[:, 0:8 * 512], w_qkv[:, 0:8192], 8, 1024, 512, [ev_tr])
            ev_q = evq_l[-1]
            ev_gq = proj(gq_sb, w_gq, 4, 512, 512, [ev_tr, gq_free])[-1]
            gq_spill[s] = dma("gpsimd", sc_gq[s], gq_sb, [ev_gq])
            gq_free = gq_spill[s]
            def mmg(e):
                r = None
                for k in range(8):
                    r = e.matmul(ps[7][0:32, 0:512], w_gate[:, k * 32:(k + 1) * 32],
                                 q_fm[:, k * 512:(k + 1) * 512], start=(k == 0), stop=(k == 7))
                return r
            tg = top(mmg, [ev_q, bank_free[7]])
            tg = vop(lambda e: e.tensor_copy(out=graw[0:32, :], in_=ps[7][0:32, 0:512]), [tg])
            bank_free[7] = tg
            def mmgt(e):
                r = None
                for p in range(NPT):
                    r = e.matmul(ps[7][:, p * 32:(p + 1) * 32], graw[0:32, p * 128:(p + 1) * 128],
                                 idf32[0:32, 0:32], start=True, stop=True)
                return r
            tgt = top(mmgt, [tg, bank_free[7]])
            tge = sop(lambda e: e.activation(out=gates, in_=ps[7][:, 0:128], func=AF.Exp,
                                             scale=cpool[:, 3:4]),
                      [tgt, gr_spill[s - 1] if s > 0 else None])
            tge = vop(lambda e: e.tensor_scalar(out=gates, in0=gates, scalar1=1.0,
                                                scalar2=None, op0=ALU.add), [tge])
            tge = vop(lambda e: e.reciprocal(out=gates, in_=gates), [tge])
            bank_free[7] = tge
            gr_spill[s] = dma("gpsimd", sc_gr[s], r3(gates, 32)[:, :, 16:32], [tge])
            kv_block(2 * s, 0, ev_tr)
            kv_block(2 * s + 1, 256, ev_tr)
            ln1f_last_reader = S.fence(("tt",))[0]
            if s + 1 < NSEG:
                emit_ln1(s + 1, ev_tr)

            for lb in range(2):
                blk = 2 * s + lb
                mask = wmask0 if blk == 0 else wmask
                last_tsc = None
                pv_last = {}

                def gate_grp(hgrp, p_idx_base):
                    nh = (7, 7, 2)[hgrp]
                    hoff = 7 * hgrp
                    for qc in range(2):
                        p_idx = p_idx_base + qc
                        bk = (4 if qc == 0 else 6) + (hgrp & 1)
                        tr_ = vop(lambda e, bk=bk, nh=nh, hoff=hoff: e.reciprocal(
                            out=glr[:, hoff:hoff + nh],
                            in_=r3(ps[bk][:, 0:nh * 65], 65)[:, 0:nh, 64:65].rearrange(
                                "q a b -> q (a b)")), [pv_last[bk], glr_war.get(hgrp)])
                        tr_ = vop(lambda e, nh=nh, hoff=hoff, p_idx=p_idx: e.tensor_mul(
                            out=glr[:, hoff:hoff + nh], in0=glr[:, hoff:hoff + nh],
                            in1=gates[:, 32 * p_idx + hoff:32 * p_idx + hoff + nh]), [tr_, tge])
                        tr_ = vop(lambda e, bk=bk, nh=nh, hoff=hoff, p_idx=p_idx: e.tensor_tensor(
                            out=r3(localg[:, p_idx * D:(p_idx + 1) * D], 64)[:, hoff:hoff + nh, :],
                            in0=r3(ps[bk][:, 0:nh * 65], 65)[:, 0:nh, 0:64],
                            in1=glr[:, hoff:hoff + nh][:, :, None].broadcast_to([128, nh, 64]),
                            op=ALU.mult), [tr_, lg_free])
                        glr_war[hgrp] = tr_
                        bank_free[bk] = tr_

                def emit_sc(h):
                    ptb = p_t[h % 4]
                    sb0, sb1 = (0, 1) if h % 2 == 0 else (2, 3)
                    def mms(e, h=h, blk=blk, lb=lb, sb0=sb0, sb1=sb1):
                        r = None
                        for jc in range(4):
                            kslot = ((blk - 1 if jc < 2 else blk) + 1) % 3
                            kc = 256 * (h // 2) + 128 * (jc % 2)
                            i0_ = 128 if jc == 3 else 0
                            i1_ = 128 if jc == 0 else 256
                            bnk = sb0 if jc < 2 else sb1
                            r = e.matmul(ps[bnk][:, 256 * (jc % 2) + i0_:256 * (jc % 2) + i1_],
                                         k_ring[kslot][64 * (h % 2):64 * (h % 2) + 64, kc:kc + 128],
                                         q_fm[64 * (h % 2):64 * (h % 2) + 64,
                                              512 * (h // 2) + 256 * lb + i0_:512 * (h // 2) + 256 * lb + i1_],
                                         start=True, stop=True)
                        return r
                    tsc = top(mms, [kv_ktok[blk - 1], kv_ktok[blk], evq_l[h // 2],
                                    bank_free[sb0], bank_free[sb1]])
                    te = sop(lambda e, ptb=ptb, sb0=sb0: e.activation(
                        out=ptb, in_=ps2[sb0 // 2], func=AF.Exp, scale=cpool[:, 2:3]),
                        [tsc, pt_free[h % 4]])
                    bank_free[sb0] = te
                    bank_free[sb1] = te
                    tmk = vop(lambda e, ptb=ptb, mask=mask: e.tensor_mul(
                        out=ptb, in0=ptb, in1=mask), [te])
                    hstate[h] = (tmk, tsc)

                def emit_pv(h):
                    ptb = p_t[h % 4]
                    tmk, _ = hstate[h]
                    hgrp = min(h // 7, 2)
                    hoff = h - 7 * hgrp
                    for qc in range(2):
                        bk = (4 if qc == 0 else 6) + (hgrp & 1)
                        jcs = (0, 1, 2) if qc == 0 else (1, 2, 3)
                        def mmpv(e, h=h, qc=qc, bk=bk, hoff=hoff, blk=blk, ptb=ptb, jcs=jcs):
                            r = None
                            for ji, jc in enumerate(jcs):
                                vs = (((blk - 1 if jc < 2 else blk) + 1) % 3) * 2 + (jc % 2)
                                r = e.matmul(ps[bk][:, hoff * 65:hoff * 65 + 65],
                                             ptb[:, 256 * jc + 128 * qc:256 * jc + 128 * qc + 128],
                                             v_ring[vs][:, 65 * h:65 * h + 65],
                                             start=(ji == 0), stop=(ji == 2))
                            return r
                        tpv = top(mmpv, [tmk, bank_free[bk], kv_tok[blk - 1], kv_tok[blk]])
                        pv_last[bk] = tpv
                    pt_free[h % 4] = pv_last[6 + (hgrp & 1)]
                    if h == 6:
                        gate_grp(0, 2 * lb)
                    elif h == 13:
                        gate_grp(1, 2 * lb)
                    elif h == 15:
                        gate_grp(2, 2 * lb)

                hstate = {}
                for h in range(NH):
                    emit_sc(h)
                    if h >= 2:
                        emit_pv(h - 2)
                emit_pv(NH - 2)
                emit_pv(NH - 1)
                last_tsc = hstate[NH - 1][1]
                kslot_free[blk % 3] = last_tsc
                vslot_free[(blk % 3) * 2] = pv_last[6]
                vslot_free[(blk % 3) * 2 + 1] = pv_last[6]
            lg_spill[s] = dma("gpsimd", sc_localg[s], localg,
                              [bank_free[4], bank_free[5], bank_free[6], bank_free[7]])
            lg_free = lg_spill[s]

        def finish_debug():
            tk = None
            for s_ in range(NSEG):
                tk = dma("gpsimd", y_out[s_], x_ring[0])
            S.add("sync", None, S.dma_fence(), semname="fin", inc=1)
            with nc.Block() as block:
                S.emit(block)

        ph1 = S.fence()
        if DEBUG_PHASES <= 1:
            finish_debug()
            return nc

        t_wgk = ldw(w_gkv[:, 0:4096], wgk_d, 8, 512, ph1)
        t_wgv = ldw(w_gkv[:, 4096:8192], wgv_d, 8, 512, ph1)
        t_wro = ldw(w_ro, ro_d, 4, 1024, ph1)
        t_wwo = ldw(w_wo, wo_d, 8, 1024, ph1)

        A32.reset(); A16.reset()
        A32.mark(); A16.mark()
        pooled_tk = A32.alloc(1024)
        t_ptk = dma("sync", pooled_tk[0:64, :], cc_po, [t_ag] + ph1)
        ev_pt = None
        for g in range(8):
            bk = g % 2
            tm = top(lambda e, g=g, bk=bk: e.matmul(
                ps[bk][:, 0:64], pooled_tk[0:64, g * 128:(g + 1) * 128],
                idf32[0:64, 0:64], start=True, stop=True),
                [t_ptk, bank_free[bk]] + ph1)
            ev_pt = vop(lambda e, g=g, bk=bk: e.tensor_copy(
                out=pooled_fm[:, g * 64:(g + 1) * 64], in_=ps[bk][:, 0:64]), [tm])
            bank_free[bk] = ev_pt

        def gkgv(dst, woff, depw):
            ev = None
            for i in range(4):
                bk = 2 + (i % 2)
                def mm(e, i=i, bk=bk, woff=woff):
                    r = None
                    for k in range(8):
                        r = e.matmul(ps[bk][:, 0:64],
                                     w_gkv[:, woff + k * 512 + i * 128:woff + k * 512 + (i + 1) * 128],
                                     pooled_fm[:, k * 64:(k + 1) * 64],
                                     start=(k == 0), stop=(k == 7))
                    return r
                tm = top(mm, [ev_pt, bank_free[bk]] + list(depw))
                ev = vop(lambda e, i=i, bk=bk, dst=dst: e.tensor_copy(
                    out=dst[:, i * 64:(i + 1) * 64], in_=ps[bk][:, 0:64]), [tm])
                bank_free[bk] = ev
            return ev
        ev_gk = gkgv(gk_fm, 0, t_wgk)
        ev_gv = gkgv(gv_fm, 4096, t_wgv)
        ev_gvr = None
        for lh in range(8):
            for jh in range(2):
                bk = 4 + 2 * (lh % 2) + jh
                def mm(e, lh=lh, jh=jh, bk=bk):
                    return e.matmul(ps[bk][0:64, 0:512],
                                    gv_fm[64 * (lh % 2):64 * (lh % 2) + 64,
                                          64 * (lh // 2):64 * (lh // 2) + 64],
                                    w_ro[64 * (lh % 2):64 * (lh % 2) + 64,
                                         (lh // 2) * 1024 + jh * 512:(lh // 2) * 1024 + (jh + 1) * 512],
                                    start=True, stop=True)
                tm = top(mm, [ev_gv, bank_free[bk]] + t_wro)
                ev_gvr = vop(lambda e, lh=lh, jh=jh, bk=bk: e.tensor_copy(
                    out=gvr_ap(lh)[:, jh * 512:(jh + 1) * 512], in_=ps[bk][0:64, 0:512]), [tm])
                bank_free[bk] = ev_gvr
        ph15 = S.fence()
        if DEBUG_PHASES <= 2:
            finish_debug()
            return nc

        A32.reset(); A16.reset()
        A32.mark(); A16.mark()
        xs = A32.alloc(NPT * D)
        ln3 = A32.alloc(NPT * D)
        o0t = ln3
        o0f = A32.alloc(8 * 512)
        remg_r = [A32.alloc(D) for _ in range(2)]
        lmask = A16.alloc(NPT * 512)
        plat_r = [A16.alloc(512) for _ in range(2)]
        platT = A16.alloc(8 * 512)
        lgseg = A16.alloc(NPT * D)
        comb_r = [A16.alloc(D) for _ in range(2)]
        comb_f = A16.alloc(8 * 256)
        gq2r = [A16.alloc(4 * 512) for _ in range(2)]
        attn_r = [A16.alloc(NPT * D) for _ in range(2)]

        gsum_tok = [None]
        x2_spill = [None] * NSEG
        o0_spill = [None] * NSEG
        pv = {"gq2": [None, None], "lg": None, "plat": [None, None],
              "remg": [None, None], "comb": [None, None],
              "o0f": None, "o0t": None, "ln3": None, "x2": None, "gr": None,
              "lm": None, "attn": [None, None], "xs": None, "den": [None, None]}
        AB = {}

        def ph2_A(s):
            xr = s % 2
            gq2 = gq2r[s % 2]
            t_lm = dma("sync", lmask, lm_d[s], [pv["lm"]] + ph15)
            t_gq = dma("sync", gq2, sc_gq[s], [gq_spill[s], pv["gq2"][s % 2]] + ph15)
            t_lg = dma("sync", lgseg, sc_localg[s], [lg_spill[s], pv["lg"]] + ph15)
            t_gr = dma("sync", gates[:, 0:64], sc_gr[s], [gr_spill[s], pv["gr"]] + ph15)
            st = [dict() for _ in range(NPT)]

            def st0(p):
                pr = p % 2
                plat = plat_r[pr]
                dn = den[:, 8 * pr:8 * pr + 8]
                def mml(e, p=p):
                    r = None
                    for lh in range(8):
                        bnk = 0 if lh % 2 == 0 else 1
                        r = e.matmul(ps[bnk][:, 64 * (lh // 2):64 * (lh // 2) + 64],
                                     gq2[64 * (lh % 2):64 * (lh % 2) + 64,
                                         512 * (lh // 2) + 128 * p:512 * (lh // 2) + 128 * p + 128],
                                     gk_fm[64 * (lh % 2):64 * (lh % 2) + 64,
                                           64 * (lh // 2):64 * (lh // 2) + 64],
                                     start=True, stop=True)
                    return r
                tl = top(mml, [t_gq, ev_gk, bank_free[0], bank_free[1]])
                te = sop(lambda e, plat=plat: e.activation(
                    out=r3(plat, 64)[:, 0::2, :], in_=r3(ps[0][:, 0:256], 64),
                    func=AF.Exp, scale=cpool[:, 2:3]), [tl, pv["plat"][pr]])
                te = sop(lambda e, plat=plat: e.activation(
                    out=r3(plat, 64)[:, 1::2, :], in_=r3(ps[1][:, 0:256], 64),
                    func=AF.Exp, scale=cpool[:, 2:3]), [tl])
                bank_free[0] = te
                bank_free[1] = te
                tm_ = vop(lambda e, p=p, plat=plat: e.tensor_mul(
                    out=plat, in0=plat, in1=lmask[:, p * 512:(p + 1) * 512]), [te, t_lm])
                st[p]["lm"] = tm_
                td = vop(lambda e, plat=plat, dn=dn: e.reduce_sum(
                    dn, r3(plat, 64), axis=mybir.AxisListType.X), [tm_, pv["den"][pr]])
                td = vop(lambda e, dn=dn: e.reciprocal(out=dn, in_=dn), [td])
                tn = vop(lambda e, plat=plat, dn=dn: e.tensor_tensor(
                    out=r3(plat, 64), in0=r3(plat, 64),
                    in1=dn[:, :, None].broadcast_to([128, 8, 64]), op=ALU.mult), [td])
                pv["plat"][pr] = tn
                pv["den"][pr] = tn
                st[p]["tn"] = tn

            def st1(p):
                pr = p % 2
                plat = plat_r[pr]
                tn = st[p]["tn"]
                tts = {}
                for lh in range(8):
                    bk = 2 + (lh % 2)
                    tts[bk] = top(lambda e, lh=lh, bk=bk, plat=plat, p=p: e.matmul(
                        ps[bk][0:64, 128 * (lh // 2):128 * (lh // 2) + 128],
                        plat[:, 64 * lh:64 * lh + 64], idbf, start=True, stop=True),
                        [tn, bank_free[bk]])
                tev = None
                for bk in (2, 3):
                    tev = sop(lambda e, bk=bk, p=p: e.copy(
                        out=r3(platT, 512)[0:64, (bk - 2)::2, 128 * p:128 * p + 128],
                        in_=r3(ps[bk][0:64, 0:512], 128)), [tts[2], tts[3]])
                    bank_free[bk] = tev
                st[p]["tev"] = tev

            def st2(p):
                pr = p % 2
                remg = remg_r[pr]
                def mmr(e, p=p):
                    r = None
                    for lh in range(8):
                        for jh in range(2):
                            r = e.matmul(ps[4 + jh][:, 0:512],
                                         platT[0:64, 512 * lh + 128 * p:512 * lh + 128 * p + 128],
                                         gvr_ap(lh)[:, jh * 512:(jh + 1) * 512],
                                         start=(lh == 0), stop=(lh == 7))
                    return r
                trm = top(mmr, [st[p]["tev"], ev_gvr, bank_free[4], bank_free[5]])
                gpair = gates[:, 16 * p:16 * p + 16][:, :, None].broadcast_to([128, 16, 64])
                tg1 = vop(lambda e, remg=remg, gpair=gpair: e.tensor_tensor(
                    out=r3(remg, 64), in0=r3(ps2[2], 64), in1=gpair, op=ALU.mult),
                    [trm, t_gr, pv["remg"][pr]])
                bank_free[4] = tg1
                bank_free[5] = tg1
                st[p]["tg1"] = tg1
                st[p]["trm"] = trm

            def st3(p):
                pr = p % 2
                remg = remg_r[pr]
                comb = comb_r[pr]
                tcb = vop(lambda e, p=p, remg=remg, comb=comb: e.tensor_add(
                    out=comb, in0=remg, in1=lgseg[:, p * D:(p + 1) * D]),
                    [st[p]["tg1"], t_lg, pv["comb"][pr]])
                pv["remg"][pr] = tcb
                tev2 = None
                for half in range(2):
                    bk = 2 + half
                    def mmt(e, half=half, bk=bk, comb=comb):
                        r = None
                        for gg in range(4):
                            g = half * 4 + gg
                            r = e.matmul(ps[bk][:, gg * 128:(gg + 1) * 128],
                                         comb[:, g * 128:(g + 1) * 128],
                                         idbf, start=True, stop=True)
                        return r
                    tmt = top(mmt, [tcb, bank_free[bk]])
                    tev2 = sop(lambda e, p=p, half=half, bk=bk: e.copy(
                        out=r3(comb_f, 256)[:, half * 4:(half + 1) * 4,
                                            (p % 2) * 128:(p % 2) * 128 + 128],
                        in_=r3(ps[bk], 128)), [tmt])
                    bank_free[bk] = tev2
                pv["comb"][pr] = tev2
                st[p]["tev2"] = tev2

            def st4(p):
                def mmw(e, p=p):
                    r = None
                    for jh in range(2):
                        for k in range(8):
                            r = e.matmul(ps[6 + jh][:, 0:512],
                                         comb_f[:, k * 256 + 128 * (p % 2):
                                                k * 256 + 128 * (p % 2) + 128],
                                         w_wo[:, k * 1024 + jh * 512:k * 1024 + (jh + 1) * 512],
                                         start=(k == 0), stop=(k == 7))
                    return r
                tw = top(mmw, [st[p]["tev2"], bank_free[6], bank_free[7]] + t_wwo)
                ta = None
                for jh in range(2):
                    ta = sop(lambda e, p=p, jh=jh, xr=xr: e.copy(
                        out=attn_r[xr][:, p * D + jh * 512:p * D + jh * 512 + 512],
                        in_=ps[6 + jh]), [tw, pv["attn"][xr]])
                    bank_free[6 + jh] = ta
                st[p]["ta"] = ta
                st[p]["tw"] = tw

            stages = (st0, st1, st2, st3, st4)
            for k in range(len(stages) + NPT - 1):
                for p in range(NPT):
                    kk = k - p
                    if 0 <= kk < len(stages):
                        stages[kk](p)
            AB[s] = (st[NPT - 1]["ta"], st[NPT - 1]["lm"],
                     st[NPT - 1]["trm"], st[NPT - 1]["tw"])

        def ph2_B1(s):
            xr = s % 2
            ta_done = AB.pop(s)[0]
            t_xs = dma("sync", xs, x_in[s], [pv["xs"], pv["x2"]] + ph15)
            lnt = layernorm(lambda p, xr=xr: (attn_r[xr][:, p * D:p * D + 512],
                                              attn_r[xr][:, p * D + 512:(p + 1) * D]),
                            lambda p: ln3[:, p * D:(p + 1) * D], NPT, [ta_done],
                            apply_deps=[pv["ln3"]])
            pv["attn"][xr] = lnt[-1]
            tx2 = None
            for p in range(NPT):
                tx2 = vop(lambda e, p=p: e.tensor_add(
                    out=xs[:, p * D:(p + 1) * D], in0=ln3[:, p * D:(p + 1) * D],
                    in1=xs[:, p * D:(p + 1) * D]), [lnt[2 * p + 1], t_xs])
            pv["ln3"] = tx2
            ln2t = layernorm(lambda p: (xs[:, p * D:p * D + 512],
                                        xs[:, p * D + 512:(p + 1) * D]),
                             lambda p: o0t[:, p * D:(p + 1) * D], NPT, [tx2],
                             apply_deps=[pv["o0t"]])
            x2_spill[s] = dma("gpsimd", sc_x2[s], xs, [tx2, S.fence(("vv",))[0]])
            pv["x2"] = x2_spill[s]
            pv["xs"] = ln2t[-1]
            B1T[s] = ln2t

        def ph2_B2(s):
            ln2t = B1T.pop(s)
            tev3 = None
            tmo = None
            for p in range(NPT):
                for half in range(2):
                    obk = 2 + half
                    def mmo(e, p=p, half=half, obk=obk):
                        r = None
                        for gg in range(4):
                            g = half * 4 + gg
                            r = e.matmul(ps[obk][:, gg * 128:(gg + 1) * 128],
                                         o0t[:, p * D + g * 128:p * D + (g + 1) * 128],
                                         idf32, start=True, stop=True)
                        return r
                    tmo = top(mmo, [ln2t[2 * p + 1], bank_free[obk]])
                    tev3 = sop(lambda e, p=p, half=half, obk=obk: e.copy(
                        out=r3(o0f, 512)[:, half * 4:(half + 1) * 4, p * 128:(p + 1) * 128],
                        in_=r3(ps[obk], 128)), [tmo, pv["o0f"]])
                    bank_free[obk] = tev3
            pv["o0t"] = tmo
            pv["o0f"] = tev3
            tgp = vop(lambda e: e.reduce_sum(pooled[:, 0:8], r3(o0f, 512),
                                             axis=mybir.AxisListType.X),
                      [pv["o0f"], gsum_tok[0]])
            if s == 0:
                tgp = vop(lambda e: e.tensor_copy(out=gsum, in_=pooled[:, 0:8]), [tgp])
            else:
                tgp = vop(lambda e: e.tensor_add(out=gsum, in0=gsum, in1=pooled[:, 0:8]),
                          [tgp, gsum_tok[0]])
            gsum_tok[0] = tgp
            o0_spill[s] = dma("gpsimd", sc_oa[s], o0f, [pv["o0f"]])
            pv["o0f"] = o0_spill[s]

        B1T = {}
        w3toks = []
        AB7 = [None]
        for s in range(NSEG):
            if s >= 2:
                ph2_B2(s - 2)
            ph2_A(s)
            pv["gq2"][s % 2] = S.fence(("tt",))[0]
            pv["lg"] = S.fence(("vv",))[0]
            pv["gr"] = pv["lg"]
            pv["lm"] = AB[s][1]
            if s == NSEG - 1:
                w3toks = [AB[s][2], AB[s][3]]
                AB7[0] = AB[s][1]
            if s >= 1:
                ph2_B1(s - 1)

        def emit_b_tail():
            ph2_B2(NSEG - 2)
            ph2_B1(NSEG - 1)
            ph2_B2(NSEG - 1)

        ph2 = S.fence()

        t_cn1 = ldw(w_cn1, cn1_d, 8, 1024, w3toks)
        t_cn2 = ldw(w_cn2, cn2_d, 8, 512, w3toks)
        t_cn3 = ldw(w_cn3, cn3_d, 4, 256, w3toks)
        t_cn4 = ldw(w_cn4, cn4_d, 2, 1, w3toks)
        t_gc1 = ldw(w_gc1, g1_d, 8, 512, w3toks)
        t_gc2 = ldw(w_gc2, g2_d, 4, 256, w3toks)
        t_gc3 = ldw(w_gc3, g3_d, 2, 1, w3toks)

        ph25_tok = {}

        def emit_ar():
            t_gso = dma("sync", cc_gi, gsum, [gsum_tok[0]])
            if "noar" in DEBUG_CUT:
                t_arr = dma("gpsimd", cc_go, cc_gi, [t_gso])
            else:
                t_arr = S.add("gpsimd", lambda e: e.collective_compute(
                    "AllReduce", ALU.add, replica_groups=[[0, 1, 2, 3], [4, 5, 6, 7]],
                    ins=[cc_gi], outs=[cc_go]), [t_gso], semname="cc", inc=1)
            ph25_tok["gsi"] = dma("sync", gsum, cc_go, [t_arr, gsum_tok[0]])

        def emit_ph25():
            tmn = vop(lambda e: e.tensor_scalar(out=gsum, in0=gsum, scalar1=1.0 / 16384,
                                                scalar2=None, op0=ALU.mult),
                      [ph25_tok["gsi"]] + ph2)
            tmn = vop(lambda e: e.tensor_copy(out=mean_bf, in_=gsum), [tmn])
            th = None
            for i in range(4):
                th = top(lambda e, i=i: [e.matmul(
                    ps[0][:, i:i + 1], w_gc1[:, k * 512 + i * 128:k * 512 + (i + 1) * 128],
                    mean_bf[:, k:k + 1], start=(k == 0), stop=(k == 7)) for k in range(8)][-1],
                    [tmn, bank_free[0]] + t_gc1)
            tg1e = None
            for i in range(4):
                tg1e = sop(lambda e, i=i: e.activation(
                    out=hg1[:, i:i + 1], in_=ps[0][:, i:i + 1], func=AF.Gelu,
                    bias=biases[:, 14 + i:15 + i]), [th])
            bank_free[0] = tg1e
            th2 = None
            for i in range(2):
                th2 = top(lambda e, i=i: [e.matmul(
                    ps[1][:, i:i + 1], w_gc2[:, k * 256 + i * 128:k * 256 + (i + 1) * 128],
                    hg1[:, k:k + 1], start=(k == 0), stop=(k == 3)) for k in range(4)][-1],
                    [tg1e, bank_free[1]] + t_gc2)
            tg2e = None
            for i in range(2):
                tg2e = sop(lambda e, i=i: e.activation(
                    out=hg2[:, i:i + 1], in_=ps[1][:, i:i + 1], func=AF.Gelu,
                    bias=biases[:, 18 + i:19 + i]), [th2])
            bank_free[1] = tg2e
            tm3 = top(lambda e: [e.matmul(ps[3][0:1, 0:1], w_gc3[:, k:k + 1], hg2[:, k:k + 1],
                                          start=(k == 0), stop=(k == 1)) for k in range(2)][-1],
                      [tg2e, bank_free[3]] + t_gc3)
            tgf = sop(lambda e: e.activation(out=ar_sb[0:1, 4:5], in_=ps[3][0:1, 0:1], func=AF.Exp,
                                             scale=cpool[0:1, 3:4], bias=biases[0:1, 23:24]), [tm3])
            bank_free[3] = tgf
            tgf = vop(lambda e: e.tensor_scalar(out=ar_sb[0:1, 4:5], in0=ar_sb[0:1, 4:5],
                                                scalar1=1.0, scalar2=None, op0=ALU.add), [tgf])
            tgf = vop(lambda e: e.reciprocal(out=ar_sb[0:1, 4:5], in_=ar_sb[0:1, 4:5]), [tgf])
            tab = vop(lambda e: e.tensor_scalar(out=ar_sb[0:1, 3:4], in0=ar_sb[0:1, 4:5],
                                                scalar1=1.4, scalar2=-0.7, op0=ALU.mult,
                                                op1=ALU.add), [tgf])
            tab = vop(lambda e: e.tensor_mul(out=ar_sb[0:1, 3:4], in0=ar_sb[0:1, 3:4],
                                             in1=ar_sb[0:1, 0:1]), [tab])
            tab = vop(lambda e: e.tensor_scalar(out=ar_sb[0:1, 1:2], in0=ar_sb[0:1, 3:4],
                                                scalar1=1.0, scalar2=0.1, op0=ALU.mult,
                                                op1=ALU.add), [tab])
            tab = vop(lambda e: e.tensor_scalar(out=ar_sb[0:1, 2:3], in0=ar_sb[0:1, 0:1],
                                                scalar1=0.3, scalar2=None, op0=ALU.mult), [tab])
            ph25_tok["tab"] = tab

        if DEBUG_PHASES <= 3:
            emit_b_tail()
            emit_ar()
            emit_ph25()
            finish_debug()
            return nc

        A32.reset(); A16.reset()
        A32.mark(); A16.mark()
        x2l = A32.alloc(NPT * D)
        of_ring = [A32.alloc(8 * 512) for _ in range(2)]
        o3t = A32.alloc(NPT * D)
        tmpq = A32.alloc(1024)
        crow = A32.alloc(512)
        cb_sb = A32.alloc(512)
        ob_ring = [A16.alloc(8 * 512) for _ in range(2)]
        h1 = A16.alloc(8 * 512)
        h2 = A16.alloc(4 * 512)
        h3 = A16.alloc(2 * 512)
        crow_bf = A16.alloc(512)

        of_free = [None, None]
        ob_free = [None, None]
        st_tok = [o0_spill[s] for s in range(NSEG)]
        p3 = {"h1": None, "h2": None, "h3": None, "crow": None, "cb": None,
              "cb2": None, "tmpq": None, "y": None, "x2l": None}
        dep_ld = S.fence(("tt", "ss", "vv"))
        osrc = [sc_oa, sc_ob, sc_oa]
        odst = [sc_ob, sc_oa, None]
        TWSRC = {0: 4, 1: 5, 2: 6, 3: 7, 4: 0, 5: 1, 6: 2, 7: 3}
        TWSGN = {0: 1, 1: -1, 2: -1, 3: 1}

        iters = [(cyc, s) for cyc in range(3) for s in range(NSEG)]
        pend = {}

        def stok(s):
            return st_tok[s] if st_tok[s] is not None else o0_spill[s]

        def ph3_loads(cyc, s):
            xr = s % 2
            if cyc == 0 and s == 0:
                t_of = None
                t_ob = dma("gpsimd", ob_ring[0], osrc[0][0],
                           [stok(0), AB7[0]])
            elif cyc == 0 and s == 1:
                t_of = dma("sync", of_ring[1], osrc[0][1],
                           [stok(1), o0_spill[NSEG - 1]] + dep_ld)
                t_ob = dma("gpsimd", ob_ring[1], osrc[0][1],
                           [stok(1)] + dep_ld)
            else:
                t_of = dma("sync", of_ring[xr], osrc[cyc][s],
                           [of_free[xr], stok(s)] + dep_ld)
                t_ob = dma("gpsimd", ob_ring[xr], osrc[cyc][s],
                           [ob_free[xr], stok(s)] + dep_ld)
            pend[(cyc, s)] = (t_of, t_ob)

        ph3_loads(0, 0)
        for it_idx, (cyc, s) in enumerate(iters):
            if True:
                xr = s % 2
                t_of, t_ob = pend.pop((cyc, s))
                ev1 = None
                for i in range(8):
                    bk = i % 2
                    tm = top(lambda e, i=i, bk=bk, xr=xr: [e.matmul(
                        ps[bk][:, 0:512], w_cn1[:, k * 1024 + i * 128:k * 1024 + (i + 1) * 128],
                        ob_ring[xr][:, k * 512:(k + 1) * 512], start=(k == 0), stop=(k == 7))
                        for k in range(8)][-1], [t_ob, bank_free[bk]] + t_cn1)
                    ev1 = sop(lambda e, i=i, bk=bk: e.activation(
                        out=h1[:, i * 512:(i + 1) * 512], in_=ps[bk], func=AF.Gelu,
                        bias=biases[:, i:i + 1]), [tm, p3["h1"]])
                    bank_free[bk] = ev1
                p3["h1"] = ev1
                if it_idx == 0:
                    emit_b_tail()
                    emit_ar()
                    t_of = dma("sync", of_ring[0], osrc[0][0],
                               [stok(0), pv["o0t"]])
                    ph3_loads(0, 1)
                ev2 = None
                for i in range(4):
                    bk = i % 2
                    tm = top(lambda e, i=i, bk=bk: [e.matmul(
                        ps[bk][:, 0:512], w_cn2[:, k * 512 + i * 128:k * 512 + (i + 1) * 128],
                        h1[:, k * 512:(k + 1) * 512], start=(k == 0), stop=(k == 7))
                        for k in range(8)][-1], [ev1, bank_free[bk]] + t_cn2)
                    ev2 = sop(lambda e, i=i, bk=bk: e.activation(
                        out=h2[:, i * 512:(i + 1) * 512], in_=ps[bk], func=AF.Gelu,
                        bias=biases[:, 8 + i:9 + i]), [tm, p3["h2"]])
                    bank_free[bk] = ev2
                p3["h2"] = ev2
                ev3 = None
                for i in range(2):
                    bk = i % 2
                    tm = top(lambda e, i=i, bk=bk: [e.matmul(
                        ps[bk][:, 0:512], w_cn3[:, k * 256 + i * 128:k * 256 + (i + 1) * 128],
                        h2[:, k * 512:(k + 1) * 512], start=(k == 0), stop=(k == 3))
                        for k in range(4)][-1], [ev2, bank_free[bk]] + t_cn3)
                    ev3 = sop(lambda e, i=i, bk=bk: e.activation(
                        out=h3[:, i * 512:(i + 1) * 512], in_=ps[bk], func=AF.Gelu,
                        bias=biases[:, 12 + i:13 + i]), [tm, p3["h3"]])
                    bank_free[bk] = ev3
                p3["h3"] = ev3
                tm4 = top(lambda e: [e.matmul(ps[2][0:1, 0:512], w_cn4[:, k:k + 1],
                                              h3[:, k * 512:(k + 1) * 512],
                                              start=(k == 0), stop=(k == 1)) for k in range(2)][-1],
                          [ev3, bank_free[2]] + t_cn4)
                if it_idx == 0:
                    emit_ph25()
                tcw = sop(lambda e: e.activation(out=crow[0:1, :], in_=ps[2][0:1, 0:512],
                                                 func=AF.Tanh, scale=cpool[0:1, 4:5],
                                                 bias=biases[0:1, 22:23]), [tm4, p3["crow"]])
                bank_free[2] = tcw
                tcw = vop(lambda e: e.tensor_scalar(out=crow[0:1, :], in0=crow[0:1, :],
                                                    scalar1=ar_sb[0:1, 2:3],
                                                    scalar2=ar_sb[0:1, 1:2],
                                                    op0=ALU.mult, op1=ALU.add),
                          [tcw, ph25_tok["tab"]])
                tcw = vop(lambda e: e.tensor_copy(out=crow_bf[0:1, :], in_=crow[0:1, :]), [tcw])
                p3["crow"] = tcw
                tcb = top(lambda e: e.matmul(ps[3][:, 0:512], oner, crow_bf[0:1, :],
                                             start=True, stop=True),
                          [tcw, bank_free[3], p3["cb"]])
                tcs = sop(lambda e: e.copy(out=cb_sb, in_=ps[3][:, 0:512]),
                          [tcb, p3["cb2"]])
                bank_free[3] = tcs
                tup = None
                tmfl = None
                tevl = None
                for f4 in range(4):
                    fa, fb = f4, f4 + 4
                    ta_ = vop(lambda e, fb=fb, xr=xr: e.tensor_tensor(
                        out=tmpq[:, 0:512], in0=cb_sb,
                        in1=of_ring[xr][:, fb * 512:(fb + 1) * 512], op=ALU.mult),
                        [tcs, t_of, p3["tmpq"]])
                    tb_ = vop(lambda e, fa=fa, xr=xr: e.tensor_tensor(
                        out=tmpq[:, 512:1024], in0=cb_sb,
                        in1=of_ring[xr][:, fa * 512:(fa + 1) * 512], op=ALU.mult),
                        [tcs, t_of])
                    opa = ALU.add if TWSGN[fa // 2] > 0 else ALU.subtract
                    opb = ALU.add if TWSGN[fb // 2] > 0 else ALU.subtract
                    tu1 = vop(lambda e, fa=fa, xr=xr, opa=opa: e.tensor_tensor(
                        out=of_ring[xr][:, fa * 512:(fa + 1) * 512],
                        in0=of_ring[xr][:, fa * 512:(fa + 1) * 512],
                        in1=tmpq[:, 0:512], op=opa), [ta_, tb_])
                    tup = vop(lambda e, fb=fb, xr=xr, opb=opb: e.tensor_tensor(
                        out=of_ring[xr][:, fb * 512:(fb + 1) * 512],
                        in0=of_ring[xr][:, fb * 512:(fb + 1) * 512],
                        in1=tmpq[:, 512:1024], op=opb), [tu1])
                    p3["tmpq"] = tup
                    if cyc == 2:
                        for g, bk, tk in ((fa, 4, tu1), (fb, 5, tup)):
                            def mmf(e, g=g, bk=bk, xr=xr):
                                r = None
                                for p in range(NPT):
                                    r = e.matmul(ps[bk][:, p * 128:(p + 1) * 128],
                                                 of_ring[xr][:, g * 512 + 128 * p:g * 512 + 128 * p + 128],
                                                 idf32, start=True, stop=True)
                                return r
                            tmfl = top(mmf, [tk, bank_free[bk]])
                            tevl = sop(lambda e, g=g, bk=bk: e.copy(
                                out=r3(o3t, D)[:, :, g * 128:(g + 1) * 128],
                                in_=r3(ps[bk][:, 0:512], 128)), [tmfl, p3["y"]])
                            bank_free[bk] = tevl
                p3["cb"] = tup
                p3["cb2"] = tup
                ob_free[xr] = ev1
                if it_idx + 1 < len(iters) and iters[it_idx + 1] not in pend:
                    ph3_loads(*iters[it_idx + 1])
                if cyc < 2:
                    st_tok[s] = dma("gpsimd", odst[cyc][s], of_ring[xr], [tup])
                    of_free[xr] = st_tok[s]
                else:
                    t_x2 = dma("gpsimd", x2l, sc_x2[s],
                               [x2_spill[s], p3["x2l"],
                                x2_spill[NSEG - 1], pv["xs"]])
                    tev4 = tevl
                    of_free[xr] = tmfl
                    ln4 = layernorm(lambda p: (o3t[:, p * D:p * D + 512],
                                               o3t[:, p * D + 512:(p + 1) * D]),
                                    lambda p: o3t[:, p * D:(p + 1) * D], NPT,
                                    [tev4])
                    tyv = None
                    for p in range(NPT):
                        tyv = vop(lambda e, p=p: e.tensor_add(
                            out=o3t[:, p * D:(p + 1) * D], in0=o3t[:, p * D:(p + 1) * D],
                            in1=x2l[:, p * D:(p + 1) * D]), [ln4[-1], t_x2])
                    ty = dma("sync", y_out[s], o3t, [tyv])
                    p3["y"] = ty
                    p3["x2l"] = tyv

        S.add("sync", None, S.dma_fence(), semname="fin", inc=1)

        with nc.Block() as block:
            S.emit(block)

    return nc


def _host_prep(inputs, core):
    b, q = core // 4, core % 4
    x = np.asarray(inputs["L0_attn_x"])
    xc = np.asarray(x[b, q * T:(q + 1) * T], np.float32)
    xseg = np.ascontiguousarray(
        xc.reshape(NSEG, NPT, 128, D).transpose(0, 2, 1, 3).reshape(NSEG, 128, NPT * D))
    if q == 0:
        xh = np.zeros((256, D), np.float32)
    else:
        xh = np.asarray(x[b, q * T - 256:q * T], np.float32)
    xhalo = np.ascontiguousarray(xh.reshape(2, 128, D).transpose(1, 0, 2).reshape(128, 2 * D))

    jj = np.arange(128)[:, None]
    ii = np.arange(256)[None, :]
    m0 = (ii < jj).astype(np.float32)
    m1 = (ii < jj + 128).astype(np.float32)
    m2 = (ii >= jj).astype(np.float32)
    m3 = (ii >= jj + 128).astype(np.float32)
    wmask = np.concatenate([m0, m1, m2, m3], axis=1).astype(BF)
    if q == 0:
        wmask0 = np.concatenate([0 * m0, 0 * m1, m2, m3], axis=1).astype(BF)
    else:
        wmask0 = wmask

    lat = np.zeros((NSEG, 128, NPT * 512), np.float32)
    for s in range(NSEG):
        for p in range(NPT):
            C = q * 16 + (s * NPT + p) // 2
            n_ok = min(C + 1, 64)
            for lh in range(8):
                lat[s, :, p * 512 + lh * 64: p * 512 + lh * 64 + n_ok] = 1.0
    latmask = lat.astype(BF)

    glw = np.asarray(inputs["L0_attn_gate_local_w"], np.float32)
    grw = np.asarray(inputs["L0_attn_gate_remote_w"], np.float32)
    wgate = np.zeros((D, 32), np.float32)
    for h in range(16):
        wgate[64 * h:64 * h + 64, h] = glw[h]
        wgate[64 * h:64 * h + 64, 16 + h] = grw[h]

    f32 = np.float32
    g = lambda k: np.asarray(inputs[k], f32)
    return {
        "xseg": xseg, "xhalo": xhalo,
        "wq": g("L0_attn_wq"), "wk": g("L0_attn_wk"), "wv": g("L0_attn_wv"),
        "wgq": g("L0_attn_wgq"), "wgk": g("L0_attn_wgk"), "wgv": g("L0_attn_wgv"),
        "r_out": g("L0_attn_r_out_w"), "w_out": g("L0_attn_w_out"), "wgate": wgate,
        "cn_w1": g("L0_ffn_cn_w1"), "cn_w2": g("L0_ffn_cn_w2"),
        "cn_w3": g("L0_ffn_cn_w3"), "cn_w4": g("L0_ffn_cn_w4"),
        "cn_b1": np.ascontiguousarray(g("L0_ffn_cn_b1").reshape(8, 128).T),
        "cn_b2": np.ascontiguousarray(g("L0_ffn_cn_b2").reshape(4, 128).T),
        "cn_b3": np.ascontiguousarray(g("L0_ffn_cn_b3").reshape(2, 128).T),
        "cn_b4": g("L0_ffn_cn_b4").reshape(1, 1),
        "gc_w1": g("L0_ffn_gc_w1"), "gc_w2": g("L0_ffn_gc_w2"), "gc_w3": g("L0_ffn_gc_w3"),
        "gc_b1": np.ascontiguousarray(g("L0_ffn_gc_b1").reshape(4, 128).T),
        "gc_b2": np.ascontiguousarray(g("L0_ffn_gc_b2").reshape(2, 128).T),
        "gc_b3": g("L0_ffn_gc_b3").reshape(1, 1),
        "ar": g("L0_ffn_adaptive_range").reshape(1, 1),
        "winmask": wmask, "winmask0": wmask0, "latmask": latmask,
        "ident_bf": np.eye(128, dtype=np.float32).astype(BF),
        "ident_f32": np.eye(128, dtype=np.float32),
        "ones_col": np.ones((128, 1), BF),
        "ones_row": np.ones((1, 128), BF),
    }


def kernel(**inputs):
    if "nc" not in _CACHE:
        _CACHE["nc"] = build_nc()
    nc = _CACHE["nc"]
    in_maps = [_host_prep(inputs, c) for c in range(8)]
    res = run_bass_kernel_spmd(nc, in_maps, core_ids=list(range(8)))
    out = np.zeros((2, 4 * T, D), np.float32)
    for c in range(8):
        b, q = c // 4, c % 4
        y = np.asarray(res.results[c]["y"])
        out[b, q * T:(q + 1) * T] = y.reshape(NSEG, 128, NPT, D).transpose(
            0, 2, 1, 3).reshape(T, D)
    return out


# version 16
# speedup vs baseline: 1.0861x; 1.0084x over previous
import numpy as np
import ml_dtypes
import concourse.bass as bass
import concourse.mybir as mybir
from concourse.bass_utils import run_bass_kernel_spmd

F32 = mybir.dt.float32
BF16 = mybir.dt.bfloat16
AF = mybir.ActivationFunctionType
ALU = mybir.AluOpType
BF = ml_dtypes.bfloat16

NSEG = 8
NPT = 4
T = 4096
D = 1024
NH = 16

ENGINES = ("sync", "gpsimd", "vector", "scalar", "tensor")
_CACHE = {}
DEBUG_PHASES = 99
DEBUG_CUT = set()


class Sched:
    def __init__(self, nc, stack):
        self.nc = nc
        self.items = {e: [] for e in ENGINES}
        self.sems = {}
        self.cnt = {}
        self._stack = stack

    def sem(self, name):
        if name not in self.sems:
            self.sems[name] = self._stack.enter_context(self.nc.semaphore(name))
            self.cnt[name] = 0
        return self.sems[name]

    def add(self, eng, fn, deps=(), semname=None, inc=None):
        deps = [d for d in deps if d is not None]
        if semname is None:
            if eng in ("sync", "gpsimd"):
                base = "dx" if eng == "sync" else "dg"
                k = self.cnt.get(base + "_n", 0)
                self.cnt[base + "_n"] = k + 1
                semname = f"{base}{k % 4}"
                self.sem(semname)
                if self.cnt[semname] > 0:
                    deps.append((semname, self.cnt[semname]))
                inc = 16
            else:
                semname = {"vector": "vv", "scalar": "ss", "tensor": "tt"}[eng]
                inc = 1
        elif inc is None:
            inc = 1
        self.sem(semname)
        for (sn, val) in deps:
            assert val <= self.cnt[sn], f"future dep {sn}:{val}>{self.cnt[sn]}"
        self.cnt[semname] += inc
        self.items[eng].append((fn, tuple(deps), semname, inc))
        return (semname, self.cnt[semname])

    DMASEMS = ("dx0", "dx1", "dx2", "dx3", "dg0", "dg1", "dg2", "dg3")

    def fence(self, names=("vv", "ss", "tt") + DMASEMS):
        return [(n, self.cnt[n]) for n in names if n in self.cnt and self.cnt[n] > 0]

    def dma_fence(self):
        return [(n, self.cnt[n]) for n in self.DMASEMS if n in self.cnt and self.cnt[n] > 0]

    def barrier(self):
        name = f"ph{self.cnt.get('_nbar', 0)}"
        self.cnt["_nbar"] = self.cnt.get("_nbar", 0) + 1
        self.sem(name)
        for e in ENGINES:
            self.cnt[name] += 1
            self.items[e].append((None, (), name, 1))
        return [(name, 5)]

    def emit(self, block):
        def mk(engname):
            def body(eng):
                lastw = {}
                for (fn, deps, semname, inc) in self.items[engname]:
                    for (sn, val) in deps:
                        if lastw.get(sn, -1) < val:
                            eng.wait_ge(self.sems[sn], val)
                            lastw[sn] = val
                    inst = eng.nop() if fn is None else fn(eng)
                    inst.then_inc(self.sems[semname], inc)
            return body
        block.sync(mk("sync"))
        block.gpsimd(mk("gpsimd"))
        block.vector(mk("vector"))
        block.scalar(mk("scalar"))
        block.tensor(mk("tensor"))


class Arena:
    def __init__(self, ap, cols):
        self.ap = ap
        self.cols = cols
        self.off = 0
        self.marks = []

    def alloc(self, n):
        o = self.off
        self.off += n
        assert self.off <= self.cols, f"arena overflow {self.off}>{self.cols}"
        return self.ap[:, o:o + n]

    def mark(self):
        self.marks.append(self.off)

    def reset(self):
        self.off = self.marks.pop()


def r3(ap, c):
    return ap.rearrange("q (a b) -> q a b", b=c)


def build_nc():
    nc = bass.Bass("TRN2", target_bir_lowering=False, debug=False)

    def din(name, shape, dt=F32):
        return nc.dram_tensor(name, list(shape), dt, kind="ExternalInput").ap()

    x_in = din("xseg", (NSEG, 128, NPT * D))
    xhalo = din("xhalo", (128, 2 * D))
    wq_d = din("wq", (D, D)); wk_d = din("wk", (D, D)); wv_d = din("wv", (D, D))
    wgq_d = din("wgq", (D, 512)); wgk_d = din("wgk", (D, 512)); wgv_d = din("wgv", (D, 512))
    ro_d = din("r_out", (512, D)); wo_d = din("w_out", (D, D))
    wgate_d = din("wgate", (D, 32))
    cn1_d = din("cn_w1", (D, D)); cn2_d = din("cn_w2", (D, 512))
    cn3_d = din("cn_w3", (512, 256)); cn4_d = din("cn_w4", (256, 1))
    b1_d = din("cn_b1", (128, 8)); b2_d = din("cn_b2", (128, 4))
    b3_d = din("cn_b3", (128, 2)); b4_d = din("cn_b4", (1, 1))
    g1_d = din("gc_w1", (D, 512)); g2_d = din("gc_w2", (512, 256)); g3_d = din("gc_w3", (256, 1))
    gb1_d = din("gc_b1", (128, 4)); gb2_d = din("gc_b2", (128, 2)); gb3_d = din("gc_b3", (1, 1))
    ar_d = din("ar", (1, 1))
    wm_d = din("winmask", (128, 1024), BF16)
    wm0_d = din("winmask0", (128, 1024), BF16)
    lm_d = din("latmask", (NSEG, 128, NPT * 512), BF16)
    idb_d = din("ident_bf", (128, 128), BF16)
    idf_d = din("ident_f32", (128, 128), F32)
    onec_d = din("ones_col", (128, 1), BF16)
    oner_d = din("ones_row", (1, 128), BF16)

    y_out = nc.dram_tensor("y", [NSEG, 128, NPT * D], F32, kind="ExternalOutput").ap()

    sc_localg = nc.dram_tensor("sc_localg", [NSEG, 128, NPT * D], BF16).ap()
    sc_gq = nc.dram_tensor("sc_gq", [NSEG, 128, 4 * 512], BF16).ap()
    sc_gr = nc.dram_tensor("sc_gr", [NSEG, 128, NPT * 16], F32).ap()
    sc_x2 = nc.dram_tensor("sc_x2", [NSEG, 128, NPT * D], F32).ap()
    sc_oa = nc.dram_tensor("sc_oa", [NSEG, 128, 8 * 512], F32).ap()
    sc_ob = nc.dram_tensor("sc_ob", [NSEG, 128, 8 * 512], F32).ap()
    cc_pi = nc.dram_tensor("cc_pi", [16, D], F32).ap()
    cc_po = nc.dram_tensor("cc_po", [64, D], F32).ap()
    cc_gi = nc.dram_tensor("cc_gi", [128, 8], F32).ap()
    cc_go = nc.dram_tensor("cc_go", [128, 8], F32).ap()

    from contextlib import ExitStack
    st = ExitStack()
    with st:
        S = Sched(nc, st)

        def sb(name, shape, dt=F32):
            return st.enter_context(nc.sbuf_tensor(name, list(shape), dt))[:]

        WA = Arena(sb("w_arena", (128, 28930), BF16), 28930)
        A32 = Arena(sb("a32", (128, 19050), F32), 19050)
        A16 = Arena(sb("a16", (128, 28830), BF16), 28830)
        cpool = sb("cpool", (128, 8))
        idbf = sb("idbf", (128, 128), BF16)
        idf32 = sb("idf32", (128, 128), F32)
        onec = sb("onec", (128, 1), BF16)
        oner = sb("oner", (1, 128), BF16)
        wmask = sb("wmask", (128, 1024), BF16)
        wmask0 = sb("wmask0", (128, 1024), BF16)
        stats = sb("stats", (128, 48))
        st2 = sb("st2", (128, 16))
        pooled = sb("pooled", (128, 160))
        gates = sb("gates", (128, NPT * 32))
        glr = sb("glr", (128, 16))
        den = sb("den", (128, 16))
        biases = sb("biases", (128, 24))
        ar_sb = sb("ar_sb", (1, 8))
        gsum = sb("gsum", (128, 8))
        mean_bf = sb("mean_bf", (128, 8), BF16)
        hg1 = sb("hg1", (128, 4), BF16)
        hg2 = sb("hg2", (128, 2), BF16)
        gk_fm = sb("gk_fm", (128, 4 * 64), BF16)
        gv_fm = sb("gv_fm", (128, 4 * 64), BF16)
        gvr = sb("gvr", (128, 8 * 512), BF16)
        pooled_fm = sb("pooled_fm", (128, 8 * 64), BF16)

        def gvr_ap(lh):
            if lh % 2 == 0:
                return gvr[0:64, 1024 * (lh // 2):1024 * (lh // 2) + 1024]
            return w_ro[0:64, 1024 * (lh // 2):1024 * (lh // 2) + 1024]

        ps2 = [st.enter_context(nc.psum_tensor(f"pp{i}", [128, 1024], F32))[:] for i in range(4)]
        ps = [ps2[i // 2][:, 512 * (i % 2):512 * (i % 2) + 512] for i in range(8)]

        w_qkv = WA.alloc(24576)
        w_gq = WA.alloc(4096)
        w_gate = WA.alloc(256)
        WA.off = 0
        w_gkv = WA.alloc(8192)
        w_ro = WA.alloc(4096)
        w_wo = WA.alloc(8192)
        WA.off = 0
        w_cn1 = WA.alloc(8192)
        w_cn2 = WA.alloc(4096)
        w_cn3 = WA.alloc(1024)
        w_cn4 = WA.alloc(2)
        w_gc1 = WA.alloc(4096)
        w_gc2 = WA.alloc(1024)
        w_gc3 = WA.alloc(2)

        dma = lambda eng, out, in_, deps=(): S.add(
            eng, lambda e, o=out, i=in_: e.dma_start(out=o, in_=i), deps)
        dma_nc = lambda eng, out, in_, deps=(): S.add(
            eng, lambda e, o=out, i=in_: e.dma_start(out=o, in_=i,
                                                     allow_slow_non_contiguous=True), deps)
        vop = lambda fn, deps=(): S.add("vector", fn, deps)
        sop = lambda fn, deps=(): S.add("scalar", fn, deps)
        top = lambda fn, deps=(): S.add("tensor", fn, deps)

        dma("sync", idbf, idb_d)
        dma("sync", idf32, idf_d)
        dma("sync", onec, onec_d)
        dma("sync", oner, oner_d)
        dma("sync", wmask, wm_d)
        dma("sync", wmask0, wm0_d)
        dma("sync", ar_sb[0:1, 0:1], ar_d)
        for c, v in enumerate((1e-5, -0.5, 0.125, -1.0, 0.5, 1.0)):
            vop(lambda e, c=c, v=v: e.memset(cpool[:, c:c + 1], v))

        def ldw(dst, src, kt, cols, deps=()):
            tks = {}
            for k in range(kt):
                tk = dma("gpsimd", dst[:, k * cols:(k + 1) * cols],
                         src[k * 128:(k + 1) * 128, :], deps)
                tks[tk[0]] = tk
            return list(tks.values())

        ldw(w_qkv[:, 0:8192], wq_d, 8, 1024)
        ldw(w_qkv[:, 8192:16384], wk_d, 8, 1024)
        ldw(w_qkv[:, 16384:24576], wv_d, 8, 1024)
        ldw(w_gq, wgq_d, 8, 512)
        ldw(w_gate, wgate_d, 8, 32)
        dma("sync", biases[:, 0:8], b1_d)
        dma("sync", biases[:, 8:12], b2_d)
        dma("sync", biases[:, 12:14], b3_d)
        dma("sync", biases[:, 14:18], gb1_d)
        dma("sync", biases[:, 18:20], gb2_d)
        tb = dma("sync", biases[0:1, 20:21], b4_d)
        tb2 = dma("sync", biases[0:1, 21:22], gb3_d)
        vop(lambda e: e.tensor_scalar(out=biases[0:1, 22:23], in0=biases[0:1, 20:21],
                                      scalar1=0.5, scalar2=None, op0=ALU.mult), [tb])
        vop(lambda e: e.tensor_scalar(out=biases[0:1, 23:24], in0=biases[0:1, 21:22],
                                      scalar1=-1.0, scalar2=None, op0=ALU.mult), [tb2])

        A32.mark(); A16.mark()
        x_ring = [A32.alloc(NPT * D) for _ in range(2)]
        graw = A32.alloc(512)
        pooled_t = A32.alloc(128)
        ln1t = A16.alloc(NPT * D)
        q_fm = A16.alloc(8 * 512)
        ln1f = A16.alloc(8 * 512)
        gq_sb = A16.alloc(4 * 512)
        k_ring = [A16.alloc(8 * 256) for _ in range(3)]
        v_ring = [A16.alloc(16 * 65) for _ in range(6)]
        p_t = [A32.alloc(512).bitcast(BF16) for _ in range(4)]
        localg = A32.alloc(NPT * D // 2).bitcast(BF16)

        vr1 = None
        for sl in range(6):
            vr1 = vop(lambda e, sl=sl: e.memset(r3(v_ring[sl], 65)[:, :, 64:65], 1.0))
        ph0 = S.barrier() + S.dma_fence()
        dep_ph1 = ph0 + S.fence(("vv",)) + S.dma_fence()

        bank_free = {i: None for i in range(8)}
        x_free = [None, None]
        pt_free = [None, None, None, None]
        kv_tok = {}
        kv_ktok = {}
        kslot_free = {}
        vslot_free = {}
        ln_gate = [None]
        gq_free = None
        lg_free = None

        def layernorm(xap_fn, dst_fn, npt, deps, apply_deps=(), apply_eng="vector"):
            t1 = None
            dd = list(deps) + [ln_gate[0]]
            for p in range(npt):
                lo, hi = xap_fn(p)
                t1 = vop(lambda e, p=p, lo=lo: e.bn_stats(stats[:, 12 * p:12 * p + 6], lo), dd)
                t1 = vop(lambda e, p=p, hi=hi: e.bn_stats(stats[:, 12 * p + 6:12 * p + 12], hi), dd)
            ta = None
            for p in range(npt):
                ta = vop(lambda e, p=p: e.bn_aggr(st2[:, 2 * p:2 * p + 2],
                                                  stats[:, 12 * p:12 * p + 12]), [t1])
            va = st2[:, 1:2 * npt:2]
            tl = sop(lambda e, va=va: e.activation(out=va, in_=va, func=AF.Ln,
                                                   bias=cpool[:, 0:1]), [ta])
            tl = sop(lambda e, va=va: e.activation(out=va, in_=va, func=AF.Exp,
                                                   scale=cpool[:, 1:2]), [tl])
            tnb = vop(lambda e, va=va, npt=npt: e.scalar_tensor_tensor(
                out=st2[:, 8:8 + npt], in0=st2[:, 0:2 * npt:2], scalar=-1.0, in1=va,
                op0=ALU.mult, op1=ALU.mult), [tl])
            outs = []
            ad = [tnb] + list(apply_deps)
            for p in range(npt):
                lo, hi = xap_fn(p)
                dst = dst_fn(p)
                if apply_eng == "scalar":
                    outs.append(sop(lambda e, p=p, lo=lo, dst=dst: e.activation(
                        out=dst[:, 0:512], in_=lo, func=AF.Identity,
                        scale=st2[:, 2 * p + 1:2 * p + 2],
                        bias=st2[:, 8 + p:9 + p]), ad))
                    outs.append(sop(lambda e, p=p, hi=hi, dst=dst: e.activation(
                        out=dst[:, 512:1024], in_=hi, func=AF.Identity,
                        scale=st2[:, 2 * p + 1:2 * p + 2],
                        bias=st2[:, 8 + p:9 + p]), ad))
                    continue
                outs.append(vop(lambda e, p=p, lo=lo, dst=dst: e.tensor_scalar(
                    out=dst[:, 0:512], in0=lo, scalar1=st2[:, 2 * p + 1:2 * p + 2],
                    scalar2=st2[:, 8 + p:9 + p], op0=ALU.mult, op1=ALU.add), ad))
                outs.append(vop(lambda e, p=p, hi=hi, dst=dst: e.tensor_scalar(
                    out=dst[:, 512:1024], in0=hi, scalar1=st2[:, 2 * p + 1:2 * p + 2],
                    scalar2=st2[:, 8 + p:9 + p], op0=ALU.mult, op1=ALU.add), ad))
            ln_gate[0] = outs[-1]
            return outs

        def transpose_tok2fm(src_tok, npt, dst_fm, deps, fp32=False, pool=False,
                             gsum_first=None):
            ident = idf32 if fp32 else idbf
            ev = None
            for p in range(npt):
                for half in range(2):
                    bk = half
                    def mm(e, p=p, half=half, bk=bk, first=(gsum_first[0] if gsum_first else False)):
                        r = None
                        for gg in range(4):
                            g = half * 4 + gg
                            r = e.matmul(ps[bk][:, gg * 128:(gg + 1) * 128],
                                         src_tok[:, p * 1024 + g * 128:p * 1024 + (g + 1) * 128],
                                         ident, start=True, stop=True)
                            if pool:
                                r = e.matmul(ps[2][:, (g * 4 + p):(g * 4 + p + 1)],
                                             src_tok[:, p * 1024 + g * 128:p * 1024 + (g + 1) * 128],
                                             onec, start=True, stop=True)
                            if gsum_first is not None:
                                r = e.matmul(ps[7][:, g:g + 1],
                                             src_tok[:, p * 1024 + g * 128:p * 1024 + (g + 1) * 128],
                                             cpool[:, 5:6], start=first, stop=False,
                                             skip_group_check=True)
                        return r
                    tm = top(mm, list(deps) + [bank_free[bk]] +
                             ([bank_free[2]] if pool else []))
                    if gsum_first is not None:
                        gsum_first[0] = False
                    ev = sop(lambda e, p=p, half=half, bk=bk: e.copy(
                        out=r3(dst_fm, 512)[:, half * 4:(half + 1) * 4, p * 128:(p + 1) * 128],
                        in_=r3(ps[bk], 128)), [tm])
                    bank_free[bk] = ev
            return ev

        def kv_block(blk, tcol, dep_lnf):
            ks = (blk + 1) % 3
            ev = None
            k_ev = None
            for i in range(8):
                bk = 3 + (i % 2)
                def mmk(e, i=i, bk=bk, tcol=tcol):
                    r = None
                    for k in range(8):
                        r = e.matmul(ps[bk][:, 0:256],
                                     w_qkv[:, 8192 + k * 1024 + i * 128:8192 + k * 1024 + (i + 1) * 128],
                                     ln1f[:, k * 512 + tcol:k * 512 + tcol + 256],
                                     start=(k == 0), stop=(k == 7))
                    return r
                tm = top(mmk, [dep_lnf, bank_free[bk], kslot_free.get(ks)])
                ev = sop(lambda e, i=i, bk=bk, ks=ks: e.copy(
                    out=k_ring[ks][:, i * 256:(i + 1) * 256], in_=ps[bk][:, 0:256]), [tm])
                bank_free[bk] = ev
                k_ev = ev
            for pp in range(2):
                vs = ks * 2 + pp
                for jh in range(2):
                    bk = 5 + jh
                    def mmv(e, pp=pp, jh=jh, tcol=tcol, bk=bk):
                        r = None
                        for k in range(8):
                            r = e.matmul(ps[bk][:, 0:512],
                                         ln1f[:, k * 512 + tcol + pp * 128:k * 512 + tcol + pp * 128 + 128],
                                         w_qkv[:, 16384 + k * 1024 + jh * 512:16384 + k * 1024 + (jh + 1) * 512],
                                         start=(k == 0), stop=(k == 7))
                        return r
                    tm = top(mmv, [dep_lnf, bank_free[bk], vslot_free.get(vs)])
                    ev = sop(lambda e, vs=vs, jh=jh, bk=bk: e.copy(
                        out=r3(v_ring[vs], 65)[:, jh * 8:(jh + 1) * 8, 0:64],
                        in_=r3(ps[bk], 64)), [tm])
                    bank_free[bk] = ev
            kv_tok[blk] = ev
            kv_ktok[blk] = k_ev
            return ev

        def proj(dst, wcols, n_i, stride, jt, deps, banks=(3, 4)):
            evs = []
            ev = None
            for i in range(n_i):
                bk = banks[i % 2]
                def mm(e, i=i, bk=bk):
                    r = None
                    for k in range(8):
                        r = e.matmul(ps[bk][:, 0:jt],
                                     wcols[:, k * stride + i * 128:k * stride + (i + 1) * 128],
                                     ln1f[:, k * 512:k * 512 + jt],
                                     start=(k == 0), stop=(k == 7))
                    return r
                tm = top(mm, list(deps) + [bank_free[bk]])
                ev = sop(lambda e, i=i, bk=bk, jt=jt: e.copy(
                    out=dst[:, i * jt:(i + 1) * jt], in_=ps[bk][:, 0:jt]), [tm])
                bank_free[bk] = ev
                evs.append(ev)
            return evs

        for _b in range(4):
            bank_free[_b] = vop(lambda e, _b=_b: e.memset(ps[_b][:, 0:512], 0.0), ph0)
        t_xh = dma("sync", x_ring[1][:, 0:2 * D], xhalo, dep_ph1)
        ln_h = layernorm(lambda p: (x_ring[1][:, p * D:p * D + 512],
                                    x_ring[1][:, p * D + 512:(p + 1) * D]),
                         lambda p: ln1t[:, p * D:(p + 1) * D], 2, [t_xh] + dep_ph1)
        ev_trh = transpose_tok2fm(ln1t, 2, ln1f, [ln_h[-1]] + dep_ph1)
        t_kvh = kv_block(-1, 0, ev_trh)
        x_free[1] = ln_h[-1]

        lg_spill = [None] * NSEG
        gq_spill = [None] * NSEG
        gr_spill = [None] * NSEG
        glr_war = {}
        ln1f_last_reader = t_kvh
        tp = None
        ln_pipe = {}

        def emit_ln1(s, ln1t_free):
            xr = s % 2
            t_x = dma("sync", x_ring[xr], x_in[s], [x_free[xr]] + dep_ph1)
            ln_a = layernorm(lambda p, xr=xr: (x_ring[xr][:, p * D:p * D + 512],
                                               x_ring[xr][:, p * D + 512:(p + 1) * D]),
                             lambda p: ln1t[:, p * D:(p + 1) * D], NPT, [t_x],
                             apply_deps=[ln1t_free])
            x_free[xr] = ln_a[-1]
            ln_pipe[s] = ln_a

        emit_ln1(0, ev_trh)
        for s in range(NSEG):
            xr = s % 2
            ln_a = ln_pipe.pop(s)
            ev_tr = transpose_tok2fm(ln1t, NPT, ln1f,
                                     [ln_a[-1], ln1f_last_reader])
            for cl in range(2):
                tp = vop(lambda e, cl=cl, s=s: e.reduce_sum(
                    pooled[:, 8 * (2 * s + cl):8 * (2 * s + cl) + 8],
                    r3(ln1f, 512)[:, :, 256 * cl:256 * cl + 256],
                    axis=mybir.AxisListType.X), [ev_tr])
            if s == NSEG - 1:
                tps = vop(lambda e: e.tensor_scalar(out=pooled[:, 0:128], in0=pooled[:, 0:128],
                                                    scalar1=1.0 / 256, scalar2=None, op0=ALU.mult),
                          [tp])
                tpt0 = top(lambda e: e.matmul(ps[3][:, 0:128], pooled[:, 0:128], idf32,
                                              start=True, stop=True),
                           [tps, bank_free[3]])
                tpt0 = vop(lambda e: e.tensor_copy(out=pooled_t, in_=ps[3][:, 0:128]), [tpt0])
                bank_free[3] = tpt0
                t_pool_out = dma("gpsimd", cc_pi, pooled_t, [tpt0])
                t_ag = S.add("gpsimd", lambda e: e.collective_compute(
                    "AllGather", ALU.bypass, replica_groups=[[0, 1, 2, 3], [4, 5, 6, 7]],
                    ins=[cc_pi], outs=[cc_po]), [t_pool_out], semname="cc", inc=1)
            evq_l = proj(q_fm[:, 0:8 * 512], w_qkv[:, 0:8192], 8, 1024, 512, [ev_tr])
            ev_q = evq_l[-1]
            ev_gq = proj(gq_sb, w_gq, 4, 512, 512, [ev_tr, gq_free])[-1]
            gq_spill[s] = dma("gpsimd", sc_gq[s], gq_sb, [ev_gq])
            gq_free = gq_spill[s]
            def mmg(e):
                r = None
                for k in range(8):
                    r = e.matmul(ps[7][0:32, 0:512], w_gate[:, k * 32:(k + 1) * 32],
                                 q_fm[:, k * 512:(k + 1) * 512], start=(k == 0), stop=(k == 7))
                return r
            tg = top(mmg, [ev_q, bank_free[7]])
            tg = vop(lambda e: e.tensor_copy(out=graw[0:32, :], in_=ps[7][0:32, 0:512]), [tg])
            bank_free[7] = tg
            def mmgt(e):
                r = None
                for p in range(NPT):
                    r = e.matmul(ps[7][:, p * 32:(p + 1) * 32], graw[0:32, p * 128:(p + 1) * 128],
                                 idf32[0:32, 0:32], start=True, stop=True)
                return r
            tgt = top(mmgt, [tg, bank_free[7]])
            tge = sop(lambda e: e.activation(out=gates, in_=ps[7][:, 0:128], func=AF.Exp,
                                             scale=cpool[:, 3:4]),
                      [tgt, gr_spill[s - 1] if s > 0 else None])
            tge = vop(lambda e: e.tensor_scalar(out=gates, in0=gates, scalar1=1.0,
                                                scalar2=None, op0=ALU.add), [tge])
            tge = vop(lambda e: e.reciprocal(out=gates, in_=gates), [tge])
            bank_free[7] = tge
            gr_spill[s] = dma("gpsimd", sc_gr[s], r3(gates, 32)[:, :, 16:32], [tge])
            kv_block(2 * s, 0, ev_tr)
            kv_block(2 * s + 1, 256, ev_tr)
            ln1f_last_reader = S.fence(("tt",))[0]
            if s + 1 < NSEG:
                emit_ln1(s + 1, ev_tr)

            for lb in range(2):
                blk = 2 * s + lb
                mask = wmask0 if blk == 0 else wmask
                last_tsc = None
                pv_last = {}

                def gate_grp(hgrp, p_idx_base):
                    nh = (7, 7, 2)[hgrp]
                    hoff = 7 * hgrp
                    for qc in range(2):
                        p_idx = p_idx_base + qc
                        bk = (4 if qc == 0 else 6) + (hgrp & 1)
                        tr_ = vop(lambda e, bk=bk, nh=nh, hoff=hoff: e.reciprocal(
                            out=glr[:, hoff:hoff + nh],
                            in_=r3(ps[bk][:, 0:nh * 65], 65)[:, 0:nh, 64:65].rearrange(
                                "q a b -> q (a b)")), [pv_last[bk], glr_war.get(hgrp)])
                        tr_ = vop(lambda e, nh=nh, hoff=hoff, p_idx=p_idx: e.tensor_mul(
                            out=glr[:, hoff:hoff + nh], in0=glr[:, hoff:hoff + nh],
                            in1=gates[:, 32 * p_idx + hoff:32 * p_idx + hoff + nh]), [tr_, tge])
                        tr_ = vop(lambda e, bk=bk, nh=nh, hoff=hoff, p_idx=p_idx: e.tensor_tensor(
                            out=r3(localg[:, p_idx * D:(p_idx + 1) * D], 64)[:, hoff:hoff + nh, :],
                            in0=r3(ps[bk][:, 0:nh * 65], 65)[:, 0:nh, 0:64],
                            in1=glr[:, hoff:hoff + nh][:, :, None].broadcast_to([128, nh, 64]),
                            op=ALU.mult), [tr_, lg_free])
                        glr_war[hgrp] = tr_
                        bank_free[bk] = tr_

                def emit_sc(h):
                    ptb = p_t[h % 4]
                    sb0, sb1 = (0, 1) if h % 2 == 0 else (2, 3)
                    def mms(e, h=h, blk=blk, lb=lb, sb0=sb0, sb1=sb1):
                        r = None
                        for jc in range(4):
                            kslot = ((blk - 1 if jc < 2 else blk) + 1) % 3
                            kc = 256 * (h // 2) + 128 * (jc % 2)
                            i0_ = 128 if jc == 3 else 0
                            i1_ = 128 if jc == 0 else 256
                            bnk = sb0 if jc < 2 else sb1
                            r = e.matmul(ps[bnk][:, 256 * (jc % 2) + i0_:256 * (jc % 2) + i1_],
                                         k_ring[kslot][64 * (h % 2):64 * (h % 2) + 64, kc:kc + 128],
                                         q_fm[64 * (h % 2):64 * (h % 2) + 64,
                                              512 * (h // 2) + 256 * lb + i0_:512 * (h // 2) + 256 * lb + i1_],
                                         start=True, stop=True)
                        return r
                    tsc = top(mms, [kv_ktok[blk - 1], kv_ktok[blk], evq_l[h // 2],
                                    bank_free[sb0], bank_free[sb1]])
                    te = sop(lambda e, ptb=ptb, sb0=sb0: e.activation(
                        out=ptb, in_=ps2[sb0 // 2], func=AF.Exp, scale=cpool[:, 2:3]),
                        [tsc, pt_free[h % 4]])
                    bank_free[sb0] = te
                    bank_free[sb1] = te
                    tmk = vop(lambda e, ptb=ptb, mask=mask: e.tensor_mul(
                        out=ptb, in0=ptb, in1=mask), [te])
                    hstate[h] = (tmk, tsc)

                def emit_pv(h):
                    ptb = p_t[h % 4]
                    tmk, _ = hstate[h]
                    hgrp = min(h // 7, 2)
                    hoff = h - 7 * hgrp
                    for qc in range(2):
                        bk = (4 if qc == 0 else 6) + (hgrp & 1)
                        jcs = (0, 1, 2) if qc == 0 else (1, 2, 3)
                        def mmpv(e, h=h, qc=qc, bk=bk, hoff=hoff, blk=blk, ptb=ptb, jcs=jcs):
                            r = None
                            for ji, jc in enumerate(jcs):
                                vs = (((blk - 1 if jc < 2 else blk) + 1) % 3) * 2 + (jc % 2)
                                r = e.matmul(ps[bk][:, hoff * 65:hoff * 65 + 65],
                                             ptb[:, 256 * jc + 128 * qc:256 * jc + 128 * qc + 128],
                                             v_ring[vs][:, 65 * h:65 * h + 65],
                                             start=(ji == 0), stop=(ji == 2))
                            return r
                        tpv = top(mmpv, [tmk, bank_free[bk], kv_tok[blk - 1], kv_tok[blk]])
                        pv_last[bk] = tpv
                    pt_free[h % 4] = pv_last[6 + (hgrp & 1)]
                    if h == 6:
                        gate_grp(0, 2 * lb)
                    elif h == 13:
                        gate_grp(1, 2 * lb)
                    elif h == 15:
                        gate_grp(2, 2 * lb)

                hstate = {}
                for h in range(NH):
                    emit_sc(h)
                    if h >= 2:
                        emit_pv(h - 2)
                emit_pv(NH - 2)
                emit_pv(NH - 1)
                last_tsc = hstate[NH - 1][1]
                kslot_free[blk % 3] = last_tsc
                vslot_free[(blk % 3) * 2] = pv_last[6]
                vslot_free[(blk % 3) * 2 + 1] = pv_last[6]
            lg_spill[s] = dma("gpsimd", sc_localg[s], localg,
                              [bank_free[4], bank_free[5], bank_free[6], bank_free[7]])
            lg_free = lg_spill[s]

        def finish_debug():
            tk = None
            for s_ in range(NSEG):
                tk = dma("gpsimd", y_out[s_], x_ring[0])
            S.add("sync", None, S.dma_fence(), semname="fin", inc=1)
            with nc.Block() as block:
                S.emit(block)

        ph1 = S.fence()
        if DEBUG_PHASES <= 1:
            finish_debug()
            return nc

        t_wgk = ldw(w_gkv[:, 0:4096], wgk_d, 8, 512, ph1)
        t_wgv = ldw(w_gkv[:, 4096:8192], wgv_d, 8, 512, ph1)
        t_wro = ldw(w_ro, ro_d, 4, 1024, ph1)
        t_wwo = ldw(w_wo, wo_d, 8, 1024, ph1)

        A32.reset(); A16.reset()
        A32.mark(); A16.mark()
        pooled_tk = A32.alloc(1024)
        t_ptk = dma("sync", pooled_tk[0:64, :], cc_po, [t_ag] + ph1)
        ev_pt = None
        for g in range(8):
            bk = g % 2
            tm = top(lambda e, g=g, bk=bk: e.matmul(
                ps[bk][:, 0:64], pooled_tk[0:64, g * 128:(g + 1) * 128],
                idf32[0:64, 0:64], start=True, stop=True),
                [t_ptk, bank_free[bk]] + ph1)
            ev_pt = vop(lambda e, g=g, bk=bk: e.tensor_copy(
                out=pooled_fm[:, g * 64:(g + 1) * 64], in_=ps[bk][:, 0:64]), [tm])
            bank_free[bk] = ev_pt

        def gkgv(dst, woff, depw):
            ev = None
            for i in range(4):
                bk = 2 + (i % 2)
                def mm(e, i=i, bk=bk, woff=woff):
                    r = None
                    for k in range(8):
                        r = e.matmul(ps[bk][:, 0:64],
                                     w_gkv[:, woff + k * 512 + i * 128:woff + k * 512 + (i + 1) * 128],
                                     pooled_fm[:, k * 64:(k + 1) * 64],
                                     start=(k == 0), stop=(k == 7))
                    return r
                tm = top(mm, [ev_pt, bank_free[bk]] + list(depw))
                ev = vop(lambda e, i=i, bk=bk, dst=dst: e.tensor_copy(
                    out=dst[:, i * 64:(i + 1) * 64], in_=ps[bk][:, 0:64]), [tm])
                bank_free[bk] = ev
            return ev
        ev_gk = gkgv(gk_fm, 0, t_wgk)
        ev_gv = gkgv(gv_fm, 4096, t_wgv)
        ev_gvr = None
        for lh in range(8):
            for jh in range(2):
                bk = 4 + 2 * (lh % 2) + jh
                def mm(e, lh=lh, jh=jh, bk=bk):
                    return e.matmul(ps[bk][0:64, 0:512],
                                    gv_fm[64 * (lh % 2):64 * (lh % 2) + 64,
                                          64 * (lh // 2):64 * (lh // 2) + 64],
                                    w_ro[64 * (lh % 2):64 * (lh % 2) + 64,
                                         (lh // 2) * 1024 + jh * 512:(lh // 2) * 1024 + (jh + 1) * 512],
                                    start=True, stop=True)
                tm = top(mm, [ev_gv, bank_free[bk]] + t_wro)
                ev_gvr = vop(lambda e, lh=lh, jh=jh, bk=bk: e.tensor_copy(
                    out=gvr_ap(lh)[:, jh * 512:(jh + 1) * 512], in_=ps[bk][0:64, 0:512]), [tm])
                bank_free[bk] = ev_gvr
        ph15 = S.fence()
        if DEBUG_PHASES <= 2:
            finish_debug()
            return nc

        A32.reset(); A16.reset()
        A32.mark(); A16.mark()
        xs = A32.alloc(NPT * D)
        ln3 = A32.alloc(NPT * D)
        o0t = ln3
        o0f = A32.alloc(8 * 512)
        remg_r = [A32.alloc(D) for _ in range(2)]
        lmask = A16.alloc(NPT * 512)
        plat_r = [A16.alloc(512) for _ in range(2)]
        platT = A16.alloc(8 * 512)
        lgseg = A16.alloc(NPT * D)
        comb_r = [A16.alloc(D) for _ in range(2)]
        comb_f = A16.alloc(8 * 256)
        gq2r = [A16.alloc(4 * 512) for _ in range(2)]
        attn_r = [A16.alloc(NPT * D) for _ in range(2)]

        gsum_tok = [None]
        x2_spill = [None] * NSEG
        o0_spill = [None] * NSEG
        pv = {"gq2": [None, None], "lg": None, "plat": [None, None],
              "remg": [None, None], "comb": [None, None],
              "o0f": None, "o0t": None, "ln3": None, "x2": None, "gr": None,
              "lm": None, "attn": [None, None], "xs": None, "den": [None, None]}
        AB = {}

        def ph2_A(s):
            xr = s % 2
            gq2 = gq2r[s % 2]
            t_lm = dma("sync", lmask, lm_d[s], [pv["lm"]] + ph15)
            t_gq = dma("sync", gq2, sc_gq[s], [gq_spill[s], pv["gq2"][s % 2]] + ph15)
            t_lg = dma("sync", lgseg, sc_localg[s], [lg_spill[s], pv["lg"]] + ph15)
            t_gr = dma("sync", gates[:, 0:64], sc_gr[s], [gr_spill[s], pv["gr"]] + ph15)
            st = [dict() for _ in range(NPT)]

            def st0(p):
                pr = p % 2
                plat = plat_r[pr]
                dn = den[:, 8 * pr:8 * pr + 8]
                def mml(e, p=p):
                    r = None
                    for lh in range(8):
                        bnk = 0 if lh % 2 == 0 else 1
                        r = e.matmul(ps[bnk][:, 64 * (lh // 2):64 * (lh // 2) + 64],
                                     gq2[64 * (lh % 2):64 * (lh % 2) + 64,
                                         512 * (lh // 2) + 128 * p:512 * (lh // 2) + 128 * p + 128],
                                     gk_fm[64 * (lh % 2):64 * (lh % 2) + 64,
                                           64 * (lh // 2):64 * (lh // 2) + 64],
                                     start=True, stop=True)
                    return r
                tl = top(mml, [t_gq, ev_gk, bank_free[0], bank_free[1]])
                te = sop(lambda e, plat=plat: e.activation(
                    out=r3(plat, 64)[:, 0::2, :], in_=r3(ps[0][:, 0:256], 64),
                    func=AF.Exp, scale=cpool[:, 2:3]), [tl, pv["plat"][pr]])
                te = sop(lambda e, plat=plat: e.activation(
                    out=r3(plat, 64)[:, 1::2, :], in_=r3(ps[1][:, 0:256], 64),
                    func=AF.Exp, scale=cpool[:, 2:3]), [tl])
                bank_free[0] = te
                bank_free[1] = te
                tm_ = vop(lambda e, p=p, plat=plat: e.tensor_mul(
                    out=plat, in0=plat, in1=lmask[:, p * 512:(p + 1) * 512]), [te, t_lm])
                st[p]["lm"] = tm_
                td = vop(lambda e, plat=plat, dn=dn: e.reduce_sum(
                    dn, r3(plat, 64), axis=mybir.AxisListType.X), [tm_, pv["den"][pr]])
                td = vop(lambda e, dn=dn: e.reciprocal(out=dn, in_=dn), [td])
                tn = vop(lambda e, plat=plat, dn=dn: e.tensor_tensor(
                    out=r3(plat, 64), in0=r3(plat, 64),
                    in1=dn[:, :, None].broadcast_to([128, 8, 64]), op=ALU.mult), [td])
                pv["plat"][pr] = tn
                pv["den"][pr] = tn
                st[p]["tn"] = tn

            def st1(p):
                pr = p % 2
                plat = plat_r[pr]
                tn = st[p]["tn"]
                tts = {}
                for lh in range(8):
                    bk = 2 + (lh % 2)
                    tts[bk] = top(lambda e, lh=lh, bk=bk, plat=plat, p=p: e.matmul(
                        ps[bk][0:64, 128 * (lh // 2):128 * (lh // 2) + 128],
                        plat[:, 64 * lh:64 * lh + 64], idbf, start=True, stop=True),
                        [tn, bank_free[bk]])
                tev = None
                for bk in (2, 3):
                    tev = sop(lambda e, bk=bk, p=p: e.copy(
                        out=r3(platT, 512)[0:64, (bk - 2)::2, 128 * p:128 * p + 128],
                        in_=r3(ps[bk][0:64, 0:512], 128)), [tts[2], tts[3]])
                    bank_free[bk] = tev
                st[p]["tev"] = tev

            def st2(p):
                pr = p % 2
                remg = remg_r[pr]
                def mmr(e, p=p):
                    r = None
                    for lh in range(8):
                        for jh in range(2):
                            r = e.matmul(ps[4 + jh][:, 0:512],
                                         platT[0:64, 512 * lh + 128 * p:512 * lh + 128 * p + 128],
                                         gvr_ap(lh)[:, jh * 512:(jh + 1) * 512],
                                         start=(lh == 0), stop=(lh == 7))
                    return r
                trm = top(mmr, [st[p]["tev"], ev_gvr, bank_free[4], bank_free[5]])
                gpair = gates[:, 16 * p:16 * p + 16][:, :, None].broadcast_to([128, 16, 64])
                tg1 = vop(lambda e, remg=remg, gpair=gpair: e.tensor_tensor(
                    out=r3(remg, 64), in0=r3(ps2[2], 64), in1=gpair, op=ALU.mult),
                    [trm, t_gr, pv["remg"][pr]])
                bank_free[4] = tg1
                bank_free[5] = tg1
                st[p]["tg1"] = tg1
                st[p]["trm"] = trm

            def st3(p):
                pr = p % 2
                remg = remg_r[pr]
                comb = comb_r[pr]
                tcb = vop(lambda e, p=p, remg=remg, comb=comb: e.tensor_add(
                    out=comb, in0=remg, in1=lgseg[:, p * D:(p + 1) * D]),
                    [st[p]["tg1"], t_lg, pv["comb"][pr]])
                pv["remg"][pr] = tcb
                tev2 = None
                for half in range(2):
                    bk = 2 + half
                    def mmt(e, half=half, bk=bk, comb=comb):
                        r = None
                        for gg in range(4):
                            g = half * 4 + gg
                            r = e.matmul(ps[bk][:, gg * 128:(gg + 1) * 128],
                                         comb[:, g * 128:(g + 1) * 128],
                                         idbf, start=True, stop=True)
                        return r
                    tmt = top(mmt, [tcb, bank_free[bk]])
                    tev2 = sop(lambda e, p=p, half=half, bk=bk: e.copy(
                        out=r3(comb_f, 256)[:, half * 4:(half + 1) * 4,
                                            (p % 2) * 128:(p % 2) * 128 + 128],
                        in_=r3(ps[bk], 128)), [tmt])
                    bank_free[bk] = tev2
                pv["comb"][pr] = tev2
                st[p]["tev2"] = tev2

            def st4(p):
                def mmw(e, p=p):
                    r = None
                    for jh in range(2):
                        for k in range(8):
                            r = e.matmul(ps[6 + jh][:, 0:512],
                                         comb_f[:, k * 256 + 128 * (p % 2):
                                                k * 256 + 128 * (p % 2) + 128],
                                         w_wo[:, k * 1024 + jh * 512:k * 1024 + (jh + 1) * 512],
                                         start=(k == 0), stop=(k == 7))
                    return r
                tw = top(mmw, [st[p]["tev2"], bank_free[6], bank_free[7]] + t_wwo)
                ta = None
                for jh in range(2):
                    ta = sop(lambda e, p=p, jh=jh, xr=xr: e.copy(
                        out=attn_r[xr][:, p * D + jh * 512:p * D + jh * 512 + 512],
                        in_=ps[6 + jh]), [tw, pv["attn"][xr]])
                    bank_free[6 + jh] = ta
                st[p]["ta"] = ta
                st[p]["tw"] = tw

            stages = (st0, st1, st2, st3, st4)
            for k in range(len(stages) + NPT - 1):
                for p in range(NPT):
                    kk = k - p
                    if 0 <= kk < len(stages):
                        stages[kk](p)
            AB[s] = (st[NPT - 1]["ta"], st[NPT - 1]["lm"],
                     st[NPT - 1]["trm"], st[NPT - 1]["tw"])

        def ph2_B1(s):
            xr = s % 2
            ta_done = AB.pop(s)[0]
            t_xs = dma("sync", xs, x_in[s], [pv["xs"], pv["x2"]] + ph15)
            lnt = layernorm(lambda p, xr=xr: (attn_r[xr][:, p * D:p * D + 512],
                                              attn_r[xr][:, p * D + 512:(p + 1) * D]),
                            lambda p: ln3[:, p * D:(p + 1) * D], NPT, [ta_done],
                            apply_deps=[pv["ln3"]])
            pv["attn"][xr] = lnt[-1]
            tx2 = None
            for p in range(NPT):
                tx2 = vop(lambda e, p=p: e.tensor_add(
                    out=xs[:, p * D:(p + 1) * D], in0=ln3[:, p * D:(p + 1) * D],
                    in1=xs[:, p * D:(p + 1) * D]), [lnt[2 * p + 1], t_xs])
            pv["ln3"] = tx2
            ln2t = layernorm(lambda p: (xs[:, p * D:p * D + 512],
                                        xs[:, p * D + 512:(p + 1) * D]),
                             lambda p: o0t[:, p * D:(p + 1) * D], NPT, [tx2],
                             apply_deps=[pv["o0t"]])
            x2_spill[s] = dma("sync", sc_x2[s], xs, [tx2, S.fence(("vv",))[0]])
            pv["x2"] = x2_spill[s]
            pv["xs"] = ln2t[-1]
            B1T[s] = ln2t

        def ph2_B2(s):
            ln2t = B1T.pop(s)
            tev3 = None
            tmo = None
            for p in range(NPT):
                for half in range(2):
                    obk = 2 + half
                    def mmo(e, p=p, half=half, obk=obk):
                        r = None
                        for gg in range(4):
                            g = half * 4 + gg
                            r = e.matmul(ps[obk][:, gg * 128:(gg + 1) * 128],
                                         o0t[:, p * D + g * 128:p * D + (g + 1) * 128],
                                         idf32, start=True, stop=True)
                        return r
                    tmo = top(mmo, [ln2t[2 * p + 1], bank_free[obk]])
                    tev3 = sop(lambda e, p=p, half=half, obk=obk: e.copy(
                        out=r3(o0f, 512)[:, half * 4:(half + 1) * 4, p * 128:(p + 1) * 128],
                        in_=r3(ps[obk], 128)), [tmo, pv["o0f"]])
                    bank_free[obk] = tev3
            pv["o0t"] = tmo
            pv["o0f"] = tev3
            tgp = vop(lambda e: e.reduce_sum(pooled[:, 0:8], r3(o0f, 512),
                                             axis=mybir.AxisListType.X),
                      [pv["o0f"], gsum_tok[0]])
            if s == 0:
                tgp = vop(lambda e: e.tensor_copy(out=gsum, in_=pooled[:, 0:8]), [tgp])
            else:
                tgp = vop(lambda e: e.tensor_add(out=gsum, in0=gsum, in1=pooled[:, 0:8]),
                          [tgp, gsum_tok[0]])
            gsum_tok[0] = tgp
            o0_spill[s] = dma("gpsimd", sc_oa[s], o0f, [pv["o0f"]])
            pv["o0f"] = o0_spill[s]

        B1T = {}
        w3toks = []
        AB7 = [None]
        for s in range(NSEG):
            if s >= 2:
                ph2_B2(s - 2)
            ph2_A(s)
            pv["gq2"][s % 2] = S.fence(("tt",))[0]
            pv["lg"] = S.fence(("vv",))[0]
            pv["gr"] = pv["lg"]
            pv["lm"] = AB[s][1]
            if s == NSEG - 1:
                w3toks = [AB[s][2], AB[s][3]]
                AB7[0] = AB[s][1]
            if s >= 1:
                ph2_B1(s - 1)

        def emit_b_tail():
            ph2_B2(NSEG - 2)
            ph2_B1(NSEG - 1)
            ph2_B2(NSEG - 1)

        ph2 = S.fence()

        t_cn1 = ldw(w_cn1, cn1_d, 8, 1024, w3toks)
        t_cn2 = ldw(w_cn2, cn2_d, 8, 512, w3toks)
        t_cn3 = ldw(w_cn3, cn3_d, 4, 256, w3toks)
        t_cn4 = ldw(w_cn4, cn4_d, 2, 1, w3toks)
        t_gc1 = ldw(w_gc1, g1_d, 8, 512, w3toks)
        t_gc2 = ldw(w_gc2, g2_d, 4, 256, w3toks)
        t_gc3 = ldw(w_gc3, g3_d, 2, 1, w3toks)

        ph25_tok = {}

        def emit_ar():
            t_gso = dma("sync", cc_gi, gsum, [gsum_tok[0]])
            if "noar" in DEBUG_CUT:
                t_arr = dma("gpsimd", cc_go, cc_gi, [t_gso])
            else:
                t_arr = S.add("gpsimd", lambda e: e.collective_compute(
                    "AllReduce", ALU.add, replica_groups=[[0, 1, 2, 3], [4, 5, 6, 7]],
                    ins=[cc_gi], outs=[cc_go]), [t_gso], semname="cc", inc=1)
            ph25_tok["gsi"] = dma("sync", gsum, cc_go, [t_arr, gsum_tok[0]])

        def emit_ph25():
            tmn = vop(lambda e: e.tensor_scalar(out=gsum, in0=gsum, scalar1=1.0 / 16384,
                                                scalar2=None, op0=ALU.mult),
                      [ph25_tok["gsi"]] + ph2)
            tmn = vop(lambda e: e.tensor_copy(out=mean_bf, in_=gsum), [tmn])
            th = None
            for i in range(4):
                th = top(lambda e, i=i: [e.matmul(
                    ps[0][:, i:i + 1], w_gc1[:, k * 512 + i * 128:k * 512 + (i + 1) * 128],
                    mean_bf[:, k:k + 1], start=(k == 0), stop=(k == 7)) for k in range(8)][-1],
                    [tmn, bank_free[0]] + t_gc1)
            tg1e = None
            for i in range(4):
                tg1e = sop(lambda e, i=i: e.activation(
                    out=hg1[:, i:i + 1], in_=ps[0][:, i:i + 1], func=AF.Gelu,
                    bias=biases[:, 14 + i:15 + i]), [th])
            bank_free[0] = tg1e
            th2 = None
            for i in range(2):
                th2 = top(lambda e, i=i: [e.matmul(
                    ps[1][:, i:i + 1], w_gc2[:, k * 256 + i * 128:k * 256 + (i + 1) * 128],
                    hg1[:, k:k + 1], start=(k == 0), stop=(k == 3)) for k in range(4)][-1],
                    [tg1e, bank_free[1]] + t_gc2)
            tg2e = None
            for i in range(2):
                tg2e = sop(lambda e, i=i: e.activation(
                    out=hg2[:, i:i + 1], in_=ps[1][:, i:i + 1], func=AF.Gelu,
                    bias=biases[:, 18 + i:19 + i]), [th2])
            bank_free[1] = tg2e
            tm3 = top(lambda e: [e.matmul(ps[3][0:1, 0:1], w_gc3[:, k:k + 1], hg2[:, k:k + 1],
                                          start=(k == 0), stop=(k == 1)) for k in range(2)][-1],
                      [tg2e, bank_free[3]] + t_gc3)
            tgf = sop(lambda e: e.activation(out=ar_sb[0:1, 4:5], in_=ps[3][0:1, 0:1], func=AF.Exp,
                                             scale=cpool[0:1, 3:4], bias=biases[0:1, 23:24]), [tm3])
            bank_free[3] = tgf
            tgf = vop(lambda e: e.tensor_scalar(out=ar_sb[0:1, 4:5], in0=ar_sb[0:1, 4:5],
                                                scalar1=1.0, scalar2=None, op0=ALU.add), [tgf])
            tgf = vop(lambda e: e.reciprocal(out=ar_sb[0:1, 4:5], in_=ar_sb[0:1, 4:5]), [tgf])
            tab = vop(lambda e: e.tensor_scalar(out=ar_sb[0:1, 3:4], in0=ar_sb[0:1, 4:5],
                                                scalar1=1.4, scalar2=-0.7, op0=ALU.mult,
                                                op1=ALU.add), [tgf])
            tab = vop(lambda e: e.tensor_mul(out=ar_sb[0:1, 3:4], in0=ar_sb[0:1, 3:4],
                                             in1=ar_sb[0:1, 0:1]), [tab])
            tab = vop(lambda e: e.tensor_scalar(out=ar_sb[0:1, 1:2], in0=ar_sb[0:1, 3:4],
                                                scalar1=1.0, scalar2=0.1, op0=ALU.mult,
                                                op1=ALU.add), [tab])
            tab = vop(lambda e: e.tensor_scalar(out=ar_sb[0:1, 2:3], in0=ar_sb[0:1, 0:1],
                                                scalar1=0.3, scalar2=None, op0=ALU.mult), [tab])
            ph25_tok["tab"] = tab

        if DEBUG_PHASES <= 3:
            emit_b_tail()
            emit_ar()
            emit_ph25()
            finish_debug()
            return nc

        A32.reset(); A16.reset()
        A32.mark(); A16.mark()
        x2l = A32.alloc(NPT * D)
        of_ring = [A32.alloc(8 * 512) for _ in range(2)]
        o3t = A32.alloc(NPT * D)
        tmpq = A32.alloc(1024)
        crow = A32.alloc(512)
        cb_sb = A32.alloc(512)
        ob_ring = [A16.alloc(8 * 512) for _ in range(2)]
        h1 = A16.alloc(8 * 512)
        h2 = A16.alloc(4 * 512)
        h3 = A16.alloc(2 * 512)
        crow_bf = A16.alloc(512)

        of_free = [None, None]
        ob_free = [None, None]
        st_tok = [o0_spill[s] for s in range(NSEG)]
        p3 = {"h1": None, "h2": None, "h3": None, "crow": None, "cb": None,
              "cb2": None, "tmpq": None, "y": None, "x2l": None}
        dep_ld = S.fence(("tt", "ss", "vv"))
        osrc = [sc_oa, sc_ob, sc_oa]
        odst = [sc_ob, sc_oa, None]
        TWSRC = {0: 4, 1: 5, 2: 6, 3: 7, 4: 0, 5: 1, 6: 2, 7: 3}
        TWSGN = {0: 1, 1: -1, 2: -1, 3: 1}

        iters = [(cyc, s) for cyc in range(3) for s in range(NSEG)]
        pend = {}
        loaded = set()

        def stok(s):
            return st_tok[s] if st_tok[s] is not None else o0_spill[s]

        def ph3_loads(cyc, s):
            xr = s % 2
            if cyc == 0 and s == 0:
                t_of = None
                t_ob = dma("gpsimd", ob_ring[0], osrc[0][0],
                           [stok(0), AB7[0]])
            elif cyc == 0 and s == 1:
                t_of = dma("sync", of_ring[1], osrc[0][1],
                           [stok(1), o0_spill[NSEG - 1]] + dep_ld)
                t_ob = dma("gpsimd", ob_ring[1], osrc[0][1],
                           [stok(1)] + dep_ld)
            else:
                t_of = dma("sync", of_ring[xr], osrc[cyc][s],
                           [of_free[xr], stok(s)] + dep_ld)
                t_ob = dma("gpsimd", ob_ring[xr], osrc[cyc][s],
                           [ob_free[xr], stok(s)] + dep_ld)
            pend[(cyc, s)] = (t_of, t_ob)
            loaded.add((cyc, s))

        ph3_loads(0, 0)

        def ph3_head(cyc, s, it_idx):
            xr = s % 2
            t_of, t_ob = pend.pop((cyc, s))
            ev1 = None
            for i in range(8):
                bk = i % 2
                tm = top(lambda e, i=i, bk=bk, xr=xr: [e.matmul(
                    ps[bk][:, 0:512], w_cn1[:, k * 1024 + i * 128:k * 1024 + (i + 1) * 128],
                    ob_ring[xr][:, k * 512:(k + 1) * 512], start=(k == 0), stop=(k == 7))
                    for k in range(8)][-1], [t_ob, bank_free[bk]] + t_cn1)
                ev1 = sop(lambda e, i=i, bk=bk: e.activation(
                    out=h1[:, i * 512:(i + 1) * 512], in_=ps[bk], func=AF.Gelu,
                    bias=biases[:, i:i + 1]), [tm, p3["h1"]])
                bank_free[bk] = ev1
            p3["h1"] = ev1
            if it_idx == 0:
                emit_b_tail()
                emit_ar()
                t_of = dma("sync", of_ring[0], osrc[0][0],
                           [stok(0), pv["o0t"]])
                ph3_loads(0, 1)
            ev2 = None
            for i in range(4):
                bk = i % 2
                tm = top(lambda e, i=i, bk=bk: [e.matmul(
                    ps[bk][:, 0:512], w_cn2[:, k * 512 + i * 128:k * 512 + (i + 1) * 128],
                    h1[:, k * 512:(k + 1) * 512], start=(k == 0), stop=(k == 7))
                    for k in range(8)][-1], [ev1, bank_free[bk]] + t_cn2)
                ev2 = sop(lambda e, i=i, bk=bk: e.activation(
                    out=h2[:, i * 512:(i + 1) * 512], in_=ps[bk], func=AF.Gelu,
                    bias=biases[:, 8 + i:9 + i]), [tm, p3["h2"]])
                bank_free[bk] = ev2
            p3["h2"] = ev2
            ev3 = None
            for i in range(2):
                bk = i % 2
                tm = top(lambda e, i=i, bk=bk: [e.matmul(
                    ps[bk][:, 0:512], w_cn3[:, k * 256 + i * 128:k * 256 + (i + 1) * 128],
                    h2[:, k * 512:(k + 1) * 512], start=(k == 0), stop=(k == 3))
                    for k in range(4)][-1], [ev2, bank_free[bk]] + t_cn3)
                ev3 = sop(lambda e, i=i, bk=bk: e.activation(
                    out=h3[:, i * 512:(i + 1) * 512], in_=ps[bk], func=AF.Gelu,
                    bias=biases[:, 12 + i:13 + i]), [tm, p3["h3"]])
                bank_free[bk] = ev3
            p3["h3"] = ev3
            t4b = 6 if it_idx == 1 else 2
            tm4 = top(lambda e, t4b=t4b: [e.matmul(ps[t4b][0:1, 0:512], w_cn4[:, k:k + 1],
                                                   h3[:, k * 512:(k + 1) * 512],
                                                   start=(k == 0), stop=(k == 1))
                                          for k in range(2)][-1],
                      [ev3, bank_free[t4b]] + t_cn4)
            return dict(cyc=cyc, s=s, xr=xr, t_of=t_of, ev1=ev1, tm4=tm4, t4b=t4b,
                        it=it_idx)

        def ph3_tail(c):
            cyc, s, xr, t_of, ev1, tm4, t4b = (c["cyc"], c["s"], c["xr"], c["t_of"],
                                               c["ev1"], c["tm4"], c["t4b"])
            tcw = sop(lambda e, t4b=t4b: e.activation(
                out=crow[0:1, :], in_=ps[t4b][0:1, 0:512],
                func=AF.Tanh, scale=cpool[0:1, 4:5],
                bias=biases[0:1, 22:23]), [tm4, p3["crow"]])
            bank_free[t4b] = tcw
            tcw = vop(lambda e: e.tensor_scalar(out=crow[0:1, :], in0=crow[0:1, :],
                                                scalar1=ar_sb[0:1, 2:3],
                                                scalar2=ar_sb[0:1, 1:2],
                                                op0=ALU.mult, op1=ALU.add),
                      [tcw, ph25_tok["tab"]])
            tcw = vop(lambda e: e.tensor_copy(out=crow_bf[0:1, :], in_=crow[0:1, :]), [tcw])
            p3["crow"] = tcw
            tcb = top(lambda e: e.matmul(ps[3][:, 0:512], oner, crow_bf[0:1, :],
                                         start=True, stop=True),
                      [tcw, bank_free[3], p3["cb"]])
            tcs = sop(lambda e: e.copy(out=cb_sb, in_=ps[3][:, 0:512]),
                      [tcb, p3["cb2"]])
            bank_free[3] = tcs
            tup = None
            tmfl = None
            tevl = None
            for f4 in range(4):
                fa, fb = f4, f4 + 4
                ta_ = vop(lambda e, fb=fb, xr=xr: e.tensor_tensor(
                    out=tmpq[:, 0:512], in0=cb_sb,
                    in1=of_ring[xr][:, fb * 512:(fb + 1) * 512], op=ALU.mult),
                    [tcs, t_of, p3["tmpq"]])
                tb_ = vop(lambda e, fa=fa, xr=xr: e.tensor_tensor(
                    out=tmpq[:, 512:1024], in0=cb_sb,
                    in1=of_ring[xr][:, fa * 512:(fa + 1) * 512], op=ALU.mult),
                    [tcs, t_of])
                opa = ALU.add if TWSGN[fa // 2] > 0 else ALU.subtract
                opb = ALU.add if TWSGN[fb // 2] > 0 else ALU.subtract
                tu1 = vop(lambda e, fa=fa, xr=xr, opa=opa: e.tensor_tensor(
                    out=of_ring[xr][:, fa * 512:(fa + 1) * 512],
                    in0=of_ring[xr][:, fa * 512:(fa + 1) * 512],
                    in1=tmpq[:, 0:512], op=opa), [ta_, tb_])
                tup = vop(lambda e, fb=fb, xr=xr, opb=opb: e.tensor_tensor(
                    out=of_ring[xr][:, fb * 512:(fb + 1) * 512],
                    in0=of_ring[xr][:, fb * 512:(fb + 1) * 512],
                    in1=tmpq[:, 512:1024], op=opb), [tu1])
                p3["tmpq"] = tup
                if cyc == 2:
                    for g, bk, tk in ((fa, 4, tu1), (fb, 5, tup)):
                        def mmf(e, g=g, bk=bk, xr=xr):
                            r = None
                            for p in range(NPT):
                                r = e.matmul(ps[bk][:, p * 128:(p + 1) * 128],
                                             of_ring[xr][:, g * 512 + 128 * p:g * 512 + 128 * p + 128],
                                             idf32, start=True, stop=True)
                            return r
                        tmfl = top(mmf, [tk, bank_free[bk]])
                        tevl = sop(lambda e, g=g, bk=bk: e.copy(
                            out=r3(o3t, D)[:, :, g * 128:(g + 1) * 128],
                            in_=r3(ps[bk][:, 0:512], 128)), [tmfl, p3["y"]])
                        bank_free[bk] = tevl
            p3["cb"] = tup
            p3["cb2"] = tup
            ob_free[xr] = ev1
            if cyc < 2:
                st_tok[s] = dma("gpsimd", odst[cyc][s], of_ring[xr], [tup])
                of_free[xr] = st_tok[s]
            else:
                t_x2 = dma("gpsimd", x2l, sc_x2[s],
                           [x2_spill[s], p3["x2l"],
                            x2_spill[NSEG - 1], pv["xs"]])
                tev4 = tevl
                of_free[xr] = tmfl
                ln4 = layernorm(lambda p: (o3t[:, p * D:p * D + 512],
                                           o3t[:, p * D + 512:(p + 1) * D]),
                                lambda p: o3t[:, p * D:(p + 1) * D], NPT,
                                [tev4])
                tyv = None
                for p in range(NPT):
                    tyv = vop(lambda e, p=p: e.tensor_add(
                        out=o3t[:, p * D:(p + 1) * D], in0=o3t[:, p * D:(p + 1) * D],
                        in1=x2l[:, p * D:(p + 1) * D]), [ln4[-1], t_x2])
                ty = dma("sync", y_out[s], o3t, [tyv])
                p3["y"] = ty
                p3["x2l"] = tyv
            for nx in (c["it"] + 1, c["it"] + 2):
                if nx < len(iters) and iters[nx] not in loaded:
                    ph3_loads(*iters[nx])

        held = None
        for it_idx, (cyc, s) in enumerate(iters):
            c = ph3_head(cyc, s, it_idx)
            if it_idx == 0:
                held = c
                continue
            if it_idx == 1:
                emit_ph25()
                ph3_tail(held)
                held = None
            ph3_tail(c)

        S.add("sync", None, S.dma_fence(), semname="fin", inc=1)

        with nc.Block() as block:
            S.emit(block)

    return nc


def _host_prep(inputs, core):
    b, q = core // 4, core % 4
    x = np.asarray(inputs["L0_attn_x"])
    xc = np.asarray(x[b, q * T:(q + 1) * T], np.float32)
    xseg = np.ascontiguousarray(
        xc.reshape(NSEG, NPT, 128, D).transpose(0, 2, 1, 3).reshape(NSEG, 128, NPT * D))
    if q == 0:
        xh = np.zeros((256, D), np.float32)
    else:
        xh = np.asarray(x[b, q * T - 256:q * T], np.float32)
    xhalo = np.ascontiguousarray(xh.reshape(2, 128, D).transpose(1, 0, 2).reshape(128, 2 * D))

    jj = np.arange(128)[:, None]
    ii = np.arange(256)[None, :]
    m0 = (ii < jj).astype(np.float32)
    m1 = (ii < jj + 128).astype(np.float32)
    m2 = (ii >= jj).astype(np.float32)
    m3 = (ii >= jj + 128).astype(np.float32)
    wmask = np.concatenate([m0, m1, m2, m3], axis=1).astype(BF)
    if q == 0:
        wmask0 = np.concatenate([0 * m0, 0 * m1, m2, m3], axis=1).astype(BF)
    else:
        wmask0 = wmask

    lat = np.zeros((NSEG, 128, NPT * 512), np.float32)
    for s in range(NSEG):
        for p in range(NPT):
            C = q * 16 + (s * NPT + p) // 2
            n_ok = min(C + 1, 64)
            for lh in range(8):
                lat[s, :, p * 512 + lh * 64: p * 512 + lh * 64 + n_ok] = 1.0
    latmask = lat.astype(BF)

    glw = np.asarray(inputs["L0_attn_gate_local_w"], np.float32)
    grw = np.asarray(inputs["L0_attn_gate_remote_w"], np.float32)
    wgate = np.zeros((D, 32), np.float32)
    for h in range(16):
        wgate[64 * h:64 * h + 64, h] = glw[h]
        wgate[64 * h:64 * h + 64, 16 + h] = grw[h]

    f32 = np.float32
    g = lambda k: np.asarray(inputs[k], f32)
    return {
        "xseg": xseg, "xhalo": xhalo,
        "wq": g("L0_attn_wq"), "wk": g("L0_attn_wk"), "wv": g("L0_attn_wv"),
        "wgq": g("L0_attn_wgq"), "wgk": g("L0_attn_wgk"), "wgv": g("L0_attn_wgv"),
        "r_out": g("L0_attn_r_out_w"), "w_out": g("L0_attn_w_out"), "wgate": wgate,
        "cn_w1": g("L0_ffn_cn_w1"), "cn_w2": g("L0_ffn_cn_w2"),
        "cn_w3": g("L0_ffn_cn_w3"), "cn_w4": g("L0_ffn_cn_w4"),
        "cn_b1": np.ascontiguousarray(g("L0_ffn_cn_b1").reshape(8, 128).T),
        "cn_b2": np.ascontiguousarray(g("L0_ffn_cn_b2").reshape(4, 128).T),
        "cn_b3": np.ascontiguousarray(g("L0_ffn_cn_b3").reshape(2, 128).T),
        "cn_b4": g("L0_ffn_cn_b4").reshape(1, 1),
        "gc_w1": g("L0_ffn_gc_w1"), "gc_w2": g("L0_ffn_gc_w2"), "gc_w3": g("L0_ffn_gc_w3"),
        "gc_b1": np.ascontiguousarray(g("L0_ffn_gc_b1").reshape(4, 128).T),
        "gc_b2": np.ascontiguousarray(g("L0_ffn_gc_b2").reshape(2, 128).T),
        "gc_b3": g("L0_ffn_gc_b3").reshape(1, 1),
        "ar": g("L0_ffn_adaptive_range").reshape(1, 1),
        "winmask": wmask, "winmask0": wmask0, "latmask": latmask,
        "ident_bf": np.eye(128, dtype=np.float32).astype(BF),
        "ident_f32": np.eye(128, dtype=np.float32),
        "ones_col": np.ones((128, 1), BF),
        "ones_row": np.ones((1, 128), BF),
    }


def kernel(**inputs):
    if "nc" not in _CACHE:
        _CACHE["nc"] = build_nc()
    nc = _CACHE["nc"]
    in_maps = [_host_prep(inputs, c) for c in range(8)]
    res = run_bass_kernel_spmd(nc, in_maps, core_ids=list(range(8)))
    out = np.zeros((2, 4 * T, D), np.float32)
    for c in range(8):
        b, q = c // 4, c % 4
        y = np.asarray(res.results[c]["y"])
        out[b, q * T:(q + 1) * T] = y.reshape(NSEG, 128, NPT, D).transpose(
            0, 2, 1, 3).reshape(T, D)
    return out


# version 17 (final)
# speedup vs baseline: 1.0992x; 1.0120x over previous
import numpy as np
import ml_dtypes
import concourse.bass as bass
import concourse.mybir as mybir
from concourse.bass_utils import run_bass_kernel_spmd

F32 = mybir.dt.float32
BF16 = mybir.dt.bfloat16
AF = mybir.ActivationFunctionType
ALU = mybir.AluOpType
BF = ml_dtypes.bfloat16

NSEG = 8
NPT = 4
T = 4096
D = 1024
NH = 16

ENGINES = ("sync", "gpsimd", "vector", "scalar", "tensor")
_CACHE = {}
DEBUG_PHASES = 99
DEBUG_CUT = set()


class Sched:
    def __init__(self, nc, stack):
        self.nc = nc
        self.items = {e: [] for e in ENGINES}
        self.sems = {}
        self.cnt = {}
        self._stack = stack

    def sem(self, name):
        if name not in self.sems:
            self.sems[name] = self._stack.enter_context(self.nc.semaphore(name))
            self.cnt[name] = 0
        return self.sems[name]

    def add(self, eng, fn, deps=(), semname=None, inc=None):
        deps = [d for d in deps if d is not None]
        if semname is None:
            if eng in ("sync", "gpsimd"):
                base = "dx" if eng == "sync" else "dg"
                k = self.cnt.get(base + "_n", 0)
                self.cnt[base + "_n"] = k + 1
                semname = f"{base}{k % 4}"
                self.sem(semname)
                if self.cnt[semname] > 0:
                    deps.append((semname, self.cnt[semname]))
                inc = 16
            else:
                semname = {"vector": "vv", "scalar": "ss", "tensor": "tt"}[eng]
                inc = 1
        elif inc is None:
            inc = 1
        self.sem(semname)
        for (sn, val) in deps:
            assert val <= self.cnt[sn], f"future dep {sn}:{val}>{self.cnt[sn]}"
        self.cnt[semname] += inc
        self.items[eng].append((fn, tuple(deps), semname, inc))
        return (semname, self.cnt[semname])

    DMASEMS = ("dx0", "dx1", "dx2", "dx3", "dg0", "dg1", "dg2", "dg3")

    def fence(self, names=("vv", "ss", "tt") + DMASEMS):
        return [(n, self.cnt[n]) for n in names if n in self.cnt and self.cnt[n] > 0]

    def dma_fence(self):
        return [(n, self.cnt[n]) for n in self.DMASEMS if n in self.cnt and self.cnt[n] > 0]

    def barrier(self):
        name = f"ph{self.cnt.get('_nbar', 0)}"
        self.cnt["_nbar"] = self.cnt.get("_nbar", 0) + 1
        self.sem(name)
        for e in ENGINES:
            self.cnt[name] += 1
            self.items[e].append((None, (), name, 1))
        return [(name, 5)]

    def emit(self, block):
        def mk(engname):
            def body(eng):
                lastw = {}
                for (fn, deps, semname, inc) in self.items[engname]:
                    for (sn, val) in deps:
                        if lastw.get(sn, -1) < val:
                            eng.wait_ge(self.sems[sn], val)
                            lastw[sn] = val
                    inst = eng.nop() if fn is None else fn(eng)
                    inst.then_inc(self.sems[semname], inc)
            return body
        block.sync(mk("sync"))
        block.gpsimd(mk("gpsimd"))
        block.vector(mk("vector"))
        block.scalar(mk("scalar"))
        block.tensor(mk("tensor"))


class Arena:
    def __init__(self, ap, cols):
        self.ap = ap
        self.cols = cols
        self.off = 0
        self.marks = []

    def alloc(self, n):
        o = self.off
        self.off += n
        assert self.off <= self.cols, f"arena overflow {self.off}>{self.cols}"
        return self.ap[:, o:o + n]

    def mark(self):
        self.marks.append(self.off)

    def reset(self):
        self.off = self.marks.pop()


def r3(ap, c):
    return ap.rearrange("q (a b) -> q a b", b=c)


def build_nc():
    nc = bass.Bass("TRN2", target_bir_lowering=False, debug=False)

    def din(name, shape, dt=F32):
        return nc.dram_tensor(name, list(shape), dt, kind="ExternalInput").ap()

    x_in = din("xseg", (NSEG, 128, NPT * D))
    xhalo = din("xhalo", (128, 2 * D))
    wq_d = din("wq", (D, D)); wk_d = din("wk", (D, D)); wv_d = din("wv", (D, D))
    wgq_d = din("wgq", (D, 512)); wgk_d = din("wgk", (D, 512)); wgv_d = din("wgv", (D, 512))
    ro_d = din("r_out", (512, D)); wo_d = din("w_out", (D, D))
    wgate_d = din("wgate", (D, 32))
    cn1_d = din("cn_w1", (D, D)); cn2_d = din("cn_w2", (D, 512))
    cn3_d = din("cn_w3", (512, 256)); cn4_d = din("cn_w4", (256, 1))
    b1_d = din("cn_b1", (128, 8)); b2_d = din("cn_b2", (128, 4))
    b3_d = din("cn_b3", (128, 2)); b4_d = din("cn_b4", (1, 1))
    g1_d = din("gc_w1", (D, 512)); g2_d = din("gc_w2", (512, 256)); g3_d = din("gc_w3", (256, 1))
    gb1_d = din("gc_b1", (128, 4)); gb2_d = din("gc_b2", (128, 2)); gb3_d = din("gc_b3", (1, 1))
    ar_d = din("ar", (1, 1))
    wm_d = din("winmask", (128, 1024), BF16)
    wm0_d = din("winmask0", (128, 1024), BF16)
    lm_d = din("latmask", (NSEG, 128, NPT * 512), BF16)
    idb_d = din("ident_bf", (128, 128), BF16)
    idf_d = din("ident_f32", (128, 128), F32)
    onec_d = din("ones_col", (128, 1), BF16)
    oner_d = din("ones_row", (1, 128), BF16)

    y_out = nc.dram_tensor("y", [NSEG, 128, NPT * D], F32, kind="ExternalOutput").ap()

    sc_localg = nc.dram_tensor("sc_localg", [NSEG, 128, NPT * D], BF16).ap()
    sc_gq = nc.dram_tensor("sc_gq", [NSEG, 128, 4 * 512], BF16).ap()
    sc_gr = nc.dram_tensor("sc_gr", [NSEG, 128, NPT * 16], F32).ap()
    sc_x2 = nc.dram_tensor("sc_x2", [NSEG, 128, NPT * D], F32).ap()
    sc_oa = nc.dram_tensor("sc_oa", [NSEG, 128, 8 * 512], F32).ap()
    sc_ob = nc.dram_tensor("sc_ob", [NSEG, 128, 8 * 512], F32).ap()
    cc_pi = nc.dram_tensor("cc_pi", [16, D], F32).ap()
    cc_po = nc.dram_tensor("cc_po", [64, D], F32).ap()
    cc_gi = nc.dram_tensor("cc_gi", [128, 8], F32).ap()
    cc_go = nc.dram_tensor("cc_go", [128, 8], F32).ap()

    from contextlib import ExitStack
    st = ExitStack()
    with st:
        S = Sched(nc, st)

        def sb(name, shape, dt=F32):
            return st.enter_context(nc.sbuf_tensor(name, list(shape), dt))[:]

        WA = Arena(sb("w_arena", (128, 28930), BF16), 28930)
        A32 = Arena(sb("a32", (128, 19050), F32), 19050)
        A16 = Arena(sb("a16", (128, 28830), BF16), 28830)
        cpool = sb("cpool", (128, 8))
        idbf = sb("idbf", (128, 128), BF16)
        idf32 = sb("idf32", (128, 128), F32)
        onec = sb("onec", (128, 1), BF16)
        oner = sb("oner", (1, 128), BF16)
        wmask = sb("wmask", (128, 1024), BF16)
        wmask0 = sb("wmask0", (128, 1024), BF16)
        stats = sb("stats", (128, 48))
        st2 = sb("st2", (128, 16))
        pooled = sb("pooled", (128, 160))
        gates = sb("gates", (128, NPT * 32))
        glr = sb("glr", (128, 16))
        den = sb("den", (128, 16))
        biases = sb("biases", (128, 24))
        ar_sb = sb("ar_sb", (1, 8))
        gsum = sb("gsum", (128, 8))
        mean_bf = sb("mean_bf", (128, 8), BF16)
        hg1 = sb("hg1", (128, 4), BF16)
        hg2 = sb("hg2", (128, 2), BF16)
        gk_fm = sb("gk_fm", (128, 4 * 64), BF16)
        gv_fm = sb("gv_fm", (128, 4 * 64), BF16)
        gvr = sb("gvr", (128, 8 * 512), BF16)
        pooled_fm = sb("pooled_fm", (128, 8 * 64), BF16)

        def gvr_ap(lh):
            if lh % 2 == 0:
                return gvr[0:64, 1024 * (lh // 2):1024 * (lh // 2) + 1024]
            return w_ro[0:64, 1024 * (lh // 2):1024 * (lh // 2) + 1024]

        ps2 = [st.enter_context(nc.psum_tensor(f"pp{i}", [128, 1024], F32))[:] for i in range(4)]
        ps = [ps2[i // 2][:, 512 * (i % 2):512 * (i % 2) + 512] for i in range(8)]

        w_qkv = WA.alloc(24576)
        w_gq = WA.alloc(4096)
        w_gate = WA.alloc(256)
        WA.off = 0
        w_gkv = WA.alloc(8192)
        w_ro = WA.alloc(4096)
        w_wo = WA.alloc(8192)
        WA.off = 0
        w_cn1 = WA.alloc(8192)
        w_cn2 = WA.alloc(4096)
        w_cn3 = WA.alloc(1024)
        w_cn4 = WA.alloc(2)
        w_gc1 = WA.alloc(4096)
        w_gc2 = WA.alloc(1024)
        w_gc3 = WA.alloc(2)

        dma = lambda eng, out, in_, deps=(): S.add(
            eng, lambda e, o=out, i=in_: e.dma_start(out=o, in_=i), deps)
        dma_nc = lambda eng, out, in_, deps=(): S.add(
            eng, lambda e, o=out, i=in_: e.dma_start(out=o, in_=i,
                                                     allow_slow_non_contiguous=True), deps)
        vop = lambda fn, deps=(): S.add("vector", fn, deps)
        sop = lambda fn, deps=(): S.add("scalar", fn, deps)
        top = lambda fn, deps=(): S.add("tensor", fn, deps)

        dma("sync", idbf, idb_d)
        dma("sync", idf32, idf_d)
        dma("sync", onec, onec_d)
        dma("sync", oner, oner_d)
        dma("sync", wmask, wm_d)
        dma("sync", wmask0, wm0_d)
        dma("sync", ar_sb[0:1, 0:1], ar_d)
        for c, v in enumerate((1e-5, -0.5, 0.125, -1.0, 0.5, 1.0)):
            vop(lambda e, c=c, v=v: e.memset(cpool[:, c:c + 1], v))

        def ldw(dst, src, kt, cols, deps=()):
            tks = {}
            for k in range(kt):
                tk = dma("gpsimd", dst[:, k * cols:(k + 1) * cols],
                         src[k * 128:(k + 1) * 128, :], deps)
                tks[tk[0]] = tk
            return list(tks.values())

        ldw(w_qkv[:, 0:8192], wq_d, 8, 1024)
        ldw(w_qkv[:, 8192:16384], wk_d, 8, 1024)
        ldw(w_qkv[:, 16384:24576], wv_d, 8, 1024)
        ldw(w_gq, wgq_d, 8, 512)
        ldw(w_gate, wgate_d, 8, 32)
        dma("sync", biases[:, 0:8], b1_d)
        dma("sync", biases[:, 8:12], b2_d)
        dma("sync", biases[:, 12:14], b3_d)
        dma("sync", biases[:, 14:18], gb1_d)
        dma("sync", biases[:, 18:20], gb2_d)
        tb = dma("sync", biases[0:1, 20:21], b4_d)
        tb2 = dma("sync", biases[0:1, 21:22], gb3_d)
        vop(lambda e: e.tensor_scalar(out=biases[0:1, 22:23], in0=biases[0:1, 20:21],
                                      scalar1=0.5, scalar2=None, op0=ALU.mult), [tb])
        vop(lambda e: e.tensor_scalar(out=biases[0:1, 23:24], in0=biases[0:1, 21:22],
                                      scalar1=-1.0, scalar2=None, op0=ALU.mult), [tb2])

        A32.mark(); A16.mark()
        x_ring = [A32.alloc(NPT * D) for _ in range(2)]
        graw = A32.alloc(512)
        pooled_t = A32.alloc(128)
        ln1t = A16.alloc(NPT * D)
        q_fm = A16.alloc(8 * 512)
        ln1f = A16.alloc(8 * 512)
        gq_sb = A16.alloc(4 * 512)
        k_ring = [A16.alloc(8 * 256) for _ in range(3)]
        v_ring = [A16.alloc(16 * 65) for _ in range(6)]
        p_t = [A32.alloc(512).bitcast(BF16) for _ in range(4)]
        localg = A32.alloc(NPT * D // 2).bitcast(BF16)

        vr1 = None
        for sl in range(6):
            vr1 = vop(lambda e, sl=sl: e.memset(r3(v_ring[sl], 65)[:, :, 64:65], 1.0))
        ph0 = S.barrier() + S.dma_fence()
        dep_ph1 = ph0 + S.fence(("vv",)) + S.dma_fence()

        bank_free = {i: None for i in range(8)}
        x_free = [None, None]
        pt_free = [None, None, None, None]
        kv_tok = {}
        kv_ktok = {}
        kslot_free = {}
        vslot_free = {}
        ln_gate = [None]
        gq_free = None
        lg_free = None

        def layernorm(xap_fn, dst_fn, npt, deps, apply_deps=(), apply_eng="vector"):
            t1 = None
            dd = list(deps) + [ln_gate[0]]
            for p in range(npt):
                lo, hi = xap_fn(p)
                t1 = vop(lambda e, p=p, lo=lo: e.bn_stats(stats[:, 12 * p:12 * p + 6], lo), dd)
                t1 = vop(lambda e, p=p, hi=hi: e.bn_stats(stats[:, 12 * p + 6:12 * p + 12], hi), dd)
            ta = None
            for p in range(npt):
                ta = vop(lambda e, p=p: e.bn_aggr(st2[:, 2 * p:2 * p + 2],
                                                  stats[:, 12 * p:12 * p + 12]), [t1])
            va = st2[:, 1:2 * npt:2]
            tl = sop(lambda e, va=va: e.activation(out=va, in_=va, func=AF.Ln,
                                                   bias=cpool[:, 0:1]), [ta])
            tl = sop(lambda e, va=va: e.activation(out=va, in_=va, func=AF.Exp,
                                                   scale=cpool[:, 1:2]), [tl])
            tnb = vop(lambda e, va=va, npt=npt: e.scalar_tensor_tensor(
                out=st2[:, 8:8 + npt], in0=st2[:, 0:2 * npt:2], scalar=-1.0, in1=va,
                op0=ALU.mult, op1=ALU.mult), [tl])
            outs = []
            ad = [tnb] + list(apply_deps)
            for p in range(npt):
                lo, hi = xap_fn(p)
                dst = dst_fn(p)
                if apply_eng == "scalar":
                    outs.append(sop(lambda e, p=p, lo=lo, dst=dst: e.activation(
                        out=dst[:, 0:512], in_=lo, func=AF.Identity,
                        scale=st2[:, 2 * p + 1:2 * p + 2],
                        bias=st2[:, 8 + p:9 + p]), ad))
                    outs.append(sop(lambda e, p=p, hi=hi, dst=dst: e.activation(
                        out=dst[:, 512:1024], in_=hi, func=AF.Identity,
                        scale=st2[:, 2 * p + 1:2 * p + 2],
                        bias=st2[:, 8 + p:9 + p]), ad))
                    continue
                outs.append(vop(lambda e, p=p, lo=lo, dst=dst: e.tensor_scalar(
                    out=dst[:, 0:512], in0=lo, scalar1=st2[:, 2 * p + 1:2 * p + 2],
                    scalar2=st2[:, 8 + p:9 + p], op0=ALU.mult, op1=ALU.add), ad))
                outs.append(vop(lambda e, p=p, hi=hi, dst=dst: e.tensor_scalar(
                    out=dst[:, 512:1024], in0=hi, scalar1=st2[:, 2 * p + 1:2 * p + 2],
                    scalar2=st2[:, 8 + p:9 + p], op0=ALU.mult, op1=ALU.add), ad))
            ln_gate[0] = outs[-1]
            return outs

        def transpose_tok2fm(src_tok, npt, dst_fm, deps, fp32=False, pool=False,
                             gsum_first=None):
            ident = idf32 if fp32 else idbf
            ev = None
            for p in range(npt):
                for half in range(2):
                    bk = half
                    def mm(e, p=p, half=half, bk=bk, first=(gsum_first[0] if gsum_first else False)):
                        r = None
                        for gg in range(4):
                            g = half * 4 + gg
                            r = e.matmul(ps[bk][:, gg * 128:(gg + 1) * 128],
                                         src_tok[:, p * 1024 + g * 128:p * 1024 + (g + 1) * 128],
                                         ident, start=True, stop=True)
                            if pool:
                                r = e.matmul(ps[2][:, (g * 4 + p):(g * 4 + p + 1)],
                                             src_tok[:, p * 1024 + g * 128:p * 1024 + (g + 1) * 128],
                                             onec, start=True, stop=True)
                            if gsum_first is not None:
                                r = e.matmul(ps[7][:, g:g + 1],
                                             src_tok[:, p * 1024 + g * 128:p * 1024 + (g + 1) * 128],
                                             cpool[:, 5:6], start=first, stop=False,
                                             skip_group_check=True)
                        return r
                    tm = top(mm, list(deps) + [bank_free[bk]] +
                             ([bank_free[2]] if pool else []))
                    if gsum_first is not None:
                        gsum_first[0] = False
                    ev = sop(lambda e, p=p, half=half, bk=bk: e.copy(
                        out=r3(dst_fm, 512)[:, half * 4:(half + 1) * 4, p * 128:(p + 1) * 128],
                        in_=r3(ps[bk], 128)), [tm])
                    bank_free[bk] = ev
            return ev

        def kv_block(blk, tcol, dep_lnf):
            ks = (blk + 1) % 3
            ev = None
            k_ev = None
            for i in range(8):
                bk = 3 + (i % 2)
                def mmk(e, i=i, bk=bk, tcol=tcol):
                    r = None
                    for k in range(8):
                        r = e.matmul(ps[bk][:, 0:256],
                                     w_qkv[:, 8192 + k * 1024 + i * 128:8192 + k * 1024 + (i + 1) * 128],
                                     ln1f[:, k * 512 + tcol:k * 512 + tcol + 256],
                                     start=(k == 0), stop=(k == 7))
                    return r
                tm = top(mmk, [dep_lnf, bank_free[bk], kslot_free.get(ks)])
                ev = sop(lambda e, i=i, bk=bk, ks=ks: e.copy(
                    out=k_ring[ks][:, i * 256:(i + 1) * 256], in_=ps[bk][:, 0:256]), [tm])
                bank_free[bk] = ev
                k_ev = ev
            for pp in range(2):
                vs = ks * 2 + pp
                for jh in range(2):
                    bk = 5 + jh
                    def mmv(e, pp=pp, jh=jh, tcol=tcol, bk=bk):
                        r = None
                        for k in range(8):
                            r = e.matmul(ps[bk][:, 0:512],
                                         ln1f[:, k * 512 + tcol + pp * 128:k * 512 + tcol + pp * 128 + 128],
                                         w_qkv[:, 16384 + k * 1024 + jh * 512:16384 + k * 1024 + (jh + 1) * 512],
                                         start=(k == 0), stop=(k == 7))
                        return r
                    tm = top(mmv, [dep_lnf, bank_free[bk], vslot_free.get(vs)])
                    ev = sop(lambda e, vs=vs, jh=jh, bk=bk: e.copy(
                        out=r3(v_ring[vs], 65)[:, jh * 8:(jh + 1) * 8, 0:64],
                        in_=r3(ps[bk], 64)), [tm])
                    bank_free[bk] = ev
            kv_tok[blk] = ev
            kv_ktok[blk] = k_ev
            return ev

        def proj(dst, wcols, n_i, stride, jt, deps, banks=(3, 4)):
            evs = []
            ev = None
            for i in range(n_i):
                bk = banks[i % 2]
                def mm(e, i=i, bk=bk):
                    r = None
                    for k in range(8):
                        r = e.matmul(ps[bk][:, 0:jt],
                                     wcols[:, k * stride + i * 128:k * stride + (i + 1) * 128],
                                     ln1f[:, k * 512:k * 512 + jt],
                                     start=(k == 0), stop=(k == 7))
                    return r
                tm = top(mm, list(deps) + [bank_free[bk]])
                ev = sop(lambda e, i=i, bk=bk, jt=jt: e.copy(
                    out=dst[:, i * jt:(i + 1) * jt], in_=ps[bk][:, 0:jt]), [tm])
                bank_free[bk] = ev
                evs.append(ev)
            return evs

        for _b in range(4):
            bank_free[_b] = vop(lambda e, _b=_b: e.memset(ps[_b][:, 0:512], 0.0), ph0)
        t_xh = dma("sync", x_ring[1][:, 0:2 * D], xhalo, dep_ph1)
        ln_h = layernorm(lambda p: (x_ring[1][:, p * D:p * D + 512],
                                    x_ring[1][:, p * D + 512:(p + 1) * D]),
                         lambda p: ln1t[:, p * D:(p + 1) * D], 2, [t_xh] + dep_ph1)
        ev_trh = transpose_tok2fm(ln1t, 2, ln1f, [ln_h[-1]] + dep_ph1)
        t_kvh = kv_block(-1, 0, ev_trh)
        x_free[1] = ln_h[-1]

        lg_spill = [None] * NSEG
        gq_spill = [None] * NSEG
        gr_spill = [None] * NSEG
        glr_war = {}
        ln1f_last_reader = t_kvh
        tp = None
        ln_pipe = {}

        def emit_ln1(s, ln1t_free):
            xr = s % 2
            t_x = dma("sync", x_ring[xr], x_in[s], [x_free[xr]] + dep_ph1)
            ln_a = layernorm(lambda p, xr=xr: (x_ring[xr][:, p * D:p * D + 512],
                                               x_ring[xr][:, p * D + 512:(p + 1) * D]),
                             lambda p: ln1t[:, p * D:(p + 1) * D], NPT, [t_x],
                             apply_deps=[ln1t_free])
            x_free[xr] = ln_a[-1]
            ln_pipe[s] = ln_a

        emit_ln1(0, ev_trh)
        for s in range(NSEG):
            xr = s % 2
            ln_a = ln_pipe.pop(s)
            ev_tr = transpose_tok2fm(ln1t, NPT, ln1f,
                                     [ln_a[-1], ln1f_last_reader])
            for cl in range(2):
                tp = vop(lambda e, cl=cl, s=s: e.reduce_sum(
                    pooled[:, 8 * (2 * s + cl):8 * (2 * s + cl) + 8],
                    r3(ln1f, 512)[:, :, 256 * cl:256 * cl + 256],
                    axis=mybir.AxisListType.X), [ev_tr])
            if s == NSEG - 1:
                tps = vop(lambda e: e.tensor_scalar(out=pooled[:, 0:128], in0=pooled[:, 0:128],
                                                    scalar1=1.0 / 256, scalar2=None, op0=ALU.mult),
                          [tp])
                tpt0 = top(lambda e: e.matmul(ps[3][:, 0:128], pooled[:, 0:128], idf32,
                                              start=True, stop=True),
                           [tps, bank_free[3]])
                tpt0 = vop(lambda e: e.tensor_copy(out=pooled_t, in_=ps[3][:, 0:128]), [tpt0])
                bank_free[3] = tpt0
                t_pool_out = dma("gpsimd", cc_pi, pooled_t, [tpt0])
                t_ag = S.add("gpsimd", lambda e: e.collective_compute(
                    "AllGather", ALU.bypass, replica_groups=[[0, 1, 2, 3], [4, 5, 6, 7]],
                    ins=[cc_pi], outs=[cc_po]), [t_pool_out], semname="cc", inc=1)
            evq_l = proj(q_fm[:, 0:8 * 512], w_qkv[:, 0:8192], 8, 1024, 512, [ev_tr])
            ev_q = evq_l[-1]
            ev_gq = proj(gq_sb, w_gq, 4, 512, 512, [ev_tr, gq_free])[-1]
            gq_spill[s] = dma("gpsimd", sc_gq[s], gq_sb, [ev_gq])
            gq_free = gq_spill[s]
            def mmg(e):
                r = None
                for k in range(8):
                    r = e.matmul(ps[7][0:32, 0:512], w_gate[:, k * 32:(k + 1) * 32],
                                 q_fm[:, k * 512:(k + 1) * 512], start=(k == 0), stop=(k == 7))
                return r
            tg = top(mmg, [ev_q, bank_free[7]])
            tg = vop(lambda e: e.tensor_copy(out=graw[0:32, :], in_=ps[7][0:32, 0:512]), [tg])
            bank_free[7] = tg
            def mmgt(e):
                r = None
                for p in range(NPT):
                    r = e.matmul(ps[7][:, p * 32:(p + 1) * 32], graw[0:32, p * 128:(p + 1) * 128],
                                 idf32[0:32, 0:32], start=True, stop=True)
                return r
            tgt = top(mmgt, [tg, bank_free[7]])
            tge = sop(lambda e: e.activation(out=gates, in_=ps[7][:, 0:128], func=AF.Exp,
                                             scale=cpool[:, 3:4]),
                      [tgt, gr_spill[s - 1] if s > 0 else None])
            tge = vop(lambda e: e.tensor_scalar(out=gates, in0=gates, scalar1=1.0,
                                                scalar2=None, op0=ALU.add), [tge])
            tge = vop(lambda e: e.reciprocal(out=gates, in_=gates), [tge])
            bank_free[7] = tge
            gr_spill[s] = dma("gpsimd", sc_gr[s], r3(gates, 32)[:, :, 16:32], [tge])
            kv_block(2 * s, 0, ev_tr)
            kv_block(2 * s + 1, 256, ev_tr)
            ln1f_last_reader = S.fence(("tt",))[0]
            if s + 1 < NSEG:
                emit_ln1(s + 1, ev_tr)

            for lb in range(2):
                blk = 2 * s + lb
                mask = wmask0 if blk == 0 else wmask
                last_tsc = None
                pv_last = {}

                def gate_grp(hgrp, p_idx_base):
                    nh = (7, 7, 2)[hgrp]
                    hoff = 7 * hgrp
                    for qc in range(2):
                        p_idx = p_idx_base + qc
                        bk = (4 if qc == 0 else 6) + (hgrp & 1)
                        tr_ = vop(lambda e, bk=bk, nh=nh, hoff=hoff: e.reciprocal(
                            out=glr[:, hoff:hoff + nh],
                            in_=r3(ps[bk][:, 0:nh * 65], 65)[:, 0:nh, 64:65].rearrange(
                                "q a b -> q (a b)")), [pv_last[bk], glr_war.get(hgrp)])
                        tr_ = vop(lambda e, nh=nh, hoff=hoff, p_idx=p_idx: e.tensor_mul(
                            out=glr[:, hoff:hoff + nh], in0=glr[:, hoff:hoff + nh],
                            in1=gates[:, 32 * p_idx + hoff:32 * p_idx + hoff + nh]), [tr_, tge])
                        tr_ = vop(lambda e, bk=bk, nh=nh, hoff=hoff, p_idx=p_idx: e.tensor_tensor(
                            out=r3(localg[:, p_idx * D:(p_idx + 1) * D], 64)[:, hoff:hoff + nh, :],
                            in0=r3(ps[bk][:, 0:nh * 65], 65)[:, 0:nh, 0:64],
                            in1=glr[:, hoff:hoff + nh][:, :, None].broadcast_to([128, nh, 64]),
                            op=ALU.mult), [tr_, lg_free])
                        glr_war[hgrp] = tr_
                        bank_free[bk] = tr_

                def emit_sc(h):
                    ptb = p_t[h % 4]
                    sb0, sb1 = (0, 1) if h % 2 == 0 else (2, 3)
                    def mms(e, h=h, blk=blk, lb=lb, sb0=sb0, sb1=sb1):
                        r = None
                        for jc in range(4):
                            kslot = ((blk - 1 if jc < 2 else blk) + 1) % 3
                            kc = 256 * (h // 2) + 128 * (jc % 2)
                            i0_ = 128 if jc == 3 else 0
                            i1_ = 128 if jc == 0 else 256
                            bnk = sb0 if jc < 2 else sb1
                            r = e.matmul(ps[bnk][:, 256 * (jc % 2) + i0_:256 * (jc % 2) + i1_],
                                         k_ring[kslot][64 * (h % 2):64 * (h % 2) + 64, kc:kc + 128],
                                         q_fm[64 * (h % 2):64 * (h % 2) + 64,
                                              512 * (h // 2) + 256 * lb + i0_:512 * (h // 2) + 256 * lb + i1_],
                                         start=True, stop=True)
                        return r
                    tsc = top(mms, [kv_ktok[blk - 1], kv_ktok[blk], evq_l[h // 2],
                                    bank_free[sb0], bank_free[sb1]])
                    te = sop(lambda e, ptb=ptb, sb0=sb0: e.activation(
                        out=ptb, in_=ps2[sb0 // 2], func=AF.Exp, scale=cpool[:, 2:3]),
                        [tsc, pt_free[h % 4]])
                    bank_free[sb0] = te
                    bank_free[sb1] = te
                    tmk = vop(lambda e, ptb=ptb, mask=mask: e.tensor_mul(
                        out=ptb, in0=ptb, in1=mask), [te])
                    hstate[h] = (tmk, tsc)

                def emit_pv(h):
                    ptb = p_t[h % 4]
                    tmk, _ = hstate[h]
                    hgrp = min(h // 7, 2)
                    hoff = h - 7 * hgrp
                    for qc in range(2):
                        bk = (4 if qc == 0 else 6) + (hgrp & 1)
                        jcs = (0, 1, 2) if qc == 0 else (1, 2, 3)
                        def mmpv(e, h=h, qc=qc, bk=bk, hoff=hoff, blk=blk, ptb=ptb, jcs=jcs):
                            r = None
                            for ji, jc in enumerate(jcs):
                                vs = (((blk - 1 if jc < 2 else blk) + 1) % 3) * 2 + (jc % 2)
                                r = e.matmul(ps[bk][:, hoff * 65:hoff * 65 + 65],
                                             ptb[:, 256 * jc + 128 * qc:256 * jc + 128 * qc + 128],
                                             v_ring[vs][:, 65 * h:65 * h + 65],
                                             start=(ji == 0), stop=(ji == 2))
                            return r
                        tpv = top(mmpv, [tmk, bank_free[bk], kv_tok[blk - 1], kv_tok[blk]])
                        pv_last[bk] = tpv
                    pt_free[h % 4] = pv_last[6 + (hgrp & 1)]
                    if h == 6:
                        gate_grp(0, 2 * lb)
                    elif h == 13:
                        gate_grp(1, 2 * lb)
                    elif h == 15:
                        gate_grp(2, 2 * lb)

                hstate = {}
                for h in range(NH):
                    emit_sc(h)
                    if h >= 2:
                        emit_pv(h - 2)
                emit_pv(NH - 2)
                emit_pv(NH - 1)
                last_tsc = hstate[NH - 1][1]
                kslot_free[blk % 3] = last_tsc
                vslot_free[(blk % 3) * 2] = pv_last[6]
                vslot_free[(blk % 3) * 2 + 1] = pv_last[6]
            lg_spill[s] = dma("gpsimd", sc_localg[s], localg,
                              [bank_free[4], bank_free[5], bank_free[6], bank_free[7]])
            lg_free = lg_spill[s]

        def finish_debug():
            tk = None
            for s_ in range(NSEG):
                tk = dma("gpsimd", y_out[s_], x_ring[0])
            S.add("sync", None, S.dma_fence(), semname="fin", inc=1)
            with nc.Block() as block:
                S.emit(block)

        ph1 = S.fence()
        if DEBUG_PHASES <= 1:
            finish_debug()
            return nc

        t_wgk = ldw(w_gkv[:, 0:4096], wgk_d, 8, 512, ph1)
        t_wgv = ldw(w_gkv[:, 4096:8192], wgv_d, 8, 512, ph1)
        t_wro = ldw(w_ro, ro_d, 4, 1024, ph1)
        t_wwo = ldw(w_wo, wo_d, 8, 1024, ph1)

        A32.reset(); A16.reset()
        A32.mark(); A16.mark()
        pooled_tk = A32.alloc(1024)
        t_ptk = dma("sync", pooled_tk[0:64, :], cc_po, [t_ag] + ph1)
        ev_pt = None
        for g in range(8):
            bk = g % 2
            tm = top(lambda e, g=g, bk=bk: e.matmul(
                ps[bk][:, 0:64], pooled_tk[0:64, g * 128:(g + 1) * 128],
                idf32[0:64, 0:64], start=True, stop=True),
                [t_ptk, bank_free[bk]] + ph1)
            ev_pt = vop(lambda e, g=g, bk=bk: e.tensor_copy(
                out=pooled_fm[:, g * 64:(g + 1) * 64], in_=ps[bk][:, 0:64]), [tm])
            bank_free[bk] = ev_pt

        def gkgv(dst, woff, depw):
            ev = None
            for i in range(4):
                bk = 2 + (i % 2)
                def mm(e, i=i, bk=bk, woff=woff):
                    r = None
                    for k in range(8):
                        r = e.matmul(ps[bk][:, 0:64],
                                     w_gkv[:, woff + k * 512 + i * 128:woff + k * 512 + (i + 1) * 128],
                                     pooled_fm[:, k * 64:(k + 1) * 64],
                                     start=(k == 0), stop=(k == 7))
                    return r
                tm = top(mm, [ev_pt, bank_free[bk]] + list(depw))
                ev = vop(lambda e, i=i, bk=bk, dst=dst: e.tensor_copy(
                    out=dst[:, i * 64:(i + 1) * 64], in_=ps[bk][:, 0:64]), [tm])
                bank_free[bk] = ev
            return ev
        ev_gk = gkgv(gk_fm, 0, t_wgk)
        ev_gv = gkgv(gv_fm, 4096, t_wgv)
        ev_gvr = None
        for lh in range(8):
            for jh in range(2):
                bk = 4 + 2 * (lh % 2) + jh
                def mm(e, lh=lh, jh=jh, bk=bk):
                    return e.matmul(ps[bk][0:64, 0:512],
                                    gv_fm[64 * (lh % 2):64 * (lh % 2) + 64,
                                          64 * (lh // 2):64 * (lh // 2) + 64],
                                    w_ro[64 * (lh % 2):64 * (lh % 2) + 64,
                                         (lh // 2) * 1024 + jh * 512:(lh // 2) * 1024 + (jh + 1) * 512],
                                    start=True, stop=True)
                tm = top(mm, [ev_gv, bank_free[bk]] + t_wro)
                ev_gvr = vop(lambda e, lh=lh, jh=jh, bk=bk: e.tensor_copy(
                    out=gvr_ap(lh)[:, jh * 512:(jh + 1) * 512], in_=ps[bk][0:64, 0:512]), [tm])
                bank_free[bk] = ev_gvr
        ph15 = S.fence()
        if DEBUG_PHASES <= 2:
            finish_debug()
            return nc

        A32.reset(); A16.reset()
        A32.mark(); A16.mark()
        xs = A32.alloc(NPT * D)
        ln3 = A32.alloc(NPT * D)
        o0t = ln3
        o0f = A32.alloc(8 * 512)
        remg_r = [A32.alloc(D) for _ in range(2)]
        lmask = A16.alloc(NPT * 512)
        plat_r = [A16.alloc(512) for _ in range(2)]
        platT = A16.alloc(8 * 512)
        lgseg = A16.alloc(NPT * D)
        comb_r = [A16.alloc(D) for _ in range(2)]
        comb_f = A16.alloc(8 * 256)
        gq2r = [A16.alloc(4 * 512) for _ in range(2)]
        attn_r = [A16.alloc(NPT * D) for _ in range(2)]

        gsum_tok = [None]
        x2_spill = [None] * NSEG
        o0_spill = [None] * NSEG
        pv = {"gq2": [None, None], "lg": None, "plat": [None, None],
              "remg": [None, None], "comb": [None, None],
              "o0f": None, "o0t": None, "ln3": None, "x2": None, "gr": None,
              "lm": None, "attn": [None, None], "xs": None, "den": [None, None]}
        AB = {}

        def ph2_A(s):
            xr = s % 2
            gq2 = gq2r[s % 2]
            t_lm = dma("sync", lmask, lm_d[s], [pv["lm"]] + ph15)
            t_gq = dma("sync", gq2, sc_gq[s], [gq_spill[s], pv["gq2"][s % 2]] + ph15)
            t_lg = dma("sync", lgseg, sc_localg[s], [lg_spill[s], pv["lg"]] + ph15)
            t_gr = dma("sync", gates[:, 0:64], sc_gr[s], [gr_spill[s], pv["gr"]] + ph15)
            st = [dict() for _ in range(NPT)]

            def st0(p):
                pr = p % 2
                plat = plat_r[pr]
                dn = den[:, 8 * pr:8 * pr + 8]
                def mml(e, p=p):
                    r = None
                    for lh in range(8):
                        bnk = 0 if lh % 2 == 0 else 1
                        r = e.matmul(ps[bnk][:, 64 * (lh // 2):64 * (lh // 2) + 64],
                                     gq2[64 * (lh % 2):64 * (lh % 2) + 64,
                                         512 * (lh // 2) + 128 * p:512 * (lh // 2) + 128 * p + 128],
                                     gk_fm[64 * (lh % 2):64 * (lh % 2) + 64,
                                           64 * (lh // 2):64 * (lh // 2) + 64],
                                     start=True, stop=True)
                    return r
                tl = top(mml, [t_gq, ev_gk, bank_free[0], bank_free[1]])
                te = sop(lambda e, plat=plat: e.activation(
                    out=r3(plat, 64)[:, 0::2, :], in_=r3(ps[0][:, 0:256], 64),
                    func=AF.Exp, scale=cpool[:, 2:3]), [tl, pv["plat"][pr]])
                te = sop(lambda e, plat=plat: e.activation(
                    out=r3(plat, 64)[:, 1::2, :], in_=r3(ps[1][:, 0:256], 64),
                    func=AF.Exp, scale=cpool[:, 2:3]), [tl])
                bank_free[0] = te
                bank_free[1] = te
                tm_ = vop(lambda e, p=p, plat=plat: e.tensor_mul(
                    out=plat, in0=plat, in1=lmask[:, p * 512:(p + 1) * 512]), [te, t_lm])
                st[p]["lm"] = tm_
                td = vop(lambda e, plat=plat, dn=dn: e.reduce_sum(
                    dn, r3(plat, 64), axis=mybir.AxisListType.X), [tm_, pv["den"][pr]])
                td = vop(lambda e, dn=dn: e.reciprocal(out=dn, in_=dn), [td])
                tn = vop(lambda e, plat=plat, dn=dn: e.tensor_tensor(
                    out=r3(plat, 64), in0=r3(plat, 64),
                    in1=dn[:, :, None].broadcast_to([128, 8, 64]), op=ALU.mult), [td])
                pv["plat"][pr] = tn
                pv["den"][pr] = tn
                st[p]["tn"] = tn

            def st1(p):
                pr = p % 2
                plat = plat_r[pr]
                tn = st[p]["tn"]
                tts = {}
                for lh in range(8):
                    bk = 2 + (lh % 2)
                    tts[bk] = top(lambda e, lh=lh, bk=bk, plat=plat, p=p: e.matmul(
                        ps[bk][0:64, 128 * (lh // 2):128 * (lh // 2) + 128],
                        plat[:, 64 * lh:64 * lh + 64], idbf, start=True, stop=True),
                        [tn, bank_free[bk]])
                tev = None
                for bk in (2, 3):
                    tev = sop(lambda e, bk=bk, p=p: e.copy(
                        out=r3(platT, 512)[0:64, (bk - 2)::2, 128 * p:128 * p + 128],
                        in_=r3(ps[bk][0:64, 0:512], 128)), [tts[2], tts[3]])
                    bank_free[bk] = tev
                st[p]["tev"] = tev

            def st2(p):
                pr = p % 2
                remg = remg_r[pr]
                def mmr(e, p=p):
                    r = None
                    for lh in range(8):
                        for jh in range(2):
                            r = e.matmul(ps[4 + jh][:, 0:512],
                                         platT[0:64, 512 * lh + 128 * p:512 * lh + 128 * p + 128],
                                         gvr_ap(lh)[:, jh * 512:(jh + 1) * 512],
                                         start=(lh == 0), stop=(lh == 7))
                    return r
                trm = top(mmr, [st[p]["tev"], ev_gvr, bank_free[4], bank_free[5]])
                gpair = gates[:, 16 * p:16 * p + 16][:, :, None].broadcast_to([128, 16, 64])
                tg1 = vop(lambda e, remg=remg, gpair=gpair: e.tensor_tensor(
                    out=r3(remg, 64), in0=r3(ps2[2], 64), in1=gpair, op=ALU.mult),
                    [trm, t_gr, pv["remg"][pr]])
                bank_free[4] = tg1
                bank_free[5] = tg1
                st[p]["tg1"] = tg1
                st[p]["trm"] = trm

            def st3(p):
                pr = p % 2
                remg = remg_r[pr]
                comb = comb_r[pr]
                tcb = vop(lambda e, p=p, remg=remg, comb=comb: e.tensor_add(
                    out=comb, in0=remg, in1=lgseg[:, p * D:(p + 1) * D]),
                    [st[p]["tg1"], t_lg, pv["comb"][pr]])
                pv["remg"][pr] = tcb
                tev2 = None
                for half in range(2):
                    bk = 2 + half
                    def mmt(e, half=half, bk=bk, comb=comb):
                        r = None
                        for gg in range(4):
                            g = half * 4 + gg
                            r = e.matmul(ps[bk][:, gg * 128:(gg + 1) * 128],
                                         comb[:, g * 128:(g + 1) * 128],
                                         idbf, start=True, stop=True)
                        return r
                    tmt = top(mmt, [tcb, bank_free[bk]])
                    tev2 = sop(lambda e, p=p, half=half, bk=bk: e.copy(
                        out=r3(comb_f, 256)[:, half * 4:(half + 1) * 4,
                                            (p % 2) * 128:(p % 2) * 128 + 128],
                        in_=r3(ps[bk], 128)), [tmt])
                    bank_free[bk] = tev2
                pv["comb"][pr] = tev2
                st[p]["tev2"] = tev2

            def st4(p):
                def mmw(e, p=p):
                    r = None
                    for jh in range(2):
                        for k in range(8):
                            r = e.matmul(ps[6 + jh][:, 0:512],
                                         comb_f[:, k * 256 + 128 * (p % 2):
                                                k * 256 + 128 * (p % 2) + 128],
                                         w_wo[:, k * 1024 + jh * 512:k * 1024 + (jh + 1) * 512],
                                         start=(k == 0), stop=(k == 7))
                    return r
                tw = top(mmw, [st[p]["tev2"], bank_free[6], bank_free[7]] + t_wwo)
                ta = None
                for jh in range(2):
                    ta = sop(lambda e, p=p, jh=jh, xr=xr: e.copy(
                        out=attn_r[xr][:, p * D + jh * 512:p * D + jh * 512 + 512],
                        in_=ps[6 + jh]), [tw, pv["attn"][xr]])
                    bank_free[6 + jh] = ta
                st[p]["ta"] = ta
                st[p]["tw"] = tw

            stages = (st0, st1, st2, st3, st4)
            for k in range(len(stages) + NPT - 1):
                for p in range(NPT):
                    kk = k - p
                    if 0 <= kk < len(stages):
                        stages[kk](p)
            AB[s] = (st[NPT - 1]["ta"], st[NPT - 1]["lm"],
                     st[NPT - 1]["trm"], st[NPT - 1]["tw"])

        def ph2_B1(s):
            xr = s % 2
            ta_done = AB.pop(s)[0]
            t_xs = dma("sync", xs, x_in[s], [pv["xs"], pv["x2"]] + ph15)
            lnt = layernorm(lambda p, xr=xr: (attn_r[xr][:, p * D:p * D + 512],
                                              attn_r[xr][:, p * D + 512:(p + 1) * D]),
                            lambda p: ln3[:, p * D:(p + 1) * D], NPT, [ta_done],
                            apply_deps=[pv["ln3"]])
            pv["attn"][xr] = lnt[-1]
            tx2 = None
            for p in range(NPT):
                tx2 = vop(lambda e, p=p: e.tensor_add(
                    out=xs[:, p * D:(p + 1) * D], in0=ln3[:, p * D:(p + 1) * D],
                    in1=xs[:, p * D:(p + 1) * D]), [lnt[2 * p + 1], t_xs])
            pv["ln3"] = tx2
            ln2t = layernorm(lambda p: (xs[:, p * D:p * D + 512],
                                        xs[:, p * D + 512:(p + 1) * D]),
                             lambda p: o0t[:, p * D:(p + 1) * D], NPT, [tx2],
                             apply_deps=[pv["o0t"]])
            x2_spill[s] = dma("sync", sc_x2[s], xs, [tx2, S.fence(("vv",))[0]])
            pv["x2"] = x2_spill[s]
            pv["xs"] = ln2t[-1]
            B1T[s] = ln2t

        def ph2_B2(s):
            ln2t = B1T.pop(s)
            tev3 = None
            tmo = None
            for p in range(NPT):
                for half in range(2):
                    obk = 2 + half
                    def mmo(e, p=p, half=half, obk=obk):
                        r = None
                        for gg in range(4):
                            g = half * 4 + gg
                            r = e.matmul(ps[obk][:, gg * 128:(gg + 1) * 128],
                                         o0t[:, p * D + g * 128:p * D + (g + 1) * 128],
                                         idf32, start=True, stop=True)
                        return r
                    tmo = top(mmo, [ln2t[2 * p + 1], bank_free[obk]])
                    tev3 = sop(lambda e, p=p, half=half, obk=obk: e.copy(
                        out=r3(o0f, 512)[:, half * 4:(half + 1) * 4, p * 128:(p + 1) * 128],
                        in_=r3(ps[obk], 128)), [tmo, pv["o0f"]])
                    bank_free[obk] = tev3
            pv["o0t"] = tmo
            pv["o0f"] = tev3
            tgp = vop(lambda e: e.reduce_sum(pooled[:, 0:8], r3(o0f, 512),
                                             axis=mybir.AxisListType.X),
                      [pv["o0f"], gsum_tok[0]])
            if s == 0:
                tgp = vop(lambda e: e.tensor_copy(out=gsum, in_=pooled[:, 0:8]), [tgp])
            else:
                tgp = vop(lambda e: e.tensor_add(out=gsum, in0=gsum, in1=pooled[:, 0:8]),
                          [tgp, gsum_tok[0]])
            gsum_tok[0] = tgp
            o0_spill[s] = dma("gpsimd", sc_oa[s], o0f, [pv["o0f"]])
            pv["o0f"] = o0_spill[s]

        B1T = {}
        w3toks = []
        AB7 = [None]
        for s in range(NSEG):
            if s >= 2:
                ph2_B2(s - 2)
            ph2_A(s)
            pv["gq2"][s % 2] = S.fence(("tt",))[0]
            pv["lg"] = S.fence(("vv",))[0]
            pv["gr"] = pv["lg"]
            pv["lm"] = AB[s][1]
            if s == NSEG - 1:
                w3toks = [AB[s][2], AB[s][3]]
                AB7[0] = AB[s][1]
            if s >= 1:
                ph2_B1(s - 1)

        def emit_b_tail():
            ph2_B2(NSEG - 2)
            ph2_B1(NSEG - 1)
            ph2_B2(NSEG - 1)

        ph2 = S.fence()

        t_cn1 = ldw(w_cn1, cn1_d, 8, 1024, w3toks)
        t_cn2 = ldw(w_cn2, cn2_d, 8, 512, w3toks)
        t_cn3 = ldw(w_cn3, cn3_d, 4, 256, w3toks)
        t_cn4 = ldw(w_cn4, cn4_d, 2, 1, w3toks)
        t_gc1 = ldw(w_gc1, g1_d, 8, 512, w3toks)
        t_gc2 = ldw(w_gc2, g2_d, 4, 256, w3toks)
        t_gc3 = ldw(w_gc3, g3_d, 2, 1, w3toks)

        ph25_tok = {}

        def emit_ar():
            t_gso = dma("sync", cc_gi, gsum, [gsum_tok[0]])
            if "noar" in DEBUG_CUT:
                t_arr = dma("gpsimd", cc_go, cc_gi, [t_gso])
            else:
                t_arr = S.add("gpsimd", lambda e: e.collective_compute(
                    "AllReduce", ALU.add, replica_groups=[[0, 1, 2, 3], [4, 5, 6, 7]],
                    ins=[cc_gi], outs=[cc_go]), [t_gso], semname="cc", inc=1)
            ph25_tok["gsi"] = dma("sync", gsum, cc_go, [t_arr, gsum_tok[0]])

        def emit_ph25():
            tmn = vop(lambda e: e.tensor_scalar(out=gsum, in0=gsum, scalar1=1.0 / 16384,
                                                scalar2=None, op0=ALU.mult),
                      [ph25_tok["gsi"]] + ph2)
            tmn = vop(lambda e: e.tensor_copy(out=mean_bf, in_=gsum), [tmn])
            th = None
            for i in range(4):
                th = top(lambda e, i=i: [e.matmul(
                    ps[0][:, i:i + 1], w_gc1[:, k * 512 + i * 128:k * 512 + (i + 1) * 128],
                    mean_bf[:, k:k + 1], start=(k == 0), stop=(k == 7)) for k in range(8)][-1],
                    [tmn, bank_free[0]] + t_gc1)
            tg1e = None
            for i in range(4):
                tg1e = sop(lambda e, i=i: e.activation(
                    out=hg1[:, i:i + 1], in_=ps[0][:, i:i + 1], func=AF.Gelu,
                    bias=biases[:, 14 + i:15 + i]), [th])
            bank_free[0] = tg1e
            th2 = None
            for i in range(2):
                th2 = top(lambda e, i=i: [e.matmul(
                    ps[1][:, i:i + 1], w_gc2[:, k * 256 + i * 128:k * 256 + (i + 1) * 128],
                    hg1[:, k:k + 1], start=(k == 0), stop=(k == 3)) for k in range(4)][-1],
                    [tg1e, bank_free[1]] + t_gc2)
            tg2e = None
            for i in range(2):
                tg2e = sop(lambda e, i=i: e.activation(
                    out=hg2[:, i:i + 1], in_=ps[1][:, i:i + 1], func=AF.Gelu,
                    bias=biases[:, 18 + i:19 + i]), [th2])
            bank_free[1] = tg2e
            tm3 = top(lambda e: [e.matmul(ps[3][0:1, 0:1], w_gc3[:, k:k + 1], hg2[:, k:k + 1],
                                          start=(k == 0), stop=(k == 1)) for k in range(2)][-1],
                      [tg2e, bank_free[3]] + t_gc3)
            tgf = sop(lambda e: e.activation(out=ar_sb[0:1, 4:5], in_=ps[3][0:1, 0:1], func=AF.Exp,
                                             scale=cpool[0:1, 3:4], bias=biases[0:1, 23:24]), [tm3])
            bank_free[3] = tgf
            tgf = vop(lambda e: e.tensor_scalar(out=ar_sb[0:1, 4:5], in0=ar_sb[0:1, 4:5],
                                                scalar1=1.0, scalar2=None, op0=ALU.add), [tgf])
            tgf = vop(lambda e: e.reciprocal(out=ar_sb[0:1, 4:5], in_=ar_sb[0:1, 4:5]), [tgf])
            tab = vop(lambda e: e.tensor_scalar(out=ar_sb[0:1, 3:4], in0=ar_sb[0:1, 4:5],
                                                scalar1=1.4, scalar2=-0.7, op0=ALU.mult,
                                                op1=ALU.add), [tgf])
            tab = vop(lambda e: e.tensor_mul(out=ar_sb[0:1, 3:4], in0=ar_sb[0:1, 3:4],
                                             in1=ar_sb[0:1, 0:1]), [tab])
            tab = vop(lambda e: e.tensor_scalar(out=ar_sb[0:1, 1:2], in0=ar_sb[0:1, 3:4],
                                                scalar1=1.0, scalar2=0.1, op0=ALU.mult,
                                                op1=ALU.add), [tab])
            tab = vop(lambda e: e.tensor_scalar(out=ar_sb[0:1, 2:3], in0=ar_sb[0:1, 0:1],
                                                scalar1=0.3, scalar2=None, op0=ALU.mult), [tab])
            ph25_tok["tab"] = tab

        if DEBUG_PHASES <= 3:
            emit_b_tail()
            emit_ar()
            emit_ph25()
            finish_debug()
            return nc

        A32.reset(); A16.reset()
        A32.mark(); A16.mark()
        x2l = A32.alloc(NPT * D)
        of_ring = [A32.alloc(8 * 512) for _ in range(2)]
        o3t = A32.alloc(NPT * D)
        tmpq = A32.alloc(1024)
        crow = A32.alloc(512)
        cb_sb = A32.alloc(512)
        ob_ring = [A16.alloc(8 * 512) for _ in range(2)]
        h1 = A16.alloc(8 * 512)
        h2 = A16.alloc(4 * 512)
        h3 = A16.alloc(2 * 512)
        crow_bf = A16.alloc(512)

        of_free = [None, None]
        ob_free = [None, None]
        st_tok = [o0_spill[s] for s in range(NSEG)]
        p3 = {"h1": None, "h2": None, "h3": None, "crow": None, "cb": None,
              "cb2": None, "tmpq": None, "y": None, "x2l": None}
        dep_ld = S.fence(("tt", "ss", "vv"))
        osrc = [sc_oa, sc_ob, sc_oa]
        odst = [sc_ob, sc_oa, None]
        TWSRC = {0: 4, 1: 5, 2: 6, 3: 7, 4: 0, 5: 1, 6: 2, 7: 3}
        TWSGN = {0: 1, 1: -1, 2: -1, 3: 1}

        iters = [(cyc, s) for cyc in range(3) for s in range(NSEG)]
        pend = {}
        loaded = set()

        def stok(s):
            return st_tok[s] if st_tok[s] is not None else o0_spill[s]

        def ph3_loads(cyc, s):
            xr = s % 2
            if cyc == 0 and s == 0:
                t_of = None
                t_ob = dma("gpsimd", ob_ring[0], osrc[0][0],
                           [stok(0), AB7[0]])
            elif cyc == 0 and s == 1:
                t_of = dma("sync", of_ring[1], osrc[0][1],
                           [stok(1), o0_spill[NSEG - 1]] + dep_ld)
                t_ob = dma("gpsimd", ob_ring[1], osrc[0][1],
                           [stok(1)] + dep_ld)
            else:
                t_of = dma("sync", of_ring[xr], osrc[cyc][s],
                           [of_free[xr], stok(s)] + dep_ld)
                t_ob = dma("gpsimd", ob_ring[xr], osrc[cyc][s],
                           [ob_free[xr], stok(s)] + dep_ld)
            pend[(cyc, s)] = (t_of, t_ob)
            loaded.add((cyc, s))

        ph3_loads(0, 0)

        def ph3_head(cyc, s, it_idx):
            xr = s % 2
            t_of, t_ob = pend.pop((cyc, s))
            ev1 = None
            for i in range(8):
                bk = i % 2
                tm = top(lambda e, i=i, bk=bk, xr=xr: [e.matmul(
                    ps[bk][:, 0:512], w_cn1[:, k * 1024 + i * 128:k * 1024 + (i + 1) * 128],
                    ob_ring[xr][:, k * 512:(k + 1) * 512], start=(k == 0), stop=(k == 7))
                    for k in range(8)][-1], [t_ob, bank_free[bk]] + t_cn1)
                ev1 = sop(lambda e, i=i, bk=bk: e.activation(
                    out=h1[:, i * 512:(i + 1) * 512], in_=ps[bk], func=AF.Gelu,
                    bias=biases[:, i:i + 1]), [tm, p3["h1"]])
                bank_free[bk] = ev1
            p3["h1"] = ev1
            if it_idx == 0:
                ph2_B2(NSEG - 2)
                ph2_B1(NSEG - 1)
            ev2 = None
            for i in range(4):
                bk = i % 2
                tm = top(lambda e, i=i, bk=bk: [e.matmul(
                    ps[bk][:, 0:512], w_cn2[:, k * 512 + i * 128:k * 512 + (i + 1) * 128],
                    h1[:, k * 512:(k + 1) * 512], start=(k == 0), stop=(k == 7))
                    for k in range(8)][-1], [ev1, bank_free[bk]] + t_cn2)
                ev2 = sop(lambda e, i=i, bk=bk: e.activation(
                    out=h2[:, i * 512:(i + 1) * 512], in_=ps[bk], func=AF.Gelu,
                    bias=biases[:, 8 + i:9 + i]), [tm, p3["h2"]])
                bank_free[bk] = ev2
            p3["h2"] = ev2
            ev3 = None
            for i in range(2):
                bk = i % 2
                tm = top(lambda e, i=i, bk=bk: [e.matmul(
                    ps[bk][:, 0:512], w_cn3[:, k * 256 + i * 128:k * 256 + (i + 1) * 128],
                    h2[:, k * 512:(k + 1) * 512], start=(k == 0), stop=(k == 3))
                    for k in range(4)][-1], [ev2, bank_free[bk]] + t_cn3)
                ev3 = sop(lambda e, i=i, bk=bk: e.activation(
                    out=h3[:, i * 512:(i + 1) * 512], in_=ps[bk], func=AF.Gelu,
                    bias=biases[:, 12 + i:13 + i]), [tm, p3["h3"]])
                bank_free[bk] = ev3
            p3["h3"] = ev3
            t4b = {0: 7, 1: 6}.get(it_idx, 2)
            tm4 = top(lambda e, t4b=t4b: [e.matmul(ps[t4b][0:1, 0:512], w_cn4[:, k:k + 1],
                                                   h3[:, k * 512:(k + 1) * 512],
                                                   start=(k == 0), stop=(k == 1))
                                          for k in range(2)][-1],
                      [ev3, bank_free[t4b]] + t_cn4)
            if it_idx == 0:
                ph2_B2(NSEG - 1)
                emit_ar()
                t_of = dma("sync", of_ring[0], osrc[0][0],
                           [stok(0), pv["o0t"]])
                ph3_loads(0, 1)
            return dict(cyc=cyc, s=s, xr=xr, t_of=t_of, ev1=ev1, tm4=tm4, t4b=t4b,
                        it=it_idx)

        def ph3_tail(c):
            cyc, s, xr, t_of, ev1, tm4, t4b = (c["cyc"], c["s"], c["xr"], c["t_of"],
                                               c["ev1"], c["tm4"], c["t4b"])
            tcw = sop(lambda e, t4b=t4b: e.activation(
                out=crow[0:1, :], in_=ps[t4b][0:1, 0:512],
                func=AF.Tanh, scale=cpool[0:1, 4:5],
                bias=biases[0:1, 22:23]), [tm4, p3["crow"]])
            bank_free[t4b] = tcw
            tcw = vop(lambda e: e.tensor_scalar(out=crow[0:1, :], in0=crow[0:1, :],
                                                scalar1=ar_sb[0:1, 2:3],
                                                scalar2=ar_sb[0:1, 1:2],
                                                op0=ALU.mult, op1=ALU.add),
                      [tcw, ph25_tok["tab"]])
            tcw = vop(lambda e: e.tensor_copy(out=crow_bf[0:1, :], in_=crow[0:1, :]), [tcw])
            p3["crow"] = tcw
            tcb = top(lambda e: e.matmul(ps[3][:, 0:512], oner, crow_bf[0:1, :],
                                         start=True, stop=True),
                      [tcw, bank_free[3], p3["cb"]])
            tcs = sop(lambda e: e.copy(out=cb_sb, in_=ps[3][:, 0:512]),
                      [tcb, p3["cb2"]])
            bank_free[3] = tcs
            tup = None
            tmfl = None
            tevl = None
            for f4 in range(4):
                fa, fb = f4, f4 + 4
                ta_ = vop(lambda e, fb=fb, xr=xr: e.tensor_tensor(
                    out=tmpq[:, 0:512], in0=cb_sb,
                    in1=of_ring[xr][:, fb * 512:(fb + 1) * 512], op=ALU.mult),
                    [tcs, t_of, p3["tmpq"]])
                tb_ = vop(lambda e, fa=fa, xr=xr: e.tensor_tensor(
                    out=tmpq[:, 512:1024], in0=cb_sb,
                    in1=of_ring[xr][:, fa * 512:(fa + 1) * 512], op=ALU.mult),
                    [tcs, t_of])
                opa = ALU.add if TWSGN[fa // 2] > 0 else ALU.subtract
                opb = ALU.add if TWSGN[fb // 2] > 0 else ALU.subtract
                tu1 = vop(lambda e, fa=fa, xr=xr, opa=opa: e.tensor_tensor(
                    out=of_ring[xr][:, fa * 512:(fa + 1) * 512],
                    in0=of_ring[xr][:, fa * 512:(fa + 1) * 512],
                    in1=tmpq[:, 0:512], op=opa), [ta_, tb_])
                tup = vop(lambda e, fb=fb, xr=xr, opb=opb: e.tensor_tensor(
                    out=of_ring[xr][:, fb * 512:(fb + 1) * 512],
                    in0=of_ring[xr][:, fb * 512:(fb + 1) * 512],
                    in1=tmpq[:, 512:1024], op=opb), [tu1])
                p3["tmpq"] = tup
                if cyc == 2:
                    for g, bk, tk in ((fa, 4, tu1), (fb, 5, tup)):
                        def mmf(e, g=g, bk=bk, xr=xr):
                            r = None
                            for p in range(NPT):
                                r = e.matmul(ps[bk][:, p * 128:(p + 1) * 128],
                                             of_ring[xr][:, g * 512 + 128 * p:g * 512 + 128 * p + 128],
                                             idf32, start=True, stop=True)
                            return r
                        tmfl = top(mmf, [tk, bank_free[bk]])
                        tevl = sop(lambda e, g=g, bk=bk: e.copy(
                            out=r3(o3t, D)[:, :, g * 128:(g + 1) * 128],
                            in_=r3(ps[bk][:, 0:512], 128)), [tmfl, p3["y"]])
                        bank_free[bk] = tevl
            p3["cb"] = tup
            p3["cb2"] = tup
            ob_free[xr] = ev1
            if cyc < 2:
                st_tok[s] = dma("gpsimd", odst[cyc][s], of_ring[xr], [tup])
                of_free[xr] = st_tok[s]
            else:
                t_x2 = dma("gpsimd", x2l, sc_x2[s],
                           [x2_spill[s], p3["x2l"],
                            x2_spill[NSEG - 1], pv["xs"]])
                tev4 = tevl
                of_free[xr] = tmfl
                ln4 = layernorm(lambda p: (o3t[:, p * D:p * D + 512],
                                           o3t[:, p * D + 512:(p + 1) * D]),
                                lambda p: o3t[:, p * D:(p + 1) * D], NPT,
                                [tev4])
                tyv = None
                for p in range(NPT):
                    tyv = vop(lambda e, p=p: e.tensor_add(
                        out=o3t[:, p * D:(p + 1) * D], in0=o3t[:, p * D:(p + 1) * D],
                        in1=x2l[:, p * D:(p + 1) * D]), [ln4[-1], t_x2])
                ty = dma("sync", y_out[s], o3t, [tyv])
                p3["y"] = ty
                p3["x2l"] = tyv
            for nx in (c["it"] + 1, c["it"] + 2):
                if nx < len(iters) and iters[nx] not in loaded:
                    ph3_loads(*iters[nx])

        held = None
        for it_idx, (cyc, s) in enumerate(iters):
            c = ph3_head(cyc, s, it_idx)
            if it_idx == 0:
                held = c
                continue
            if it_idx == 1:
                emit_ph25()
                ph3_tail(held)
                held = None
            ph3_tail(c)

        S.add("sync", None, S.dma_fence(), semname="fin", inc=1)

        with nc.Block() as block:
            S.emit(block)

    return nc


def _host_prep(inputs, core):
    b, q = core // 4, core % 4
    x = np.asarray(inputs["L0_attn_x"])
    xc = np.asarray(x[b, q * T:(q + 1) * T], np.float32)
    xseg = np.ascontiguousarray(
        xc.reshape(NSEG, NPT, 128, D).transpose(0, 2, 1, 3).reshape(NSEG, 128, NPT * D))
    if q == 0:
        xh = np.zeros((256, D), np.float32)
    else:
        xh = np.asarray(x[b, q * T - 256:q * T], np.float32)
    xhalo = np.ascontiguousarray(xh.reshape(2, 128, D).transpose(1, 0, 2).reshape(128, 2 * D))

    jj = np.arange(128)[:, None]
    ii = np.arange(256)[None, :]
    m0 = (ii < jj).astype(np.float32)
    m1 = (ii < jj + 128).astype(np.float32)
    m2 = (ii >= jj).astype(np.float32)
    m3 = (ii >= jj + 128).astype(np.float32)
    wmask = np.concatenate([m0, m1, m2, m3], axis=1).astype(BF)
    if q == 0:
        wmask0 = np.concatenate([0 * m0, 0 * m1, m2, m3], axis=1).astype(BF)
    else:
        wmask0 = wmask

    lat = np.zeros((NSEG, 128, NPT * 512), np.float32)
    for s in range(NSEG):
        for p in range(NPT):
            C = q * 16 + (s * NPT + p) // 2
            n_ok = min(C + 1, 64)
            for lh in range(8):
                lat[s, :, p * 512 + lh * 64: p * 512 + lh * 64 + n_ok] = 1.0
    latmask = lat.astype(BF)

    glw = np.asarray(inputs["L0_attn_gate_local_w"], np.float32)
    grw = np.asarray(inputs["L0_attn_gate_remote_w"], np.float32)
    wgate = np.zeros((D, 32), np.float32)
    for h in range(16):
        wgate[64 * h:64 * h + 64, h] = glw[h]
        wgate[64 * h:64 * h + 64, 16 + h] = grw[h]

    f32 = np.float32
    g = lambda k: np.asarray(inputs[k], f32)
    return {
        "xseg": xseg, "xhalo": xhalo,
        "wq": g("L0_attn_wq"), "wk": g("L0_attn_wk"), "wv": g("L0_attn_wv"),
        "wgq": g("L0_attn_wgq"), "wgk": g("L0_attn_wgk"), "wgv": g("L0_attn_wgv"),
        "r_out": g("L0_attn_r_out_w"), "w_out": g("L0_attn_w_out"), "wgate": wgate,
        "cn_w1": g("L0_ffn_cn_w1"), "cn_w2": g("L0_ffn_cn_w2"),
        "cn_w3": g("L0_ffn_cn_w3"), "cn_w4": g("L0_ffn_cn_w4"),
        "cn_b1": np.ascontiguousarray(g("L0_ffn_cn_b1").reshape(8, 128).T),
        "cn_b2": np.ascontiguousarray(g("L0_ffn_cn_b2").reshape(4, 128).T),
        "cn_b3": np.ascontiguousarray(g("L0_ffn_cn_b3").reshape(2, 128).T),
        "cn_b4": g("L0_ffn_cn_b4").reshape(1, 1),
        "gc_w1": g("L0_ffn_gc_w1"), "gc_w2": g("L0_ffn_gc_w2"), "gc_w3": g("L0_ffn_gc_w3"),
        "gc_b1": np.ascontiguousarray(g("L0_ffn_gc_b1").reshape(4, 128).T),
        "gc_b2": np.ascontiguousarray(g("L0_ffn_gc_b2").reshape(2, 128).T),
        "gc_b3": g("L0_ffn_gc_b3").reshape(1, 1),
        "ar": g("L0_ffn_adaptive_range").reshape(1, 1),
        "winmask": wmask, "winmask0": wmask0, "latmask": latmask,
        "ident_bf": np.eye(128, dtype=np.float32).astype(BF),
        "ident_f32": np.eye(128, dtype=np.float32),
        "ones_col": np.ones((128, 1), BF),
        "ones_row": np.ones((1, 128), BF),
    }


def kernel(**inputs):
    if "nc" not in _CACHE:
        _CACHE["nc"] = build_nc()
    nc = _CACHE["nc"]
    in_maps = [_host_prep(inputs, c) for c in range(8)]
    res = run_bass_kernel_spmd(nc, in_maps, core_ids=list(range(8)))
    out = np.zeros((2, 4 * T, D), np.float32)
    for c in range(8):
        b, q = c // 4, c % 4
        y = np.asarray(res.results[c]["y"])
        out[b, q * T:(q + 1) * T] = y.reshape(NSEG, 128, NPT, D).transpose(
            0, 2, 1, 3).reshape(T, D)
    return out
